# version 17
# baseline (speedup 1.0000x reference)
from contextlib import ExitStack

import numpy as np
import ml_dtypes
import concourse.bass as bass
import concourse.mybir as mybir
from concourse.bass_utils import run_bass_kernel_spmd

F32 = mybir.dt.float32
BF16 = mybir.dt.bfloat16
I8 = mybir.dt.int8
AF = mybir.ActivationFunctionType
ALU = mybir.AluOpType
AX = mybir.AxisListType

P = 128
S = 2048
D = 1024
NCORES = 8
MAGIC = 12582912.0
ENGINES = ("tensor", "vector", "scalar", "gpsimd", "sync")
REPLICA_GROUPS = [list(range(8))]
BF = ml_dtypes.bfloat16

BFPOOL_COLS = 42240
F32POOL_COLS = 4928


class Sem:
    __slots__ = ("h", "n", "name")

    def __init__(self, h, name):
        self.h, self.n, self.name = h, 0, name


class Ctx:
    def __init__(self, nc, st):
        self.nc = nc
        self.st = st
        self.ops = {e: [] for e in ENGINES}
        self.es = {e: self.sem("es_" + e) for e in ENGINES}
        self.dW = self.sem("dW")
        self.dX = self.sem("dX")

    def sem(self, name):
        return Sem(self.st.enter_context(self.nc.semaphore(name)), name)

    def sb(self, name, shape, dt):
        return self.st.enter_context(self.nc.sbuf_tensor(name, shape, dt))

    def ps(self, name, shape, dt=F32):
        return self.st.enter_context(self.nc.psum_tensor(name, shape, dt))

    def step(self, eng, body, deps=(), sem=None, inc=1):
        if sem is None:
            sem = self.es[eng]
        best = {}
        for d in deps:
            if d is None:
                continue
            s_, v_ = d
            if best.get(id(s_), (None, -1))[1] < v_:
                best[id(s_)] = (s_, v_)
        deps = list(best.values())

        def fn(e, deps=tuple(deps), body=body, sem=sem, inc=inc):
            for s_, v_ in deps:
                e.wait_ge(s_.h, v_)
            last = body(e)
            if last is not None:
                last.then_inc(sem.h, inc)
            else:
                e.sem_inc(sem.h, inc)

        self.ops[eng].append(fn)
        sem.n += inc
        return (sem, sem.n)

    def T(self, body, deps=()):
        return self.step("tensor", body, deps)

    def V(self, body, deps=()):
        return self.step("vector", body, deps)

    def A(self, body, deps=()):
        return self.step("scalar", body, deps)

    def G(self, body, deps=()):
        return self.step("gpsimd", body, deps)

    def dma(self, qeng, out, in_, deps=()):
        sem = self.dW if qeng == "gpsimd" else self.dX
        return self.step(qeng, lambda e: e.dma_start(out=out, in_=in_), deps, sem, 16)

    def all_done(self):
        return ([(s, s.n) for s in self.es.values()]
                + [(self.dW, self.dW.n), (self.dX, self.dX.n)])


def build_and_finalize(nc, C):
    with nc.Block() as block:
        @block.tensor
        def _(e):
            for fn in C.ops["tensor"]:
                fn(e)

        @block.vector
        def _(e):
            for fn in C.ops["vector"]:
                fn(e)

        @block.scalar
        def _(e):
            for fn in C.ops["scalar"]:
                fn(e)

        @block.gpsimd
        def _(e):
            for fn in C.ops["gpsimd"]:
                fn(e)

        @block.sync
        def _(e):
            for fn in C.ops["sync"]:
                fn(e)


class Pool:

    def __init__(self, C, name, dt, cols):
        self.t = C.sb(name, [P, cols], dt)
        self.cols = cols
        self.off = 0

    def take(self, cols):
        assert self.off + cols <= self.cols, f"pool overflow: {self.off}+{cols}>{self.cols}"
        ap = self.t[:, self.off:self.off + cols]
        self.off += cols
        return ap

    def reset(self, keep=0):
        self.off = keep


class Res:

    def __init__(self, C):
        self.idf = C.sb("s_idf", [P, P], F32)
        self.idb = C.sb("s_idb", [P, P], BF16)
        self.onecb = C.sb("s_onecb", [P, 1], BF16)
        self.onecf = C.sb("s_onecf", [P, 1], F32)
        self.m10b = C.sb("s_m10b", [P, 1], BF16)
        self.m11b = C.sb("s_m11b", [P, 1], BF16)
        self.m10f = C.sb("s_m10f", [P, 1], F32)
        self.eps56 = C.sb("s_eps56", [P, 2], F32)
        self.onerf = C.sb("s_onerf", [65, P], F32)
        self.pm = [C.ps(f"pm{i}", [P, 512], F32) for i in range(6)]
        self.ptr = C.ps("ptr", [P, 1024], BF16)
        self.pss = C.ps("pss", [P, 512], F32)
        self.sq = [C.sb(f"sc_sq{i}", [P, 512], BF16) for i in range(2)]
        self.ab = C.sb("sc_ab", [P, 512], F32)
        self.bb = C.sb("sc_bb", [P, 512], F32)
        self.big = C.sb("sc_big", [P, S], F32)
        self.selp = C.sb("s_selp", [2, P], BF16)
        self.mask01 = C.sb("s_mask01", [P, P], BF16)
        self.qsl = [C.sb(f"sc_qsl{i}", [P, 1024], I8) for i in range(2)]
        self.rows = C.sb("sc_rows", [1, 3 * S], F32)
        self.r0 = self.rows[0:1, 0:S]
        self.r1 = self.rows[0:1, S:2 * S]
        self.r2 = self.rows[0:1, 2 * S:3 * S]
        self.pbf = Pool(C, "pool_bf", BF16, BFPOOL_COLS)
        self.pf = Pool(C, "pool_f32", F32, F32POOL_COLS)

    def load(self, C, dr):
        d = [C.dma("sync", self.idf[:], dr["c_idf"]),
             C.dma("sync", self.idb[:], dr["c_idb"]),
             C.dma("sync", self.onecb[:], dr["c_onecb"]),
             C.dma("sync", self.onecf[:], dr["c_onecf"]),
             C.dma("sync", self.onerf[:], dr["c_onerf"]),
             C.dma("sync", self.m10b[:], dr["c_m10b"]),
             C.dma("sync", self.m11b[:], dr["c_m11b"]),
             C.dma("sync", self.m10f[:], dr["c_m10f"]),
             C.dma("sync", self.eps56[:], dr["c_eps56"])]
        self.deps = d
        return d

    @staticmethod
    def host_arrays():
        return {
            "c_idf": np.eye(P, dtype=np.float32),
            "c_idb": np.eye(P, dtype=BF),
            "c_onecb": np.ones((P, 1), dtype=BF),
            "c_onecf": np.ones((P, 1), dtype=np.float32),
            "c_onerf": np.ones((65, P), dtype=np.float32),
            "c_m10b": np.full((P, 1), 2.0 ** -10, dtype=BF),
            "c_m11b": np.full((P, 1), 2.0 ** -11, dtype=BF),
            "c_m10f": np.full((P, 1), 2.0 ** -10, dtype=np.float32),
            "c_eps56": np.tile(np.array([[1e-5, 1e-6]], np.float32), (P, 1)),
        }

    @staticmethod
    def decls():
        return [("c_idf", [P, P], F32), ("c_idb", [P, P], BF16),
                ("c_onecb", [P, 1], BF16), ("c_onecf", [P, 1], F32),
                ("c_onerf", [65, P], F32), ("c_m10b", [P, 1], BF16),
                ("c_m11b", [P, 1], BF16), ("c_m10f", [P, 1], F32),
                ("c_eps56", [P, 2], F32)]



def bcast_row(C, R, dst, row_ap, n, deps, pbank, base=0):
    d1 = C.T(lambda e: e.matmul(pbank[:, 0:n], R.onerf[base:base + 1, :], row_ap,
                                start=True, stop=True), deps)
    return C.V(lambda e: e.tensor_copy(dst[:, 0:n], pbank[:, 0:n]), [d1])


def ln_fm(C, R, SRC, DST, eps, deps, nf=None, sqw=None):
    nch = len(SRC)
    assert (nf or nch * P) == 1024
    NG = S // 512
    sum_deps = []
    prev_dsv = None
    prev_dqv = None
    sq_slots = None
    if sqw is not None:
        sq_slots = [sqw[:, i * 512:(i + 1) * 512] for i in range(8)]
    for ng in range(NG):
        sl = slice(ng * 512, (ng + 1) * 512)

        def mksum(sl=sl):
            def body(e):
                last = None
                for cc in range(nch):
                    last = e.matmul(R.pm[4][0:1, 0:512], R.m10f[:, 0:1], SRC[cc][:, sl],
                                    start=(cc == 0), stop=(cc == nch - 1))
                return last
            return body
        dsu = C.T(mksum(), list(deps) + ([prev_dsv] if prev_dsv else []))
        dsv = C.V(lambda e, sl=sl: e.tensor_copy(R.r0[:, sl], R.pm[4][0:1, 0:512]), [dsu])
        prev_dsv = dsv

        if sq_slots is not None:
            dqs = []
            for cc in range(nch):
                dq = C.A(lambda e, cc=cc, sl=sl: e.activation(sq_slots[cc][:], SRC[cc][:, sl],
                                                              AF.Square, bias=0.0, scale=1.0),
                         (list(deps) + ([prev_dqv] if prev_dqv else [])) if cc == 0 else [dqs[-1]])
                dqs.append(dq)

            def mkq2():
                def body(e):
                    last = None
                    for cc in range(nch):
                        last = e.matmul(R.pm[5][0:1, 0:512], R.m10b[:, 0:1], sq_slots[cc][:],
                                        start=(cc == 0), stop=(cc == nch - 1))
                    return last
                return body
            prev = C.T(mkq2(), [dqs[-1], dsu])
        else:
            prev = dsv
            prev_mm2 = None
            for cc in range(nch):
                sqdeps = [prev] if cc == 0 else ([prev_mm2] if prev_mm2 else [])
                if cc == 0:
                    sqdeps += list(deps) + ([prev_dqv] if prev_dqv else [])
                dq = C.A(lambda e, cc=cc, sl=sl, b=cc % 2:
                         e.activation(R.sq[b][:], SRC[cc][:, sl], AF.Square, bias=0.0, scale=1.0),
                         sqdeps)
                dmm = C.T(lambda e, cc=cc, b=cc % 2:
                          e.matmul(R.pm[5][0:1, 0:512], R.m10b[:, 0:1], R.sq[b][:],
                                   start=(cc == 0), stop=(cc == nch - 1)), [dq])
                prev_mm2 = prev
                prev = dmm
        dqv = C.V(lambda e, sl=sl: e.tensor_copy(R.r1[:, sl], R.pm[5][0:1, 0:512]), [prev])
        prev_dqv = dqv
        sum_deps += [dsv, dqv]

    def stats(e):
        e.tensor_tensor(R.r2[:, :], R.r0[:, :], R.r0[:, :], ALU.mult)
        return e.tensor_tensor(R.r1[:, :], R.r1[:, :], R.r2[:, :], ALU.subtract)
    dst_ = C.V(stats, sum_deps)
    dsq = C.A(lambda e: e.activation(R.r1[:, :], R.r1[:, :], AF.Sqrt, bias=R.eps56[0:1, 0:1], scale=1.0), [dst_])
    drc = C.V(lambda e: e.reciprocal(R.r1[:, :], R.r1[:, :]), [dsq])

    def mkb(e):
        e.tensor_tensor(R.r2[:, :], R.r0[:, :], R.r1[:, :], ALU.mult)
        return e.tensor_scalar(R.r2[:, :], R.r2[:, :], -1.0, None, ALU.mult)
    dbr = C.V(mkb, [drc])

    out_deps = []
    prev_ap = None
    X3s = R.X_r.rearrange("p (cc t) -> p cc t", cc=8)
    for ng in range(NG):
        sl = slice(ng * 512, (ng + 1) * 512)
        da = C.T(lambda e, sl=sl: e.matmul(R.pm[4][:, 0:512], R.onerf[0:1, :], R.r1[:, sl],
                                           start=True, stop=True), [dbr, prev_ap])
        db = C.T(lambda e, sl=sl: e.matmul(R.pm[5][:, 0:512], R.onerf[0:1, :], R.r2[:, sl],
                                           start=True, stop=True), [dbr, prev_ap])
        dc = C.V(lambda e: e.tensor_copy(R.ab[:], R.pm[4][:, 0:512]), [da])
        dd = C.A(lambda e: e.activation(R.bb[:], R.pm[5][:, 0:512], AF.Identity,
                                        bias=0.0, scale=1.0), [db])

        def mkap(sl=sl):
            def body(e):
                e.tensor_tensor(X3s[:, :, sl], X3s[:, :, sl],
                                R.ab[:, None, :].broadcast_to([P, 8, 512]).transpose([0, 1, 2])
                                if False else R.ab[:, None, 0:512].broadcast_to([P, 8, 512]),
                                ALU.mult)
                return e.tensor_tensor(X3s[:, :, sl], X3s[:, :, sl],
                                       R.bb[:, None, 0:512].broadcast_to([P, 8, 512]), ALU.add)
            return body
        dap = C.V(mkap(), [dc, dd])
        prev_ap = dap
        out_deps.append(dap)
    return out_deps


def final_store(C, R, SRC, out_dram, deps):
    bar = C.all_done()
    deps = list(deps) + bar
    R.pf.reset(keep=64)
    xtm = R.pf.take(D)
    dmas = []
    prev_copy = None
    for tt in range(S // P):
        t0 = tt * P

        def mktr(t0=t0):
            def body(e):
                last = None
                for cc in range(8):
                    pb = R.pm[cc // 4]
                    last = e.transpose(pb[:, (cc % 4) * P:(cc % 4 + 1) * P], SRC[cc][:, t0:t0 + P], R.idf[:])
                return last
            return body
        dtr = C.T(mktr(), list(deps) + ([prev_copy] if prev_copy else []))

        def mkcp():
            def body(e):
                e.tensor_copy(xtm[:, 0:512], R.pm[0][:, 0:512])
                return e.tensor_copy(xtm[:, 512:1024], R.pm[1][:, 0:512])
            return body
        dcp = C.V(mkcp(), [dtr])
        ddma = C.dma("sync", out_dram[t0:t0 + P, :], xtm, [dcp])
        prev_copy = ddma
        dmas.append(ddma)
    return dmas



def emit_l1ffn(C, R, X, dr):
    KC, MC, NG = 8, 8, 4
    bar = C.all_done()
    R.pbf.reset()
    R.pf.reset()
    wqr = R.pbf.take(8 * D)
    xqr = R.pbf.take(8 * S)
    wq = [wqr[:, k * D:(k + 1) * D] for k in range(KC)]
    xq = [xqr[:, c * S:(c + 1) * S] for c in range(KC)]
    fr = R.pf.take(64)
    cols = fr[:, 0:4]
    wsc = fr[:, 8:16]
    bia = fr[:, 16:24]
    scl = fr[:, 24:32]
    gmx = fr[0:1, 32:36]
    ccs = R.pf.take(P)[0:1, :]

    dw = [C.dma("gpsimd", wqr.rearrange("p (kc c) -> p kc c", kc=8),
                dr["l1f_w_t"].rearrange("(kc p) m -> p kc m", p=P), bar)]
    dsc = C.dma("sync", wsc, dr["l1f_wscale_t"], bar)
    dbi = C.dma("sync", bia, dr["l1f_bias_t"], bar)

    amx = R.pf.take(8)
    def absmax(e):
        last = None
        for cc in range(8):
            last = e.tensor_reduce(amx[:, cc:cc + 1], X[cc][:], AX.X, ALU.max, apply_absolute_value=True)
        return last
    d1a = C.V(absmax, bar)
    d1 = C.V(lambda e: e.tensor_reduce(cols[:, 0:1], amx[:], AX.X, ALU.max), [d1a])
    d2 = C.T(lambda e: e.transpose(R.pss[0:1, 0:P], cols[:, 0:1], R.idf[:]), [d1] + bar)
    d3 = C.V(lambda e: e.tensor_reduce(gmx[0:1, 0:1], R.pss[0:1, 0:P], AX.X, ALU.max), [d2])
    d4 = C.V(lambda e: e.tensor_copy(ccs, gmx[0:1, 0:1].broadcast_to([1, P])), [d3])

    dci = C.dma("gpsimd", dr["l1f_cc_in"], ccs, [d4])
    dco = C.step("gpsimd",
                 lambda e: e.collective_compute("AllReduce", ALU.max,
                                                replica_groups=REPLICA_GROUPS,
                                                ins=[dr["l1f_cc_in"]],
                                                outs=[dr["l1f_cc_out"]]),
                 [dci])
    dcb = C.dma("gpsimd", gmx[0:1, 0:2], dr["l1f_cc_out"][0:1, 0:2], [dco])

    d5 = C.V(lambda e: e.tensor_scalar(gmx[0:1, 2:3], gmx[0:1, 0:1], 1.0 / 127.0, 1e-5, ALU.mult, ALU.max), [dcb])
    d6 = C.V(lambda e: e.reciprocal(gmx[0:1, 3:4], gmx[0:1, 2:3]), [d5])
    d7 = C.T(lambda e: e.matmul(R.pss[:, 0:2], R.onerf[0:1, :], gmx[0:1, 2:4], start=True, stop=True),
             [d6])
    d8 = C.V(lambda e: e.tensor_copy(cols[:, 2:4], R.pss[:, 0:2]), [d7])
    d9 = C.V(lambda e: e.tensor_tensor(scl, wsc, cols[:, 2:3].broadcast_to([P, MC]), ALU.mult),
             [d8, dsc])

    qdeps = []
    xq3 = xqr.rearrange("p (cc t) -> p cc t", cc=8)
    X3 = R.X_r.rearrange("p (cc t) -> p cc t", cc=8)
    for cc in range(8):
        def mkq(cc=cc):
            def body(e):
                e.tensor_scalar(R.big[:], X[cc][:], cols[:, 3:4], None, ALU.mult)
                return e.tensor_scalar(xq[cc][:], R.big[:], MAGIC, MAGIC, ALU.add, ALU.subtract)
            return body
        qdeps.append(C.V(mkq(), [d8] if cc == 0 else [d8, qdeps[-1]]))

    prev_ev = {}
    ev_deps = []
    for mc in range(MC):
        for ng in range(NG):
            pbi = (mc * NG + ng) % 4
            pb = R.pm[pbi]

            def mkmm(mc=mc, ng=ng, pb=pb):
                def body(e):
                    last = None
                    for kc in range(KC):
                        last = e.matmul(pb[:, 0:512], wq[kc][:, mc * P:(mc + 1) * P],
                                        xq[kc][:, ng * 512:(ng + 1) * 512],
                                        start=(kc == 0), stop=(kc == KC - 1))
                    return last
                return body
            mmd = list(qdeps) + dw + [d9, dbi]
            if pbi in prev_ev:
                mmd.append(prev_ev[pbi])
            dmm = C.T(mkmm(), mmd)

            def mkev(mc=mc, ng=ng, pb=pb):
                sl = slice(ng * 512, (ng + 1) * 512)

                def body(e):
                    e.scalar_tensor_tensor(X[mc][:, sl], pb[:, 0:512], scl[:, mc:mc + 1],
                                           X[mc][:, sl], ALU.mult, ALU.add)
                    return e.tensor_scalar(X[mc][:, sl], X[mc][:, sl], bia[:, mc:mc + 1], None, ALU.add)
                return body
            dev = C.V(mkev(), [dmm])
            prev_ev[pbi] = dev
            ev_deps.append(dev)

    return ln_fm(C, R, X, X, 1e-5, ev_deps, sqw=xqr[:, 0:4096])


def l1ffn_host(inp):
    w = np.asarray(inp["L1_ffn_weight_int8"])
    ws = np.asarray(inp["L1_ffn_weight_scale"], np.float32)
    bi = np.asarray(inp["L1_ffn_bias"], np.float32)
    return {
        "l1f_w_t": np.ascontiguousarray(w.T),
        "l1f_wscale_t": np.ascontiguousarray(ws.reshape(8, 128).T),
        "l1f_bias_t": np.ascontiguousarray(bi.reshape(8, 128).T),
    }


def l1ffn_decls():
    return [("l1f_w_t", [D, D], I8), ("l1f_wscale_t", [P, 8], F32),
            ("l1f_bias_t", [P, 8], F32)]



def emit_l1attn(C, R, X, dr):
    KC, MC = 8, 8
    BLK, WIN = 64, 68
    H = 1024
    NGH = H // 512

    bar = C.all_done()
    R.pbf.reset()
    R.pf.reset()
    wr = R.pbf.take(8 * D)
    qr = R.pbf.take(8 * H)
    kr = R.pbf.take(8 * (H + 4))
    vr = R.pbf.take(8 * (H + 4))
    q = [qr[:, c * H:(c + 1) * H] for c in range(8)]
    kx = [kr[:, c * (H + 4):(c + 1) * (H + 4)] for c in range(8)]
    vx = [vr[:, c * (H + 4):(c + 1) * (H + 4)] for c in range(8)]
    ws = [wr[:, k * D:(k + 1) * D] for k in range(KC)]

    xbr = R.pbf.take(8 * 1024)
    xb2 = [xbr[:, c * 1024:(c + 1) * 1024] for c in range(8)]
    xexr = R.pbf.take(16)
    xex = [xexr[:, c * 2:(c + 1) * 2] for c in range(8)]
    fr = R.pf.take(512)
    bcols = fr[:, 0:24]
    mcol = fr[0:BLK, 24:28]
    sc = fr[0:BLK, 28:28 + WIN]
    ot = R.pf.take(D)[0:BLK, :]
    pbr = R.pbf.take(WIN + BLK)
    pb = pbr[0:BLK, 0:WIN]
    ptb = pbr[0:WIN, WIN:WIN + BLK]
    vwt = R.pbf.take(D)[0:WIN, :]
    masks = R.pf.take(3 * WIN)[0:BLK, :]

    dbc = C.dma("sync", bcols, dr["l1a_b_t"], bar)
    dmk = C.dma("sync", masks, dr["l1a_masks"], bar)

    xs_upd = []
    prev_half_kv = None
    prev_state = None
    lp_pss = [None]
    for hh in range(2):
        H0 = hh * H
        if hh == 1:
            def mvkv(e):
                last = None
                for cc in range(8):
                    e.tensor_copy(kx[cc][:, 0:2], kx[cc][:, H:H + 2])
                    last = e.tensor_copy(vx[cc][:, 0:2], vx[cc][:, H:H + 2])
                return last
            dkv = C.V(mvkv, [xs_upd[-1]])
            dpe = C.V(lambda e: [e.memset(kx[cc][:, H + 2:H + 4], 0.0) or
                                 e.memset(vx[cc][:, H + 2:H + 4], 0.0) for cc in range(8)][-1],
                      [dkv])
        else:
            def mkpad(e):
                last = None
                for cc in range(8):
                    e.memset(kx[cc][:, 0:2], 0.0)
                    last = e.memset(vx[cc][:, 0:2], 0.0)
                return last
            dpe = C.V(mkpad, bar)

        projs = [("l1a_wq", q, 0, 0), ("l1a_wk", kx, 2, 8), ("l1a_wv", vx, 2, 16)]
        ev_all = []
        prev_mm_last = prev_state["wmm"] if prev_state else None
        prev_ev = dict(prev_state["ev"]) if prev_state else {}
        xb_deps = []
        for ng in range(NGH):
            def mkxb(ng=ng, H0=H0):
                def body(e):
                    last = None
                    for kc in range(8):
                        last = e.tensor_copy(xb[kc][:, ng * 256:(ng + 1) * 256] if False else xb[kc],
                                             X[kc][:, H0 + ng * 512:H0 + (ng + 1) * 512])
                    return last
                return body
        xbr2 = None
        def mkxb_all(H0=H0):
            def body(e):
                last = None
                for kc in range(8):
                    for ng in range(NGH):
                        last = e.tensor_copy(xb2[kc][:, ng * 512:(ng + 1) * 512],
                                             X[kc][:, H0 + ng * 512:H0 + (ng + 1) * 512])
                return last
            return body
        dxb = C.V(mkxb_all(), bar if hh == 0 else [xs_upd[-1]])
        for j, (wname, dst, off, bofs) in enumerate(projs):
            wdeps = [C.dma("gpsimd", wr.rearrange("p (kc c) -> p kc c", kc=8),
                           dr[wname].rearrange("(kc p) m -> p kc m", p=P),
                           ([prev_mm_last] if prev_mm_last else bar))]
            for mc in range(MC):
                for ng in range(NGH):
                    pbi = (mc * NGH + ng) % 4
                    pbank = R.pm[pbi]

                    def mkmm(mc=mc, ng=ng, pbank=pbank):
                        def body(e):
                            last = None
                            for kc in range(KC):
                                last = e.matmul(pbank[0:P, 0:512],
                                                ws[kc][:, mc * P:(mc + 1) * P],
                                                xb2[kc][:, ng * 512:(ng + 1) * 512],
                                                start=(kc == 0), stop=(kc == KC - 1))
                            return last
                        return body
                    mmd = wdeps + [dxb] + ([prev_ev[pbi]] if pbi in prev_ev else [])
                    dmm = C.T(mkmm(), mmd)
                    if mc == MC - 1 and ng == NGH - 1:
                        prev_mm_last = dmm

                    def mkev(mc=mc, ng=ng, pbank=pbank, dst=dst, off=off, bofs=bofs):
                        def body(e):
                            return e.activation(dst[mc][:, off + ng * 512:off + (ng + 1) * 512],
                                                pbank[0:P, 0:512], AF.Identity,
                                                bias=bcols[:, bofs + mc:bofs + mc + 1], scale=1.0)
                        return body
                    dev = C.A(mkev(), [dmm, dbc, dpe])
                    prev_ev[pbi] = dev
                    ev_all.append(dev)
            if hh == 0 and j in (1, 2):
                dxe = C.V(lambda e: [e.tensor_copy(xex[kc], X[kc][:, H:H + 2]) for kc in range(8)][-1],
                          [dxb])
                def mkexmm(dst=dst):
                    def body(e):
                        last = None
                        for mc in range(MC):
                            for kc in range(KC):
                                last = e.matmul(R.pss[0:P, mc * 2:(mc + 1) * 2],
                                                ws[kc][:, mc * P:(mc + 1) * P], xex[kc],
                                                start=(kc == 0), stop=(kc == KC - 1))
                        return last
                    return body
                dxm = C.T(mkexmm(), [dxe, prev_mm_last] + ([lp_pss[0]] if lp_pss[0] else []))
                def mkexev(dst=dst, bofs=bofs):
                    def body(e):
                        last = None
                        for mc in range(MC):
                            last = e.activation(dst[mc][:, H + 2:H + 4], R.pss[0:P, mc * 2:(mc + 1) * 2],
                                                AF.Identity, bias=bcols[:, bofs + mc:bofs + mc + 1],
                                                scale=1.0)
                        return last
                    return body
                dxv = C.A(mkexev(), [dxm])
                prev_mm_last = dxm
                ev_all.append(dxv)
        prev_half_kv = ev_all[-1]

        if "dbgq" in dr and hh == 0:
            ddq = C.dma("sync", dr["dbgq"][0:P, :], q[0][:, :], [ev_all[-1]])
            ddk = C.dma("sync", dr["dbgk"][0:P, :], kx[0][:, :], [ev_all[-1]])
            ddv = C.dma("sync", dr["dbgv"][0:P, :], vx[0][:, :], [ev_all[-1]])

        prev = prev_state["blk"] if prev_state else \
            {"sc": None, "ptr": None, "out0": None, "out1": None, "tr": None, "ex": None,
             "pss": None}
        for b in range(16):
            t0 = b * BLK
            gb = hh * 16 + b
            mi = 1 if gb == 0 else (2 if gb == 31 else 0)
            msl = slice(mi * WIN, (mi + 1) * WIN)

            def mksc(t0=t0):
                def body(e):
                    last = None
                    for kc in range(KC):
                        last = e.matmul(R.pm[4][0:BLK, 0:WIN], q[kc][:, t0:t0 + BLK],
                                        kx[kc][:, t0:t0 + WIN],
                                        start=(kc == 0), stop=(kc == KC - 1))
                    return last
                return body
            dsc_ = C.T(mksc(), [ev_all[-1]] + ([prev["sc"]] if prev["sc"] else [dmk]))

            dms_a = C.V(lambda e, msl=msl: e.tensor_tensor(sc, R.pm[4][0:BLK, 0:WIN], masks[:, msl], ALU.add),
                        [dsc_] + ([prev["ex"]] if prev.get("ex") else []))
            dms_b = C.V(lambda e: e.tensor_reduce(mcol[:, 0:1], sc, AX.X, ALU.max), [dms_a])
            dms = C.V(lambda e: e.tensor_scalar(mcol[:, 1:2], mcol[:, 0:1], -1.0 / 32.0, None, ALU.mult),
                      [dms_b])
            prev["sc"] = dms_a

            dex = C.A(lambda e: e.activation(pb, sc, AF.Exp, bias=mcol[:, 1:2], scale=1.0 / 32.0),
                      [dms])
            prev["ex"] = dex

            dz_a = C.V(lambda e: e.tensor_reduce(mcol[:, 2:3], pb, AX.X, ALU.add), [dex])
            dz = C.V(lambda e: e.reciprocal(mcol[:, 3:4], mcol[:, 2:3]), [dz_a])

            dpt = C.T(lambda e: e.transpose(R.ptr[0:WIN, 0:BLK], pb, R.idb[0:BLK, 0:BLK]),
                      [dex] + ([prev["ptr"]] if prev["ptr"] else []))
            dptv = C.V(lambda e: e.tensor_copy(ptb, R.ptr[0:WIN, 0:BLK]), [dpt])

            dprev_e = dptv
            dprev_o = prev["pss"] if prev["pss"] else ev_all[-1]
            dvts = []
            for kc in range(KC):
                if kc % 2 == 0:
                    dvt = C.T(lambda e, kc=kc, t0=t0: e.transpose(
                        R.ptr[0:WIN, 0:P], vx[kc][:, t0:t0 + WIN], R.idb[:]), [dprev_e])
                    dvv = C.A(lambda e, kc=kc: e.activation(
                        vwt[:, kc * P:(kc + 1) * P], R.ptr[0:WIN, 0:P],
                        AF.Identity, bias=0.0, scale=1.0), [dvt])
                    dprev_e = dvv
                else:
                    dvt = C.T(lambda e, kc=kc, t0=t0: e.matmul(
                        R.pss[0:WIN, 0:P], vx[kc][:, t0:t0 + WIN], R.idb[:],
                        start=True, stop=True), [dprev_o])
                    dvv = C.V(lambda e, kc=kc: e.tensor_copy(
                        vwt[:, kc * P:(kc + 1) * P], R.pss[0:WIN, 0:P]), [dvt])
                    dprev_o = dvv
                dvts.append(dvv)
            prev["ptr"] = dprev_e
            prev["pss"] = dprev_o
            lp_pss[0] = dprev_o

            dm0 = C.T(lambda e: e.matmul(R.pm[0][0:BLK, 0:512], ptb, vwt[:, 0:512], start=True, stop=True),
                      [dptv, dvts[2], dvts[3]] + ([prev["out0"]] if prev["out0"] else []))
            dm1 = C.T(lambda e: e.matmul(R.pm[1][0:BLK, 0:512], ptb, vwt[:, 512:1024], start=True, stop=True),
                      [dvts[6], dvts[7]] + ([prev["out1"]] if prev["out1"] else []))
            de0 = C.A(lambda e: e.activation(ot[:, 0:512], R.pm[0][0:BLK, 0:512], AF.Copy,
                                             bias=0.0, scale=mcol[:, 3:4]), [dm0, dz])
            de1 = C.A(lambda e: e.activation(ot[:, 512:1024], R.pm[1][0:BLK, 0:512], AF.Copy,
                                             bias=0.0, scale=mcol[:, 3:4]), [dm1, dz])
            prev["out0"], prev["out1"] = de0, de1

            dtr = C.T(lambda e: [e.transpose(R.pm[2][:, kc * BLK:(kc + 1) * BLK],
                                             ot[0:BLK, kc * P:(kc + 1) * P], R.idf[0:BLK, 0:BLK])
                                 for kc in range(8)][-1],
                      [de0, de1] + ([prev["tr"]] if prev["tr"] else []))

            def mkadd(t0=t0, H0=H0):
                def body(e):
                    X3a = R.X_r.rearrange("p (cc t) -> p cc t", cc=8)
                    sl3 = slice(H0 + t0, H0 + t0 + BLK)
                    return e.tensor_tensor(X3a[:, :, sl3], X3a[:, :, sl3],
                                           R.pm[2][:, 0:512].rearrange("p (cc t) -> p cc t", cc=8),
                                           ALU.add)
                return body
            dad = C.V(mkadd(), [dtr])
            prev["tr"] = dad
            xs_upd.append(dad)
            if "dbg_sc" in dr and gb == 0:
                C.dma("sync", dr["dbg_sc"], sc, [dms])
                C.dma("sync", dr["dbg_pb"], pb, [dex])
                C.dma("sync", dr["dbg_pt"], ptb, [dptv])
                C.dma("sync", dr["dbg_vw"], vwt, [dvts[7]])
                C.dma("sync", dr["dbg_ot"], ot, [de0, de1])
                C.dma("sync", dr["dbg_mc"], mcol, [dz])
        prev_state = {"wmm": prev_mm_last,
                      "ev": {0: prev["out0"], 1: prev["out1"], 2: prev["tr"], 3: prev_ev.get(3)},
                      "blk": prev}

    if "dbgx" in dr:
        dx = [C.dma("sync", dr["dbgx"][cc * P:(cc + 1) * P, :], X[cc][:], [xs_upd[-1]]) for cc in range(8)]
        return ln_fm(C, R, X, X, 1e-5, [xs_upd[-1]] + dx)
    return ln_fm(C, R, X, X, 1e-5, [xs_upd[-1]], sqw=qr[:, 0:4096])


def l1attn_host(inp):
    b = np.concatenate([np.asarray(inp["L1_attn_bq"], np.float32).reshape(8, 128).T,
                        np.asarray(inp["L1_attn_bk"], np.float32).reshape(8, 128).T,
                        np.asarray(inp["L1_attn_bv"], np.float32).reshape(8, 128).T], axis=1)
    WIN, BLK = 68, 64
    masks = np.zeros((BLK, 3 * WIN), np.float32)
    for mi in range(3):
        m = np.full((BLK, WIN), -1e30, np.float32)
        for t in range(BLK):
            for i in range(WIN):
                if 0 <= i - t <= 4:
                    m[t, i] = 0.0
        if mi == 1:
            m[:, 0:2] = -1e30
        if mi == 2:
            m[:, WIN - 2:WIN] = -1e30
        masks[:, mi * WIN:(mi + 1) * WIN] = m
    return {
        "l1a_wq": np.ascontiguousarray(np.asarray(inp["L1_attn_Wq"], np.float32)),
        "l1a_wk": np.ascontiguousarray(np.asarray(inp["L1_attn_Wk"], np.float32)),
        "l1a_wv": np.ascontiguousarray(np.asarray(inp["L1_attn_Wv"], np.float32)),
        "l1a_b_t": np.ascontiguousarray(b),
        "l1a_masks": masks,
    }


def l1attn_decls():
    return [("l1a_wq", [D, D], F32), ("l1a_wk", [D, D], F32), ("l1a_wv", [D, D], F32),
            ("l1a_b_t", [P, 24], F32), ("l1a_masks", [64, 3 * 68], F32)]



def declare(nc, decls, kind="ExternalInput"):
    return {n: nc.dram_tensor(n, shp, dt, kind=kind).ap() for (n, shp, dt) in decls}


BLOCK_EMIT = {}
BLOCK_DECLS = {}


def _register_blocks():
    BLOCK_EMIT.update({"l1ffn": emit_l1ffn, "l1attn": emit_l1attn,
                       "l0attn": emit_l0attn})
    BLOCK_DECLS.update({"l1ffn": l1ffn_decls, "l1attn": l1attn_decls,
                        "l0ffn": l0ffn_decls, "l0attn": l0attn_decls})


def build_launch(blocks):
    nc = bass.Bass("TRN2", target_bir_lowering=False)
    st = ExitStack()
    _register_blocks()
    dr = {}
    decls = [("x_fm", [D, S], F32)] + Res.decls()
    for b in blocks:
        decls += BLOCK_DECLS[b]()
    dr.update(declare(nc, decls))
    dr.update(declare(nc, [("y", [S, D], F32)], kind="ExternalOutput"))
    import os
    if os.environ.get("KDBG") and "l0attn" in blocks:
        dr.update(declare(nc, [("dbg_q", [P, 2048], BF16)], kind="ExternalOutput"))
    if os.environ.get("KDBG") and "l0ffn" in blocks:
        dr.update(declare(nc, [("dbg_swr", [1, 4], F32), ("dbg_w1q", [P, 512], I8),
                               ("dbg_rows", [3, S], F32), ("dbg_xq", [P, S], BF16),
                               ("dbg_hs", [P, 512], BF16), ("dbg_x", [D, S], F32)], kind="ExternalOutput"))
    if os.environ.get("KDBG") and "l1attn" in blocks:
        dr.update(declare(nc, [("dbgq", [P, 1024], BF16), ("dbgk", [P, 1028], BF16),
                               ("dbgv", [P, 1028], BF16), ("dbgx", [D, S], F32),
                               ("dbg_sc", [64, 68], F32), ("dbg_pb", [64, 68], BF16),
                               ("dbg_pt", [68, 64], BF16), ("dbg_vw", [68, D], BF16),
                               ("dbg_ot", [64, D], F32), ("dbg_mc", [64, 4], F32)], kind="ExternalOutput"))
    if "l1ffn" in blocks:
        dr["l1f_cc_in"] = nc.dram_tensor("l1f_cc_in", [1, P], F32, kind="Internal").ap()
        dr["l1f_cc_out"] = nc.dram_tensor("l1f_cc_out", [1, P], F32, kind="Internal").ap()
    if "l0ffn" in blocks:
        dr["l0f_w1q"] = nc.dram_tensor("l0f_w1q", [D, 4096], BF16, kind="Internal").ap()
        dr["l0f_w2q"] = nc.dram_tensor("l0f_w2q", [4096, D], BF16, kind="Internal").ap()
        dr["l0f_scr"] = nc.dram_tensor("l0f_scr", [16, P], F32, kind="Internal").ap()
    if "l0attn" in blocks:
        dr["l0a_scr1"] = nc.dram_tensor("l0a_scr1", [NH, P], F32, kind="Internal").ap()
        dr["l0a_scr2"] = nc.dram_tensor("l0a_scr2", [NH, P], BF16, kind="Internal").ap()
        dr["l0a_u2"] = nc.dram_tensor("l0a_u2", [D, S], BF16, kind="Internal").ap()

    C = Ctx(nc, st)
    R = Res(C)
    cdeps = R.load(C, dr)
    X_r = C.sb("x_all", [P, 8 * S], F32)
    X = [X_r[:, cc * S:(cc + 1) * S] for cc in range(8)]
    R.X_r = X_r
    xdeps = [C.dma("sync", X_r.rearrange("p (cc t) -> p cc t", cc=8),
                   dr["x_fm"].rearrange("(cc p) t -> p cc t", p=P))]
    dbar = C.V(lambda e: None, cdeps + xdeps)
    C.T(lambda e: None, [dbar])
    C.A(lambda e: None, [dbar])
    C.step("gpsimd", lambda e: None, [dbar])

    pre = None
    if "l0ffn" in blocks:
        pre = emit_l0ffn_prelude(C, R, dr, [dbar])
    deps = [dbar]
    for bname in blocks:
        if bname == "l0ffn":
            deps = emit_l0ffn(C, R, X, dr, pre[0], pre[1])
        else:
            deps = BLOCK_EMIT[bname](C, R, X, dr)

    final_store(C, R, X, dr["y"], deps)
    C.step("sync", lambda e: None, C.all_done())
    build_and_finalize(nc, C)
    st.close()
    return nc


def kernel(**inputs):
    x = inputs["L0_attn_hidden_states"]
    consts = Res.host_arrays()
    host = {}
    blocks = []
    if "L0_attn_in_proj_w" in inputs:
        host.update(l0attn_host(inputs))
        blocks.append("l0attn")
    if "L0_ffn_w1" in inputs:
        host.update(l0ffn_host(inputs))
        blocks.append("l0ffn")
    if "L1_attn_Wq" in inputs:
        host.update(l1attn_host(inputs))
        blocks.append("l1attn")
    if "L1_ffn_weight_int8" in inputs:
        host.update(l1ffn_host(inputs))
        blocks.append("l1ffn")
    nc = build_launch(blocks)
    in_maps = []
    for b in range(NCORES):
        m = {"x_fm": np.ascontiguousarray(np.asarray(x[b]).T)}
        m.update(consts)
        m.update(host)
        in_maps.append(m)
    res = run_bass_kernel_spmd(nc, in_maps, core_ids=list(range(NCORES)))
    return np.stack([res.results[b]["y"] for b in range(NCORES)], axis=0)



def emit_l0ffn_prelude(C, R, dr, deps):
    fr = R.pf.take(64)
    slots = fr[:, 0:48]
    swc = fr[:, 48:52]
    swr = fr[0:1, 56:60]

    slab_list = []
    for kc in range(8):
        for g in range(4):
            slab_list.append((dr["l0f_w1t"][kc * P:(kc + 1) * P, g * 1024:(g + 1) * 1024],
                              dr["l0f_w1q"][kc * P:(kc + 1) * P, g * 1024:(g + 1) * 1024],
                              kc))
    for kc in range(32):
        slab_list.append((dr["l0f_w2t"][kc * P:(kc + 1) * P, :],
                          dr["l0f_w2q"][kc * P:(kc + 1) * P, :],
                          16 + kc // 2))
    wslab_r = R.pbf.take(8192)
    bigs = [R.big[:, 0:1024], R.big[:, 1024:2048],
            wslab_r[:, 0:2048].bitcast(F32), wslab_r[:, 2048:4096].bitcast(F32)]
    qsl_r = R.pbf.take(2048)
    qbs = [qsl_r[:, 0:1024], qsl_r[:, 1024:2048]]

    red_deps = []
    prev_red = [None, None, None, None]
    acc_seen = set()
    for n, (ap_in, ap_out, slot) in enumerate(slab_list):
        buf = bigs[n % 4]
        dl = C.dma("sync" if n % 2 == 0 else "gpsimd", buf, ap_in,
                   list(deps) + ([prev_red[n % 4]] if prev_red[n % 4] else []))
        first = slot not in acc_seen
        acc_seen.add(slot)
        acol_ = fr[:, 60 + (n % 2):61 + (n % 2)]
        dabs = C.A(lambda e, buf=buf, acol_=acol_: e.activation(
            buf, buf, AF.Abs, bias=0.0, scale=1.0, accum_out=acol_),
            [dl] + (red_deps[-2:-1] if len(red_deps) >= 2 else []))
        if first:
            dr_ = C.V(lambda e, slot=slot, acol_=acol_: e.tensor_copy(
                slots[:, slot:slot + 1], acol_), [dabs])
        else:
            dr_ = C.V(lambda e, slot=slot, acol_=acol_: e.tensor_tensor(
                slots[:, slot:slot + 1], slots[:, slot:slot + 1], acol_, ALU.add),
                [dabs] + red_deps[-1:])
        prev_red[n % 4] = dr_
        red_deps.append(dr_)
    d1 = C.V(lambda e: e.tensor_reduce(swc[:, 0:1], slots[:, 0:8], AX.X, ALU.add), red_deps[-2:])
    d2 = C.V(lambda e: e.tensor_reduce(swc[:, 1:2], slots[:, 16:32], AX.X, ALU.add), [d1])
    d3 = C.T(lambda e: e.transpose(R.pss[0:2, 0:P], swc[:, 0:2], R.idf[:]), [d2])
    d4a = C.V(lambda e: e.tensor_copy(R.ab[0:2, 0:P], R.pss[0:2, 0:P]), [d3])
    d4b = C.V(lambda e: e.tensor_reduce(R.ab[0:2, P:P + 1], R.ab[0:2, 0:P], AX.X, ALU.add), [d4a])
    d4c = C.T(lambda e: e.transpose(R.pss[0:1, 256:258], R.ab[0:2, P:P + 1], R.idf[0:2, 0:2]), [d4b, d3])
    d5 = C.V(lambda e: e.tensor_copy(swr[0:1, 0:2], R.pss[0:1, 256:258]), [d4c])
    d6 = C.V(lambda e: e.tensor_scalar(swr[0:1, 0:2], swr[0:1, 0:2], 1.0 / (4096.0 * 1024.0), 1e-5,
                                       ALU.mult, ALU.max), [d5])
    d7 = C.V(lambda e: e.reciprocal(swr[0:1, 2:4], swr[0:1, 0:2]), [d6])
    d8 = C.T(lambda e: e.matmul(R.pss[:, 0:2], R.onerf[0:1, :], swr[0:1, 2:4], start=True, stop=True),
             [d7])
    d9 = C.V(lambda e: e.tensor_copy(swc[:, 2:4], R.pss[:, 0:2]), [d8])

    q_deps = []
    prev_qv = [None, None, None, None]
    for n, (ap_in, ap_out, slot) in enumerate(slab_list):
        buf = bigs[n % 4]
        qb = qbs[n % 2]
        swcol = swc[:, 2:3] if n < 32 else swc[:, 3:4]
        dl = C.dma("sync" if n % 2 == 0 else "gpsimd", buf, ap_in,
                   [d9] + ([prev_qv[n % 4]] if prev_qv[n % 4] else []))

        def mkq(buf=buf, qb=qb, swcol=swcol):
            def body(e):
                e.tensor_scalar(buf, buf, swcol, None, ALU.mult)
                e.tensor_scalar(buf, buf, MAGIC, MAGIC, ALU.add, ALU.subtract)
                return e.tensor_scalar(qb[:], buf, -1.0, 1.0, ALU.max, ALU.min)
            return body
        dq = C.V(mkq(), [dl] + (q_deps[-2:-1] if len(q_deps) >= 2 else []))
        ds = C.dma("gpsimd", ap_out, qb[:], [dq])
        prev_qv[n % 4] = dq
        q_deps.append(ds)
    return q_deps[-2:], swr


def emit_l0ffn(C, R, X, dr, pre_deps, swr):
    NCH = 32
    bar = C.all_done() + list(pre_deps)
    R.pbf.reset()
    R.pf.reset(keep=64)
    xqr = R.pbf.take(8 * S)
    xq = [xqr[:, c * S:(c + 1) * S] for c in range(8)]
    wtr = R.pbf.take(2 * 8 * 512)
    wt = [wtr[:, i * 4096:(i + 1) * 4096] for i in range(2)]
    hsr = R.pbf.take(NCH * 512)
    hs = [hsr[:, c * 512:(c + 1) * 512] for c in range(NCH)]

    fq = R.pf.take(2 * 515)
    hraw = [fq[:, i * 515:(i + 1) * 515] for i in range(2)]
    fc = R.pf.take(256)
    cw = fc[:, 0:96]
    cb = fc[:, 96:128]
    hleft = fc[:, 128:160]
    hrsc = fc[:, 192:193]
    s2b = R.pf.take(512)
    o1b = R.pf.take(512)
    o2b = R.pf.take(512)

    dcw = C.dma("sync", cw, dr["l0f_convw"], bar)
    dcb = C.dma("sync", cb, dr["l0f_convb"], bar)

    amxc = R.pf.take(16)
    X3f = R.X_r.rearrange("p (cc t) -> p t cc", cc=8)
    tp_deps = []
    prev_tp = None
    dtree = C.V(lambda e: e.tensor_reduce(R.big[:, 0:S], X3f, AX.X, ALU.max,
                                          apply_absolute_value=True), bar)
    for tt in range(16):
        dtp = C.T(lambda e, tt=tt: e.transpose(R.pss[:, (tt % 4) * P:(tt % 4 + 1) * P],
                                               R.big[:, tt * P:(tt + 1) * P], R.idf[:]),
                  [dtree] + ([prev_tp] if prev_tp else []))
        dtv = C.V(lambda e, tt=tt: e.tensor_reduce(amxc[:, tt:tt + 1],
                                                   R.pss[:, (tt % 4) * P:(tt % 4 + 1) * P],
                                                   AX.X, ALU.max), [dtp])
        prev_tp = dtv
        tp_deps.append(dtv)
    dft = C.T(lambda e: e.transpose(R.pm[4][0:16, 0:P], amxc[:, 0:16], R.idf[:]), tp_deps)
    dfv = C.V(lambda e: e.tensor_copy(R.ab[0:16, 0:P], R.pm[4][0:16, 0:P]), [dft])
    prev_pm4_init = dfv
    dfi = C.dma("sync", R.r0[:, 0:S], R.ab[0:16, 0:P], [dfv])
    dsx = C.V(lambda e: e.tensor_single_scalar(R.r0[:, 0:S], R.r0[:, 0:S], 1e-5, ALU.max), [dfi])
    dr1 = C.V(lambda e: e.reciprocal(R.r1[:, 0:S], R.r0[:, 0:S]), [dsx])
    dr2 = C.V(lambda e: e.tensor_single_scalar(R.r1[:, 0:S], R.r1[:, 0:S], 127.0, ALU.mult), [dr1])
    do1r = C.V(lambda e: e.tensor_scalar(R.r2[:, 0:S], R.r0[:, 0:S], swr[0:1, 0:1], 1.0 / 127.0,
                                         ALU.mult, ALU.mult), [dsx] + list(pre_deps))

    q_deps = []
    prev_b = None
    qv_last = None
    X3q = R.X_r.rearrange("p (cc t) -> p cc t", cc=8)
    xq3 = xqr.rearrange("p (cc t) -> p cc t", cc=8)
    bigq = R.big.rearrange("p (ch q) -> p ch q", ch=4)
    for ng in range(4):
        sl = slice(ng * 512, (ng + 1) * 512)
        db = bcast_row(C, R, R.ab, R.r1[:, sl], 512, [dr2, do1r, prev_b, dfv], R.pm[4])
        for se in range(2):
            g_ = C.G(lambda e, se=se, sl=sl: e.tensor_tensor(
                bigq, X3q[:, se * 4:(se + 1) * 4, sl],
                R.ab[:, None, 0:512].broadcast_to([P, 4, 512]), ALU.mult),
                [db] + ([qv_last] if qv_last else []))
            qv_last = C.V(lambda e, se=se, sl=sl: e.tensor_scalar(
                xq3[:, se * 4:(se + 1) * 4, sl], bigq, MAGIC, MAGIC, ALU.add, ALU.subtract), [g_])
            q_deps.append(qv_last)
        prev_b = qv_last

    dhl = C.V(lambda e: e.memset(hleft, 0.0), bar)
    prev_pm4_init = prev_b if prev_b else prev_pm4_init

    if "dbg_swr" in dr:
        C.dma("sync", dr["dbg_swr"], swr, list(pre_deps))
        C.dma("sync", dr["dbg_w1q"], dr["l0f_w1q"][0:P, 0:512], list(pre_deps))
        C.dma("sync", dr["dbg_rows"][0:1, :], R.r1[:, 0:S], [dr2])
        C.dma("sync", dr["dbg_rows"][1:2, :], R.r2[:, 0:S], [do1r])
        dbg_xq = C.dma("sync", dr["dbg_xq"], xq[0][:, 0:S], q_deps)
    ev_deps = []
    prev_pm = {4: prev_pm4_init}
    prev_w2_last = None
    fuser = {}
    prev_conv = {}
    prev_wdma = None
    prev_hq = None
    prev_hrsc = None
    prev_cbuf = {}
    prev_dev2 = None
    for qq in range(4):
        sl = slice(qq * 512, (qq + 1) * 512)
        do1 = bcast_row(C, R, o1b, R.r2[:, sl], 512,
                        [do1r] + ([prev_pm.get(5)] if prev_pm.get(5) else []), R.pm[5])
        prev_pm[5] = do1
        if qq < 3:
            t1 = (qq + 1) * 512
            dhs = C.T(lambda e, t1=t1: e.matmul(R.pss[:, 508:509], R.onerf[0:1, :],
                                                R.r2[:, t1:t1 + 1], start=True, stop=True),
                      [do1r] + ([prev_pm.get(7)] if prev_pm.get(7) else []))
            dhsv = C.V(lambda e: e.tensor_copy(hrsc, R.pss[:, 508:509]), [dhs])
            prev_pm[7] = dhsv
            prev_hrsc = dhsv

        conv_done = []
        for mg in range(8):
            wb = wt[mg % 2]
            wdeps = [C.dma("sync", wb.rearrange("p (kc c) -> p kc c", kc=8),
                           dr["l0f_w1q"].rearrange("(kc p) m -> p kc m", p=P)
                           [:, :, mg * 512:(mg + 1) * 512],
                           [fuser.get(mg % 2)] if fuser.get(mg % 2) else bar)]
            for sub in range(4):
                mc = mg * 4 + sub
                pbi = (mg * 4 + sub) % 4
                pbank = R.pm[pbi]

                def mk1(wb=wb, sub=sub, sl=sl, pbank=pbank):
                    def body(e):
                        last = None
                        for kc in range(8):
                            last = e.matmul(pbank[:, 0:512],
                                            wb[:, kc * 512 + sub * P: kc * 512 + (sub + 1) * P],
                                            xq[kc][:, sl], start=(kc == 0), stop=(kc == 7))
                        return last
                    return body
                dmm = C.T(mk1(), wdeps + q_deps + ([prev_pm.get(pbi)] if prev_pm.get(pbi) else []))
                if sub == 3:
                    prev_wdma = dmm
                    fuser[mg % 2] = dmm
                dhm = None
                if qq < 3:
                    def mkh(wb=wb, sub=sub, t1=(qq + 1) * 512, mc=mc):
                        def body(e):
                            last = None
                            for kc in range(8):
                                last = e.matmul(R.pm[4][:, mc:mc + 1],
                                                wb[:, kc * 512 + sub * P: kc * 512 + (sub + 1) * P],
                                                xq[kc][:, t1:t1 + 1], start=(kc == 0), stop=(kc == 7))
                            return last
                        return body
                    dhm = C.T(mkh(), ([prev_pm.get(4)] if prev_pm.get(4) else []))

                hb = hraw[mc % 2]
                dev = C.V(lambda e, pbank=pbank, hb=hb: e.tensor_tensor(hb[:, 1:513], pbank[:, 0:512],
                                                                        o1b, ALU.mult),
                          [dmm, do1] + ([prev_conv.get(mc % 2)] if prev_conv.get(mc % 2) else []) + [dhl])
                prev_pm[pbi] = dev

                def mkhal(hb=hb, mc=mc, qq=qq):
                    def body(e):
                        e.tensor_copy(hb[:, 0:1], hleft[:, mc:mc + 1])
                        if qq < 3:
                            return e.tensor_scalar(hb[:, 513:514], R.pm[4][:, mc:mc + 1],
                                                   hrsc, None, ALU.mult)
                        return e.memset(hb[:, 513:514], 0.0)
                    return body
                hal_deps = [dev] + ([dhm] if dhm else []) + ([prev_hrsc] if qq < 3 else [])
                dhal = C.V(mkhal(), hal_deps)
                if dhm is not None:
                    prev_pm[4] = dhal
                dlh = C.V(lambda e, hb=hb, mc=mc: e.tensor_copy(hleft[:, mc:mc + 1], hb[:, 512:513]),
                          [dhal])

                cbuf = R.bb if mc % 2 == 0 else R.ab
                c1 = C.G(lambda e, hb=hb, mc=mc, cbuf=cbuf: e.tensor_tensor(
                    cbuf[:], hb[:, 0:512], cw[:, mc:mc + 1].broadcast_to([P, 512]), ALU.mult),
                    [dhal, dlh, dcw] + ([prev_cbuf.get(mc % 2)] if prev_cbuf.get(mc % 2) else [])
                    + ([prev_dev2] if mc <= 1 and prev_dev2 else []))

                def mkc23(hb=hb, mc=mc, cbuf=cbuf):
                    def body(e):
                        e.scalar_tensor_tensor(cbuf[:], hb[:, 1:513],
                                               cw[:, 32 + mc:33 + mc], cbuf[:],
                                               ALU.mult, ALU.add)
                        return e.scalar_tensor_tensor(cbuf[:], hb[:, 2:514],
                                                      cw[:, 64 + mc:65 + mc], cbuf[:],
                                                      ALU.mult, ALU.add)
                    return body
                c3 = C.V(mkc23(), [c1])
                c4 = C.A(lambda e, mc=mc, cbuf=cbuf: e.activation(hs[mc][:], cbuf[:], AF.Silu,
                                                                  bias=cb[:, mc:mc + 1], scale=1.0),
                         [c3, dcb] + ([prev_w2_last] if prev_w2_last else []))
                prev_cbuf[mc % 2] = c4
                prev_conv[mc % 2] = c4
                conv_done.append(c4)

        if "dbg_swr" in dr and qq == 0:
            C.dma("sync", dr["dbg_hs"], hs[0][:], conv_done)
        hs3 = hsr.rearrange("p (ch q) -> p q ch", ch=NCH)
        dt2 = C.V(lambda e: e.tensor_reduce(R.sq[1][:], hs3, AX.X, ALU.max,
                                            apply_absolute_value=True), conv_done)
        t2deps = []
        prev_t2 = prev_pm.get(7)
        for tt in range(4):
            dtp = C.T(lambda e, tt=tt: e.transpose(R.ptr[:, tt * P:(tt + 1) * P],
                                                   R.sq[1][:, tt * P:(tt + 1) * P], R.idb[:]),
                      [dt2] + ([prev_t2] if prev_t2 else []) + ([prev_pm.get(6)] if prev_pm.get(6) else []))
            dtv = C.V(lambda e, tt=tt: e.tensor_reduce(amxc[:, tt:tt + 1],
                                                       R.ptr[:, tt * P:(tt + 1) * P], AX.X, ALU.max),
                      [dtp])
            prev_t2 = dtv
            t2deps.append(dtv)
        prev_pm[7] = prev_t2
        prev_pm[6] = prev_t2
        dft2 = C.T(lambda e: e.transpose(R.pm[4][0:4, 0:P], amxc[:, 0:4], R.idf[:]),
                   t2deps + ([prev_pm.get(44)] if prev_pm.get(44) else []))
        dfv2 = C.V(lambda e: e.tensor_copy(R.ab[0:4, 0:P], R.pm[4][0:4, 0:P]), [dft2])
        prev_pm[44] = dfv2
        prev_pm[4] = dfv2
        dfi2 = C.dma("sync", R.r0[:, sl], R.ab[0:4, 0:P], [dfv2])
        dc2 = C.V(lambda e, sl=sl: e.tensor_single_scalar(R.r0[:, sl], R.r0[:, sl], 1e-5, ALU.max), [dfi2])
        ds2 = C.V(lambda e, sl=sl: e.reciprocal(R.r1[:, sl], R.r0[:, sl]), [dc2])
        ds2b = C.V(lambda e, sl=sl: e.tensor_single_scalar(R.r1[:, sl], R.r1[:, sl], 127.0, ALU.mult), [ds2])
        do2r = C.V(lambda e, sl=sl: e.tensor_scalar(R.r0[:, sl], R.r0[:, sl], swr[0:1, 1:2], 1.0 / 127.0,
                                                    ALU.mult, ALU.mult), [ds2b] + list(pre_deps))
        db2 = bcast_row(C, R, s2b, R.r1[:, sl], 512,
                        [ds2b] + ([prev_pm.get(5)] if prev_pm.get(5) else []), R.pm[5])
        do2 = bcast_row(C, R, o2b, R.r0[:, sl], 512, [do2r, db2], R.pm[5])
        prev_pm[5] = do2

        if "dbg_swr" in dr and qq == 0:
            C.dma("sync", dr["dbg_rows"][2:3, 0:512], R.r1[:, sl], [ds2b])
        hq_deps = []
        hs3b = hsr.rearrange("p (ch q) -> p ch q", ch=NCH)
        big3 = R.big.rearrange("p (ch q) -> p ch q", ch=4)
        dprev = [db2] + ([prev_hq] if prev_hq else [])
        prev_rnd = None
        for blk4 in range(8):
            dmul = C.G(lambda e, blk4=blk4: e.tensor_tensor(
                big3, hs3b[:, blk4 * 4:(blk4 + 1) * 4, :],
                s2b[:, None, 0:512].broadcast_to([P, 4, 512]), ALU.mult),
                dprev if blk4 == 0 else [prev_rnd])
            prev_rnd = C.V(lambda e, blk4=blk4: e.tensor_scalar(
                hs3b[:, blk4 * 4:(blk4 + 1) * 4, :], big3,
                MAGIC, MAGIC, ALU.add, ALU.subtract), [dmul])
            hq_deps.append(prev_rnd)
        prev_hq = prev_rnd

        for mcg in range(2):
            for kb in range(4):
                wb = wt[kb % 2]
                wdeps2 = [C.dma("sync", wb.rearrange("p (kc c) -> p kc c", kc=8),
                                dr["l0f_w2q"].rearrange("(kc p) m -> p kc m", p=P)
                                [:, kb * 8:(kb + 1) * 8, mcg * 512:(mcg + 1) * 512],
                                [fuser.get(kb % 2)] if fuser.get(kb % 2) else [prev_wdma])]
                for sub in range(4):
                    mc = mcg * 4 + sub
                    pbi = sub
                    pbank = R.pm[pbi]

                    def mk2(wb=wb, sub=sub, sl=sl, pbank=pbank, kb=kb):
                        def body(e):
                            last = None
                            for j in range(8):
                                kc = kb * 8 + j
                                last = e.matmul(pbank[:, 0:512],
                                                wb[:, j * 512 + sub * P: j * 512 + (sub + 1) * P],
                                                hs[kc][:, 0:512],
                                                start=(kb == 0 and j == 0), stop=(kb == 3 and j == 7))
                            return last
                        return body
                    mm2deps = wdeps2 + hq_deps[2 * kb:2 * kb + 2]
                    if kb == 0 and prev_pm.get(pbi):
                        mm2deps.append(prev_pm[pbi])
                    dmm2 = C.T(mk2(), mm2deps)
                    prev_w2_last = dmm2
                    if sub == 3:
                        prev_wdma = dmm2
                        fuser[kb % 2] = dmm2
                    if kb == 3:
                        def mkev2(pbank=pbank, mc=mc, sl=sl):
                            def body(e):
                                e.tensor_tensor(R.ab[:], pbank[:, 0:512], o2b, ALU.mult)
                                return e.tensor_tensor(X[mc][:, sl], X[mc][:, sl], R.ab[:], ALU.add)
                            return body
                        dev2 = C.V(mkev2(), [dmm2, do2])
                        prev_pm[pbi] = dev2
                        prev_dev2 = dev2
                        ev_deps.append(dev2)

    if "dbg_swr" in dr:
        dxx = [C.dma("sync", dr["dbg_x"][cc * P:(cc + 1) * P, :], X[cc][:], ev_deps) for cc in range(8)]
        ev_deps = ev_deps + dxx
    return ln_fm(C, R, X, X, 1e-5, ev_deps, sqw=xqr[:, 0:4096])


def l0ffn_host(inp):
    w1 = np.asarray(inp["L0_ffn_w1"], np.float32)
    w2 = np.asarray(inp["L0_ffn_w2"], np.float32)
    cwv = np.asarray(inp["L0_ffn_conv_w"], np.float32)
    cbv = np.asarray(inp["L0_ffn_conv_b"], np.float32)
    cw = np.zeros((P, 96), np.float32)
    for k in range(3):
        cw[:, k * 32:(k + 1) * 32] = cwv[:, 0, k].reshape(32, 128).T
    cb = cbv.reshape(32, 128).T
    return {
        "l0f_w1t": np.ascontiguousarray(w1.T),
        "l0f_w2t": np.ascontiguousarray(w2.T),
        "l0f_convw": np.ascontiguousarray(cw),
        "l0f_convb": np.ascontiguousarray(cb),
    }


def l0ffn_decls():
    return [("l0f_w1t", [D, 4096], F32), ("l0f_w2t", [4096, D], F32),
            ("l0f_convw", [P, 96], F32), ("l0f_convb", [P, 32], F32)]





NH = 32
NMC = 35
EMIN_CLAMP = 30.0


def emit_l0attn(C, R, X, dr):
    import os
    PH = int(os.environ.get("KPHASE", "99"))
    KCH = int(os.environ.get("KCH", "99"))
    bar = C.all_done()
    R.pbf.reset()
    R.pf.reset(keep=64)

    u_r = R.pbf.take(4096)
    u = [u_r[:, c * 512:(c + 1) * 512] for c in range(8)]
    tmpb = u
    exps_pair2 = u_r[0:2, 0:2048]
    Mb_r = u_r[:, 2048:3072]
    sz_r = R.pbf.take(16 * 512)
    sz = [sz_r[:, c * 512:(c + 1) * 512] for c in range(16)]
    xbcs_r = R.pbf.take(18 * 512)
    xbcs = [xbcs_r[:, c * 512:(c + 1) * 512] for c in range(18)]
    xstm_r = R.pbf.take(2048)
    exps_hm_bf = xstm_r[0:NH, 0:128]
    U_r = R.pbf.take(2048)
    M_r = R.pbf.take(1024)
    Lp_r = R.pbf.take(1024)
    G_r = R.pbf.take(128)
    Btm_r = R.pbf.take(128)
    hstb_r = R.pbf.take(2048)
    Y2_r = R.pbf.take(16 * 512)
    Y2 = [Y2_r[:, c * 512:(c + 1) * 512] for c in range(16)]
    wst_r = R.pbf.take(4096)
    wst = [wst_r[:, i * 2048:(i + 1) * 2048] for i in range(2)]
    EXPSP = wst_r[:, 0:2048]

    dtq = R.pf.take(512)[0:NH, :]
    hstate = R.pf.take(2048)
    raw_r = R.pf.take(1036)
    hraw = [raw_r[:, i * 518:i * 518 + 515] for i in range(2)]
    Ebuf = raw_r[:, 0:1024]
    fcm = R.pf.take(256)
    cwc = fcm[:, 0:72]
    cbc = fcm[:, 72:90]
    xleft = fcm[:, 90:144]
    nwc = fcm[:, 144:152]
    pwc = fcm[:, 152:160]
    dcol = fcm[:, 160:176]
    acol = fcm[0:NH, 176:177]
    dtbc = fcm[0:NH, 177:178]
    expq = fcm[:, 192:224]
    wsmc = fcm[:, 224:240]
    sm_r = R.pf.take(768)
    s_hm = sm_r[0:NH, 0:128]
    dtA = sm_r[0:NH, 128:256]
    lndt = sm_r[0:NH, 256:384]
    nslndt = sm_r[0:NH, 384:512]
    w_hm = sm_r[0:NH, 512:640]
    dtw_hm = sm_r[0:NH, 640:768]
    tm_r = R.pf.take(64)
    nslndt_tm = tm_r[:, 0:32]
    dtw_tm = tm_r[:, 32:64]
    onesrow = R.pf.take(128)[0:NH, :]
    s_flat = R.rows[0:1, 2048:2048 + 4096]

    dcw = C.dma("sync", cwc, dr["l0a_convw"], bar)
    dcb2 = C.dma("sync", cbc, dr["l0a_convb"], bar)
    dnw = C.dma("sync", nwc, dr["l0a_normw"], bar)
    dpw = C.dma("sync", pwc, dr["l0a_postw"], bar)
    dDc = C.dma("sync", dcol, dr["l0a_dcol"], bar)
    dAc = C.dma("sync", fcm[0:NH, 176:178], dr["l0a_adt"], bar)
    dws = C.dma("sync", wsmc, dr["l0a_wssm"], bar)
    dsel = C.dma("sync", R.selp[:], dr["c_selp"], bar)
    dmk01 = C.dma("sync", R.mask01[:], dr["c_mask01"], bar)
    dini = C.V(lambda e: (e.memset(xleft, 0.0), e.memset(onesrow, 1.0),
                          e.memset(hstate, 0.0), e.memset(hstb_r, 0.0))[-1], bar)
    dAe = C.A(lambda e: e.activation(acol, acol, AF.Exp, bias=0.0, scale=1.0), [dAc])
    dAn = C.V(lambda e: e.tensor_single_scalar(acol, acol, -1.0, ALU.mult), [dAe])

    u2_deps = []
    PIPE = {}
    pbar = bar + [dcw, dcb2, dnw, dpw, dDc, dAn, dws, dsel, dmk01, dini]
    for qq in range(4):
        sl = slice(qq * 512, (qq + 1) * 512)

        X3m = R.X_r.rearrange("p (cc t) -> p cc t", cc=8)
        sq3 = Y2_r[:, 0:4096].rearrange("p (cc t) -> p cc t", cc=8)
        st3 = Y2_r[:, 4096:8192].rearrange("p (cc t) -> p cc t", cc=8)
        dqw = C.A(lambda e, sl=sl: e.activation(sq3, X3m[:, :, sl],
                                                AF.Square, bias=0.0, scale=1.0), pbar)

        def mksqmm():
            def body(e):
                last = None
                for cc in range(8):
                    last = e.matmul(R.pss[0:1, 0:512], R.m10b[:, 0:1], Y2[cc][:],
                                    start=(cc == 0), stop=(cc == 7))
                return last
            return body
        prevmm = C.T(mksqmm(), [dqw] + pbar)
        d2 = C.A(lambda e: e.activation(R.r0[:, 0:512], R.pss[0:1, 0:512], AF.Sqrt,
                                        bias=R.eps56[0:1, 1:2], scale=1.0), [prevmm])
        d3 = C.V(lambda e: e.reciprocal(R.r0[:, 0:512], R.r0[:, 0:512]), [d2])
        db = C.T(lambda e: e.matmul(R.pm[4][:, 0:512], R.onerf[0:1, :], R.r0[0:1, 0:512],
                                    start=True, stop=True), [d3] + pbar)
        dcpa = C.V(lambda e: e.tensor_copy(R.ab[:], R.pm[4][:, 0:512]), [db])
        u3 = u_r.rearrange("p (cc t) -> p cc t", cc=8)
        du1 = C.G(lambda e, sl=sl: e.tensor_tensor(
            st3, X3m[:, :, sl],
            R.ab[:, None, 0:512].broadcast_to([P, 8, 512]), ALU.mult), [dcpa])
        du2w = C.V(lambda e: e.tensor_tensor(
            u3, st3,
            nwc[:, :, None].broadcast_to([P, 8, 512]), ALU.mult), [du1])
        u_deps = [du2w]
        ubar = C.all_done()

        conv_ct = 0
        prev_wd = None
        prev_pmb = {}
        prev_cv = {}
        wuser = {}
        for mcg in range(18):
            wb = wst[mcg % 2]
            mclist = [m for m in (2 * mcg, 2 * mcg + 1) if m < NMC]
            ncols = 256 if len(mclist) == 2 else 32
            wdeps = [C.dma("gpsimd",
                           wb.rearrange("p (kc c) -> p kc c", kc=8)[:, :, 0:ncols],
                           dr["l0a_inproj_t"].rearrange("(kc p) m -> p kc m", p=P)
                           [:, :, 2 * mcg * P: 2 * mcg * P + ncols],
                           ([wuser.get(mcg % 2)] if wuser.get(mcg % 2) else ubar))]
            for mi, mc in enumerate(mclist):
                mwid = 32 if mc == 34 else 128
                pbi = (2 * mcg + mi) % 4
                pbank = R.pm[pbi]

                def mkmm(wb=wb, mi=mi, pbank=pbank, mwid=mwid):
                    def body(e):
                        last = None
                        for kc in range(8):
                            last = e.matmul(pbank[0:mwid, 0:512],
                                            wb[:, kc * 256 + mi * P: kc * 256 + mi * P + mwid],
                                            u[kc][:, :], start=(kc == 0), stop=(kc == 7))
                        return last
                    return body
                dmm = C.T(mkmm(), wdeps + ubar + ([prev_pmb.get(pbi)] if prev_pmb.get(pbi) else []))
                prev_wd = dmm
                wuser[mcg % 2] = dmm

                if mc < 16:
                    dev = C.A(lambda e, mc=mc, pbank=pbank:
                              e.activation(sz[mc][:], pbank[0:P, 0:512], AF.Silu, bias=0.0, scale=1.0),
                              [dmm])
                    prev_pmb[pbi] = dev
                elif mc < 34:
                    xc = mc - 16
                    hb = hraw[conv_ct % 2]
                    hkey = conv_ct % 2
                    conv_ct += 1
                    dev = C.V(lambda e, pbank=pbank, hb=hb:
                              e.tensor_copy(hb[:, 3:515], pbank[0:P, 0:512]),
                              [dmm] + ([prev_cv.get(hkey)] if prev_cv.get(hkey) else []))
                    prev_pmb[pbi] = dev
                    dhal = C.V(lambda e, hb=hb, xc=xc: e.tensor_copy(hb[:, 0:3], xleft[:, xc * 3:xc * 3 + 3]),
                               [dev])
                    dlh = C.V(lambda e, hb=hb, xc=xc: e.tensor_copy(xleft[:, xc * 3:xc * 3 + 3], hb[:, 512:515]),
                              [dhal])
                    a1 = C.A(lambda e, hb=hb, xc=xc: e.activation(R.ab[:], hb[:, 0:512], AF.Identity,
                                                                  bias=0.0, scale=cwc[:, xc:xc + 1]),
                             [dhal, dlh])
                    a2 = C.A(lambda e, hb=hb, xc=xc: e.activation(R.bb[:], hb[:, 1:513], AF.Identity,
                                                                  bias=0.0, scale=cwc[:, 18 + xc:19 + xc]),
                             [a1])
                    s1 = C.G(lambda e: e.tensor_tensor(R.ab[:], R.ab[:], R.bb[:], ALU.add), [a2])
                    v1 = C.V(lambda e, hb=hb, xc=xc: e.scalar_tensor_tensor(R.ab[:], hb[:, 2:514],
                                                                            cwc[:, 36 + xc:37 + xc],
                                                                            R.ab[:], ALU.mult, ALU.add),
                             [s1])
                    v2 = C.V(lambda e, hb=hb, xc=xc: e.scalar_tensor_tensor(R.ab[:], hb[:, 3:515],
                                                                            cwc[:, 54 + xc:55 + xc],
                                                                            R.ab[:], ALU.mult, ALU.add),
                             [v1])
                    c4 = C.A(lambda e, xc=xc: e.activation(xbcs[xc][:], R.ab[:], AF.Silu,
                                                           bias=cbc[:, xc:xc + 1], scale=1.0), [v2])
                    prev_cv[hkey] = c4
                else:
                    dev = C.V(lambda e, pbank=pbank: e.tensor_scalar(dtq[:, :], pbank[0:NH, 0:512],
                                                                     dtbc, None, ALU.add),
                              [dmm])
                    prev_pmb[pbi] = dev
                    sp1 = C.A(lambda e: e.activation(R.ab[0:NH, :], dtq, AF.Abs, bias=0.0, scale=1.0), [dev])
                    sp2 = C.A(lambda e: e.activation(R.ab[0:NH, :], R.ab[0:NH, :], AF.Exp, bias=0.0,
                                                     scale=-1.0), [sp1])
                    sp3 = C.A(lambda e: e.activation(R.ab[0:NH, :], R.ab[0:NH, :], AF.Ln, bias=1.0,
                                                     scale=1.0), [sp2])
                    sp4 = C.V(lambda e: e.tensor_single_scalar(R.bb[0:NH, :], dtq, 0.0, ALU.max), [sp3])
                    C.V(lambda e: e.tensor_tensor(dtq, R.bb[0:NH, :], R.ab[0:NH, :], ALU.add), [sp4])
        bbar = C.all_done()
        if "dbg_q" in dr and qq == 0:
            C.dma("sync", dr["dbg_q"][0:P, 0:512], sz[0], bbar)
            C.dma("sync", dr["dbg_q"][0:P, 512:1024], xbcs[0][:], bbar)
            C.dma("sync", dr["dbg_q"][0:P, 1024:1536], xbcs[17][:], bbar)
            C.dma("gpsimd", dr["dbg_q"][0:NH, 1536:2048], dtq, bbar)

        for c in (range(4) if PH >= 3 else []):
            csl = slice(c * P, (c + 1) * P)
            pp = PIPE
            m1 = C.A(lambda e, csl=csl: e.activation(dtA, dtq[:, csl], AF.Identity, bias=0.0, scale=acol),
                     bbar + [pp.get("m2")])
            m2 = C.V(lambda e: e.tensor_tensor_scan(s_hm, onesrow, dtA, 0.0, ALU.mult, ALU.add),
                     [m1, pp.get("m4"), pp.get("m5"), pp.get("m7"), pp.get("f1"), pp.get("q0")])
            m3 = C.A(lambda e, csl=csl: e.activation(lndt, dtq[:, csl], AF.Ln, bias=0.0, scale=1.0),
                     bbar + [pp.get("m4")])
            m4 = C.V(lambda e: e.tensor_tensor(nslndt, lndt, s_hm, ALU.subtract), [m2, m3, pp.get("t1")])
            m5 = C.A(lambda e: e.activation(w_hm, s_hm, AF.Exp, bias=s_hm[:, 127:128], scale=-1.0),
                     [m2, pp.get("m6")])
            m6 = C.V(lambda e, csl=csl: e.tensor_tensor(dtw_hm, dtq[:, csl], w_hm, ALU.mult),
                     [m5, pp.get("t2")])
            m7 = C.A(lambda e: e.activation(exps_hm_bf, s_hm, AF.Exp, bias=0.0, scale=1.0),
                     [m2, pp.get("f3"), pp.get("din")])
            t1 = C.T(lambda e: e.transpose(R.pss[:, 256:288], nslndt, R.idf[0:NH, 0:NH]),
                     [m4, pp.get("t1v"), pp.get("xvo")])
            t1v = C.V(lambda e: e.tensor_copy(nslndt_tm, R.pss[:, 256:288]), [t1, pp.get("ee")])
            t2 = C.T(lambda e: e.transpose(R.pss[:, 288:320], dtw_hm, R.idf[0:NH, 0:NH]), [m6, t1v])
            t2v = C.V(lambda e: e.tensor_copy(dtw_tm, R.pss[:, 288:320]), [t2, pp.get("uU")])
            f2 = C.dma("sync", s_flat, s_hm, [m2, pp.get("sr")])
            f1 = f2
            f3 = C.dma("sync", dr["l0a_scr2"], exps_hm_bf, [m7, pp.get("f4")])
            f4 = C.dma("sync", exps_pair2.rearrange("p (hh t) -> p hh t", hh=16),
                       dr["l0a_scr2"].rearrange("(hh two) t -> two hh t", two=2),
                       [f3, pp.get("e4")])
            q0 = C.T(lambda e: e.transpose(R.pss[0:1, 352:384], s_hm[:, 127:128],
                                           R.idf[0:NH, 0:NH]), [m2, t2v])
            q0v = C.V(lambda e: e.tensor_copy(R.r0[0:1, 512:544], R.pss[0:1, 352:384]), [q0])
            q1 = C.T(lambda e: e.matmul(R.pss[:, 320:352], R.onerf[0:1, :],
                                        R.r0[0:1, 512:544], start=True, stop=True),
                     [q0v, pp.get("hu1"), pp.get("q2")])
            q2 = C.A(lambda e: e.activation(expq, R.pss[:, 320:352], AF.Exp, bias=0.0, scale=1.0), [q1])
            if KCH < 2:
                continue
            g1 = C.T(lambda e, csl=csl: e.matmul(R.pss[0:P, 0:128], xbcs[16][:, csl], xbcs[17][:, csl],
                                                 start=True, stop=True), [q2] + bbar)
            g2 = C.A(lambda e: e.activation(G_r[:], R.pss[0:P, 0:128], AF.Identity, bias=0.0, scale=1.0),
                     [g1, pp.get("mm")])
            g3 = C.V(lambda e: e.tensor_tensor(G_r[:], G_r[:], R.mask01[:], ALU.mult), [g2])
            b1 = C.T(lambda e, csl=csl: e.transpose(R.ptr[0:P, 0:128], xbcs[16][:, csl], R.idb[:]),
                     [g1, pp.get("dz")])
            b2 = C.A(lambda e: e.activation(Btm_r[:], R.ptr[0:P, 0:128], AF.Identity,
                                            bias=0.0, scale=1.0), [b1])
            xst = []
            first_e = [f3, b2, pp.get("din"), pp.get("dz")]
            first_o = [b2, g2, pp.get("din")]
            prev_e = None
            prev_o = None
            xt0 = None
            for t16 in range(16):
                if t16 % 2 == 0:
                    xt = C.T(lambda e, t16=t16, csl=csl:
                             e.transpose(R.ptr[0:P, 128:256], xbcs[t16][:, csl], R.idb[:]),
                             ([prev_e] if prev_e else first_e))
                    xv = C.A(lambda e, t16=t16: e.activation(
                        xstm_r[:, t16 * 128:(t16 + 1) * 128], R.ptr[0:P, 128:256],
                        AF.Identity, bias=0.0, scale=1.0), [xt])
                    prev_e = xv
                else:
                    xt = C.T(lambda e, t16=t16, csl=csl:
                             e.matmul(R.pss[:, 0:128], xbcs[t16][:, csl], R.idb[:],
                                      start=True, stop=True),
                             ([prev_o] if prev_o else first_o))
                    xv = C.V(lambda e, t16=t16: e.tensor_copy(
                        xstm_r[:, t16 * 128:(t16 + 1) * 128], R.pss[:, 0:128]), [xt])
                    prev_o = xv
                if t16 == 0:
                    xt0 = xt
                xst.append(xv)
            uU = C.V(lambda e: e.tensor_tensor(
                U_r.rearrange("p (h x) -> p h x", h=NH),
                xstm_r.rearrange("p (h x) -> p h x", h=NH),
                dtw_tm[:, :, None].broadcast_to([P, NH, 64]), ALU.mult),
                xst + [t2v, pp.get("dz")])
            e1 = C.T(lambda e: e.matmul(R.pm[2][:, 0:512], R.selp[:], exps_pair2[:, 0:512],
                                        start=True, stop=True), [f4, g1, pp.get("hu2A")])
            e2 = C.T(lambda e: e.matmul(R.pm[3][:, 0:512], R.selp[:], exps_pair2[:, 512:1024],
                                        start=True, stop=True), [f4, pp.get("hu2B")])
            e1v = C.A(lambda e: e.activation(EXPSP[:, 0:512], R.pm[2][:, 0:512], AF.Identity,
                                             bias=0.0, scale=1.0), [e1, pp.get("o1")])
            e2v = C.A(lambda e: e.activation(EXPSP[:, 512:1024], R.pm[3][:, 0:512], AF.Identity,
                                             bias=0.0, scale=1.0), [e2])
            e3 = C.T(lambda e: e.matmul(R.pm[2][:, 0:512], R.selp[:], exps_pair2[:, 1024:1536],
                                        start=True, stop=True), [e1v])
            e4 = C.T(lambda e: e.matmul(R.pm[3][:, 0:512], R.selp[:], exps_pair2[:, 1536:2048],
                                        start=True, stop=True), [e2v])
            e3v = C.A(lambda e: e.activation(EXPSP[:, 1024:1536], R.pm[2][:, 0:512], AF.Identity,
                                             bias=0.0, scale=1.0), [e3])
            e4v = C.A(lambda e: e.activation(EXPSP[:, 1536:2048], R.pm[3][:, 0:512], AF.Identity,
                                             bias=0.0, scale=1.0), [e4])
            cbar = [e4v, e3v, uU, g3]
            PIPE_NEW = {"m2": m2, "m4": m4, "m5": m5, "m6": m6, "m7": m7,
                        "t1": t1, "t1v": t1v, "t2": t2, "f1": f1, "f2": f2,
                        "f3": f3, "f4": f4, "q0": q0, "uU": uU, "e4": e4,
                        "q2": q2, "xt0": xt0, "xvo": prev_o}
            if KCH < 4:
                PIPE = PIPE_NEW
                continue

            prevhup = {0: None, 1: None}
            prevsr = {0: None, 1: None}
            for g in range(4):
                st_ = g % 2
                BS, BI, BN = R.pm[0 + st_], R.pm[2 + st_], R.pm[4 + st_]
                ebase = raw_r[:, 0:1024] if st_ == 0 else R.big[:, 0:1024]
                eb = ebase[:, 0:512]
                eb2 = ebase[:, 512:1024]
                lpb = Lp_r if st_ == 0 else u_r[:, 3072:4096]
                m_t = M_r if st_ == 0 else Mb_r
                ftmp = R.big[:, 1024:1536] if st_ == 0 else R.big[:, 1536:2048]
                scrA = R.ab if st_ == 0 else R.bb
                gd = [prevhup[st_]] if prevhup[st_] else (cbar if g < 2 else [])
                sr1 = C.T(lambda e, g=g, BS=BS: e.matmul(BS[:, 0:512], R.onerf[0:1, :],
                                                         s_flat[0:1, g * 1024:g * 1024 + 512],
                                                         start=True, stop=True), gd + cbar)
                ee1 = C.V(lambda e, g=g, BS=BS, eb=eb: e.tensor_tensor(
                    eb.rearrange("p (h t) -> p h t", h=4),
                    BS[:, 0:512].rearrange("p (h t) -> p h t", h=4),
                    nslndt_tm[:, 8 * g:8 * g + 4, None].broadcast_to([P, 4, P]), ALU.add),
                    [sr1, t1v])
                sr2 = C.T(lambda e, g=g, BS=BS: e.matmul(BS[:, 0:512], R.onerf[0:1, :],
                                                         s_flat[0:1, g * 1024 + 512:(g + 1) * 1024],
                                                         start=True, stop=True), [ee1])
                ee2 = C.V(lambda e, g=g, BS=BS, eb2=eb2: e.tensor_tensor(
                    eb2.rearrange("p (h t) -> p h t", h=4),
                    BS[:, 0:512].rearrange("p (h t) -> p h t", h=4),
                    nslndt_tm[:, 8 * g + 4:8 * g + 8, None].broadcast_to([P, 4, P]), ALU.add),
                    [sr2])
                prevsr[st_] = sr2
                em = C.V(lambda e, ebase=ebase: e.tensor_single_scalar(
                    ebase[:, 0:1024], ebase[:, 0:1024], EMIN_CLAMP, ALU.min), [ee1, ee2])
                lp = C.A(lambda e, ebase=ebase, lpb=lpb: e.activation(
                    lpb[:, 0:1024], ebase[:, 0:1024], AF.Exp, bias=0.0, scale=1.0), [em])
                mm_ = C.G(lambda e, m_t=m_t, lpb=lpb: e.tensor_tensor(
                    m_t.rearrange("p (h t) -> p h t", h=8),
                    lpb.rearrange("p (h t) -> p h t", h=8),
                    G_r[:, None, :].broadcast_to([P, 8, P]), ALU.mult), [lp, g3])

                def mkintra(g=g, BI=BI, m_t=m_t):
                    def body(e):
                        last = None
                        for j in range(8):
                            h = 8 * g + j
                            last = e.matmul(BI[(h % 2) * 64:(h % 2) * 64 + 64,
                                               (j // 2) * 128:(j // 2) * 128 + 128],
                                            xstm_r[:, h * 64:(h + 1) * 64],
                                            m_t[:, j * 128:(j + 1) * 128],
                                            start=True, stop=True)
                        return last
                    return body
                din = C.T(mkintra(), [mm_, uU] + ([prevhup[st_]] if prevhup[st_] else []))

                def mkinter(g=g, BN=BN, csl=csl):
                    def body(e):
                        last = None
                        for jj in range(4):
                            pair = 4 * g + jj
                            last = e.matmul(BN[:, jj * 128:(jj + 1) * 128],
                                            hstb_r[:, pair * 128:(pair + 1) * 128],
                                            xbcs[17][:, csl], start=True, stop=True)
                        return last
                    return body
                dit = C.T(mkinter(), [din, pp.get("hsc")])

                xs3 = xbcs_r.rearrange("p (ch q) -> p ch q", ch=18)
                sz3 = sz_r.rearrange("p (ch q) -> p ch q", ch=16)
                y23 = Y2_r.rearrange("p (ch q) -> p ch q", ch=16)
                o1 = C.V(lambda e, g=g, BN=BN, scrA=scrA: e.tensor_tensor(
                    scrA[:], BN[:], EXPSP[:, g * 512:(g + 1) * 512], ALU.mult),
                    [dit, e4v])
                o2 = C.V(lambda e, BI=BI, scrA=scrA: e.tensor_tensor(scrA[:], scrA[:], BI[:], ALU.add),
                         [o1])
                o3 = C.G(lambda e, g=g, csl=csl, ftmp=ftmp: e.tensor_tensor(
                    ftmp.rearrange("p (ch q) -> p ch q", ch=4),
                    xs3[:, 4 * g:4 * g + 4, csl],
                    dcol[:, 4 * g:4 * g + 4, None].broadcast_to([P, 4, P]), ALU.mult),
                    [o2])
                o4 = C.V(lambda e, scrA=scrA, ftmp=ftmp: e.tensor_tensor(scrA[:], scrA[:],
                                                                         ftmp, ALU.add),
                         [o3])
                o5 = C.G(lambda e, g=g, c=c, csl=csl, scrA=scrA: e.tensor_tensor(
                    y23[:, 4 * g:4 * g + 4, c * P:(c + 1) * P],
                    scrA[:].rearrange("p (ch q) -> p ch q", ch=4),
                    sz3[:, 4 * g:4 * g + 4, csl], ALU.mult), [o4])

                def mkz(g=g, BI=BI):
                    def body(e):
                        last = None
                        for j in range(8):
                            h = 8 * g + j
                            last = e.matmul(BI[:, j * 64:(j + 1) * 64], Btm_r[:],
                                            U_r[:, h * 64:(h + 1) * 64], start=True, stop=True)
                        return last
                    return body
                dz_ = C.T(mkz(), [o2, o4, b2])
                hu1 = C.G(lambda e, g=g, ftmp=ftmp: e.tensor_tensor(
                    ftmp.rearrange("p (h x) -> p h x", h=8),
                    hstate[:, g * 512:(g + 1) * 512].rearrange("p (h x) -> p h x", h=8),
                    expq[:, 8 * g:8 * g + 8, None].broadcast_to([P, 8, 64]), ALU.mult),
                    [o5, o4, q2])
                hu2 = C.V(lambda e, g=g, BI=BI, ftmp=ftmp: e.tensor_tensor(
                    hstate[:, g * 512:(g + 1) * 512], ftmp, BI[:], ALU.add),
                    [hu1, dz_])
                prevhup[st_] = hu2
                PIPE_NEW["sr"] = sr2
                PIPE_NEW["ee"] = ee2
                PIPE_NEW["mm"] = mm_
                PIPE_NEW["din"] = din
                PIPE_NEW["dz"] = dz_
                PIPE_NEW["o1"] = o1
                PIPE_NEW["hu1"] = hu1
                PIPE_NEW["hu2A" if st_ == 0 else "hu2B"] = hu2
            if KCH >= 6:
                PIPE_NEW["hsc"] = C.V(lambda e: e.tensor_copy(hstb_r[:], hstate[:]),
                                      [prevhup[0], prevhup[1]])
            PIPE = PIPE_NEW

        if PH < 4:
            pbar = C.all_done()
            u2_deps.append(C.dma("sync", dr["l0a_u2"][0:P, sl], tmpb[0][:, :], pbar))
            continue
        gbar = C.all_done()
        gslots = [u_r[:, i * 512:(i + 1) * 512] for i in range(8)]
        dqs2 = []
        for ccc in range(8):
            dqs2.append(C.A(lambda e, ccc=ccc: e.activation(gslots[ccc][:], Y2[ccc][:],
                                                            AF.Square, bias=0.0, scale=1.0),
                            gbar if ccc == 0 else [dqs2[-1]]))

        def mkg2a():
            def body(e):
                last = None
                for j in range(8):
                    last = e.matmul(R.pss[0:1, 0:512], R.m11b[:, 0:1], gslots[j][:],
                                    start=(j == 0), stop=False)
                return last
            return body
        gmm1 = C.T(mkg2a(), [dqs2[-1]] + gbar)
        dqs3 = []
        for ccc in range(8, 16):
            dqs3.append(C.A(lambda e, ccc=ccc: e.activation(gslots[ccc - 8][:], Y2[ccc][:],
                                                            AF.Square, bias=0.0, scale=1.0),
                            [gmm1] if ccc == 8 else [dqs3[-1]]))

        def mkg2b():
            def body(e):
                last = None
                for j in range(8):
                    last = e.matmul(R.pss[0:1, 0:512], R.m11b[:, 0:1], gslots[j][:],
                                    start=False, stop=(j == 7))
                return last
            return body
        gmm2 = C.T(mkg2b(), [dqs3[-1]])
        gs3 = C.A(lambda e: e.activation(R.r0[:, 0:512], R.pss[0:1, 0:512], AF.Sqrt,
                                         bias=R.eps56[0:1, 1:2], scale=1.0), [gmm2])
        gs4 = C.V(lambda e: e.reciprocal(R.r0[:, 0:512], R.r0[:, 0:512]), [gs3])
        gb = C.T(lambda e: e.matmul(R.pm[4][:, 0:512], R.onerf[0:1, :], R.r0[0:1, 0:512],
                                    start=True, stop=True), [gs4] + gbar)
        y3d = []
        for ccc in range(16):
            y3d.append(C.V(lambda e, ccc=ccc: e.tensor_tensor(Y2[ccc][:], Y2[ccc][:],
                                                              R.pm[4][:, 0:512], ALU.mult),
                           [gb]))

        if PH < 5:
            pbar = C.all_done()
            continue
        prev_ev_o = None
        wouser = {}
        for mcg in range(2):
            for kc in range(16):
                wo = wst[kc % 2][:, 0:512]
                dwo = C.dma("gpsimd", wo, dr["l0a_outproj_t"][kc * P:(kc + 1) * P,
                                                             mcg * 512:(mcg + 1) * 512],
                            ([wouser.get(kc % 2)] if wouser.get(kc % 2) else gbar))
                dsc_ = C.V(lambda e, wo=wo, kc=kc: e.tensor_scalar(wo, wo, wsmc[:, kc:kc + 1], None,
                                                                   ALU.mult), [dwo])
                def mko(wo=wo, kc=kc):
                    def body(e):
                        last = None
                        for sub in range(4):
                            last = e.matmul(R.pm[sub][:, 0:512], wo[:, sub * 128:(sub + 1) * 128],
                                            Y2[kc][:], start=(kc == 0), stop=(kc == 15))
                        return last
                    return body
                dmo = C.T(mko(), [dsc_] + y3d + ([prev_ev_o] if kc == 0 and prev_ev_o else []))
                wouser[kc % 2] = dmo
            for sub in range(4):
                mc = mcg * 4 + sub
                ev1 = C.V(lambda e, sub=sub, mc=mc, sl=sl: e.tensor_tensor(
                    tmpb[mc][:, :], R.pm[sub][:, 0:512], X[mc][:, sl], ALU.add), [dmo])
                ev2 = C.V(lambda e, sub=sub, mc=mc, sl=sl: e.scalar_tensor_tensor(
                    X[mc][:, sl], X[mc][:, sl], 2.0, R.pm[sub][:, 0:512], ALU.mult, ALU.add), [ev1])
                prev_ev_o = ev2
        obar = C.all_done()
        usl = [Y2_r[:, i * 512:(i + 1) * 512] for i in range(8)]
        dqs4 = []
        for cc in range(8):
            dqs4.append(C.A(lambda e, cc=cc: e.activation(usl[cc][:], tmpb[cc][:, :],
                                                          AF.Square, bias=0.0, scale=1.0),
                            obar if cc == 0 else [dqs4[-1]]))

        def mkum():
            def body(e):
                last = None
                for cc in range(8):
                    last = e.matmul(R.pss[0:1, 0:512], R.m10b[:, 0:1], usl[cc][:],
                                    start=(cc == 0), stop=(cc == 7))
                return last
            return body
        usmm = C.T(mkum(), [dqs4[-1]] + obar)
        us3 = C.A(lambda e: e.activation(R.r0[:, 0:512], R.pss[0:1, 0:512], AF.Sqrt,
                                         bias=R.eps56[0:1, 1:2], scale=1.0), [usmm])
        us4 = C.V(lambda e: e.reciprocal(R.r0[:, 0:512], R.r0[:, 0:512]), [us3])
        ub = C.T(lambda e: e.matmul(R.pm[4][:, 0:512], R.onerf[0:1, :], R.r0[0:1, 0:512],
                                    start=True, stop=True), [us4] + obar)
        ucp = C.V(lambda e: e.tensor_copy(R.ab[:], R.pm[4][:, 0:512]), [ub])
        tmp3 = u_r.rearrange("p (cc t) -> p cc t", cc=8)
        dw1 = C.G(lambda e: e.tensor_tensor(
            Y2_r[:, 4096:8192].rearrange("p (cc t) -> p cc t", cc=8),
            tmp3, R.ab[:, None, 0:512].broadcast_to([P, 8, 512]), ALU.mult), [ucp])
        dw2 = C.V(lambda e: e.tensor_tensor(
            tmp3, Y2_r[:, 4096:8192].rearrange("p (cc t) -> p cc t", cc=8),
            pwc[:, :, None].broadcast_to([P, 8, 512]), ALU.mult), [dw1])
        u2_deps.append(C.dma("sync",
                             dr["l0a_u2"].rearrange("(cc p) t -> p cc t", p=P)[:, :, sl],
                             tmp3, [dw2]))
        pbar = C.all_done()

    if PH < 7:
        return ln_fm(C, R, X, X, 1e-5, C.all_done())
    mbar = C.all_done()
    R.pbf.reset()
    u2h_r = R.pbf.take(8 * 1024)
    u2h = [u2h_r[:, c * 1024:(c + 1) * 1024] for c in range(8)]
    wmt_r = R.pbf.take(2 * 4096)
    wmt = [wmt_r[:, i * 4096:(i + 1) * 4096] for i in range(2)]
    hm_r = R.pbf.take(22 * 1024)
    hm = [hm_r[:, c * 1024:(c + 1) * 1024] for c in range(22)]
    GATE_GROUPS = [(0, 512), (512, 512), (1024, 512), (1536, 512), (2048, 512), (2560, 192)]
    mlp_ev = []
    prev_wd2 = None
    muser = {}
    for hh in range(2):
        hsl = slice(hh * 1024, (hh + 1) * 1024)
        du2l = [C.dma("sync", u2h[cc], dr["l0a_u2"][cc * P:(cc + 1) * P, hsl],
                      mbar if hh == 0 else [mlp_ev[-1]]) for cc in range(8)]
        prev_ev_g = {}
        for (g0, gcols) in GATE_GROUPS:
            wdg = [C.dma("gpsimd", wmt[0].rearrange("p (kc c) -> p kc c", kc=8)[:, :, 0:gcols],
                         dr["l0a_gate_t"].rearrange("(kc p) m -> p kc m", p=P)[:, :, g0:g0 + gcols],
                         [muser.get(0)] if muser.get(0) else mbar)]
            wdu = [C.dma("gpsimd", wmt[1].rearrange("p (kc c) -> p kc c", kc=8)[:, :, 0:gcols],
                         dr["l0a_up_t"].rearrange("(kc p) m -> p kc m", p=P)[:, :, g0:g0 + gcols],
                         [muser.get(1)] if muser.get(1) else mbar)]
            nsub = (gcols + 127) // 128
            for sub in range(nsub):
                mw = min(128, gcols - sub * 128)
                mchunk = (g0 + sub * 128) // 128
                for ng in range(2):
                    def mkg(sub=sub, mw=mw, ng=ng, w=0):
                        def body(e):
                            last = None
                            for kc in range(8):
                                last = e.matmul(R.pm[0 + ng][0:mw, 0:512],
                                                wmt[0][:, kc * 512 + sub * 128: kc * 512 + sub * 128 + mw],
                                                u2h[kc][:, ng * 512:(ng + 1) * 512],
                                                start=(kc == 0), stop=(kc == 7))
                            return last
                        return body
                    dg = C.T(mkg(), wdg + du2l + ([prev_ev_g.get(ng)] if prev_ev_g.get(ng) else []))
                    def mkup(sub=sub, mw=mw, ng=ng):
                        def body(e):
                            last = None
                            for kc in range(8):
                                last = e.matmul(R.pm[2 + ng][0:mw, 0:512],
                                                wmt[1][:, kc * 512 + sub * 128: kc * 512 + sub * 128 + mw],
                                                u2h[kc][:, ng * 512:(ng + 1) * 512],
                                                start=(kc == 0), stop=(kc == 7))
                            return last
                        return body
                    du_ = C.T(mkup(), wdu + ([prev_ev_g.get(2 + ng)] if prev_ev_g.get(2 + ng) else []))
                    sgl = C.A(lambda e, mw=mw, ng=ng, b=ng: e.activation(
                        R.sq[b][0:mw, :], R.pm[0 + ng][0:mw, 0:512], AF.Silu, bias=0.0, scale=1.0), [dg])
                    hev = C.V(lambda e, mchunk=mchunk, mw=mw, ng=ng, b=ng: e.tensor_tensor(
                        hm[mchunk][0:mw, ng * 512:(ng + 1) * 512], R.sq[b][0:mw, :],
                        R.pm[2 + ng][0:mw, 0:512], ALU.mult), [sgl, du_])
                    prev_ev_g[ng] = hev
                    prev_ev_g[2 + ng] = hev
                    prev_wd2 = du_
                    muser[0] = dg
                    muser[1] = du_
        hbar = C.all_done()
        KCH = [(k * 128, 128) for k in range(21)] + [(2688, 64)]
        for mcg in range(2):
            for ng in range(2):
                prev_devd = None
                for kb in range(3):
                    kcs = KCH[kb * 8:(kb + 1) * 8]
                    wdd = [C.dma("gpsimd", wmt[kb % 2][:len_, j * 512:(j + 1) * 512],
                                 dr["l0a_down_t"][k0:k0 + len_, mcg * 512:(mcg + 1) * 512],
                                 [muser.get(kb % 2)] if muser.get(kb % 2) else hbar)
                           for j, (k0, len_) in enumerate(kcs)]
                    def mkd(kb=kb, kcs=kcs, mcg=mcg, ng=ng):
                        def body(e):
                            last = None
                            for j, (k0, len_) in enumerate(kcs):
                                kc = kb * 8 + j
                                for sub in range(4):
                                    last = e.matmul(
                                        R.pm[sub][:, 0:512],
                                        wmt[kb % 2][:len_, j * 512 + sub * 128: j * 512 + (sub + 1) * 128],
                                        hm[kc][0:len_, ng * 512:(ng + 1) * 512],
                                        start=(kb == 0 and j == 0), stop=(kb == 2 and j == len(kcs) - 1))
                            return last
                        return body
                    ddeps = wdd + ([prev_devd] if prev_devd else hbar)
                    if kb == 0:
                        ddeps = ddeps + (mlp_ev[-4:] if mlp_ev else [])
                    dmd = C.T(mkd(), ddeps)
                    prev_devd = dmd
                    prev_wd2 = dmd
                    muser[kb % 2] = dmd
                for sub in range(4):
                    mc = mcg * 4 + sub
                    tsl = slice(hh * 1024 + ng * 512, hh * 1024 + (ng + 1) * 512)
                    mev = C.V(lambda e, sub=sub, mc=mc, tsl=tsl: e.tensor_tensor(
                        X[mc][:, tsl], X[mc][:, tsl], R.pm[sub][:, 0:512], ALU.add), [dmd])
                    mlp_ev.append(mev)

    return ln_fm(C, R, X, X, 1e-5, mlp_ev[-4:], sqw=u2h_r[:, 0:4096])


def l0attn_host(inp):
    cwv = np.asarray(inp["L0_attn_conv_w"], np.float32)
    cbv = np.asarray(inp["L0_attn_conv_b"], np.float32)
    cw = np.zeros((P, 72), np.float32)
    for k in range(4):
        cw[:, k * 18:(k + 1) * 18] = cwv[:, k].reshape(18, 128).T
    Dv = np.asarray(inp["L0_attn_D"], np.float32)
    dcol = np.repeat(Dv, 64).reshape(16, 128).T
    adt = np.zeros((NH, 2), np.float32)
    adt[:, 0] = np.asarray(inp["L0_attn_A_log"], np.float32)
    adt[:, 1] = np.asarray(inp["L0_attn_dt_bias"], np.float32)
    selp = np.zeros((2, P), BF)
    selp[0, 0:64] = 1
    selp[1, 64:128] = 1
    mask01 = np.tril(np.ones((P, P), np.float32)).astype(BF)
    mask01 = np.triu(np.ones((P, P), np.float32)).astype(BF)
    return {
        "l0a_inproj_t": np.ascontiguousarray(np.asarray(inp["L0_attn_in_proj_w"], np.float32).T),
        "l0a_outproj_t": np.ascontiguousarray(np.asarray(inp["L0_attn_out_proj_w"], np.float32).T),
        "l0a_gate_t": np.ascontiguousarray(np.asarray(inp["L0_attn_gate_w"], np.float32).T),
        "l0a_up_t": np.ascontiguousarray(np.asarray(inp["L0_attn_up_w"], np.float32).T),
        "l0a_down_t": np.ascontiguousarray(np.asarray(inp["L0_attn_down_w"], np.float32).T),
        "l0a_convw": cw,
        "l0a_convb": np.ascontiguousarray(cbv.reshape(18, 128).T),
        "l0a_normw": np.ascontiguousarray(np.asarray(inp["L0_attn_norm_w"], np.float32).reshape(8, 128).T),
        "l0a_postw": np.ascontiguousarray(np.asarray(inp["L0_attn_post_norm_w"], np.float32).reshape(8, 128).T),
        "l0a_wssm": np.ascontiguousarray(np.asarray(inp["L0_attn_ssm_norm_w"], np.float32).reshape(16, 128).T),
        "l0a_dcol": dcol.astype(np.float32),
        "l0a_adt": adt,
        "c_selp": selp,
        "c_mask01": mask01,
    }


def l0attn_decls():
    return [("l0a_inproj_t", [D, 4384], F32), ("l0a_outproj_t", [2048, D], F32),
            ("l0a_gate_t", [D, 2752], F32), ("l0a_up_t", [D, 2752], F32),
            ("l0a_down_t", [2752, D], F32),
            ("l0a_convw", [P, 72], F32), ("l0a_convb", [P, 18], F32),
            ("l0a_normw", [P, 8], F32), ("l0a_postw", [P, 8], F32),
            ("l0a_wssm", [P, 16], F32), ("l0a_dcol", [P, 16], F32),
            ("l0a_adt", [NH, 2], F32), ("c_selp", [2, P], BF16),
            ("c_mask01", [P, P], BF16)]


# version 19
# speedup vs baseline: 1.0660x; 1.0660x over previous
from contextlib import ExitStack

import numpy as np
import ml_dtypes
import concourse.bass as bass
import concourse.mybir as mybir
from concourse.bass_utils import run_bass_kernel_spmd

F32 = mybir.dt.float32
BF16 = mybir.dt.bfloat16
I8 = mybir.dt.int8
AF = mybir.ActivationFunctionType
ALU = mybir.AluOpType
AX = mybir.AxisListType

P = 128
S = 2048
D = 1024
NCORES = 8
MAGIC = 12582912.0
ENGINES = ("tensor", "vector", "scalar", "gpsimd", "sync")
REPLICA_GROUPS = [list(range(8))]
BF = ml_dtypes.bfloat16

BFPOOL_COLS = 42240
F32POOL_COLS = 4928


class Sem:
    __slots__ = ("h", "n", "name")

    def __init__(self, h, name):
        self.h, self.n, self.name = h, 0, name


class Ctx:
    def __init__(self, nc, st):
        self.nc = nc
        self.st = st
        self.ops = {e: [] for e in ENGINES}
        self.es = {e: self.sem("es_" + e) for e in ENGINES}
        self.dW = self.sem("dW")
        self.dX = self.sem("dX")

    def sem(self, name):
        return Sem(self.st.enter_context(self.nc.semaphore(name)), name)

    def sb(self, name, shape, dt):
        return self.st.enter_context(self.nc.sbuf_tensor(name, shape, dt))

    def ps(self, name, shape, dt=F32):
        return self.st.enter_context(self.nc.psum_tensor(name, shape, dt))

    def step(self, eng, body, deps=(), sem=None, inc=1):
        if sem is None:
            sem = self.es[eng]
        best = {}
        for d in deps:
            if d is None:
                continue
            s_, v_ = d
            if best.get(id(s_), (None, -1))[1] < v_:
                best[id(s_)] = (s_, v_)
        deps = list(best.values())

        def fn(e, deps=tuple(deps), body=body, sem=sem, inc=inc):
            for s_, v_ in deps:
                e.wait_ge(s_.h, v_)
            last = body(e)
            if last is not None:
                last.then_inc(sem.h, inc)
            else:
                e.sem_inc(sem.h, inc)

        self.ops[eng].append(fn)
        sem.n += inc
        return (sem, sem.n)

    def T(self, body, deps=()):
        return self.step("tensor", body, deps)

    def V(self, body, deps=()):
        return self.step("vector", body, deps)

    def A(self, body, deps=()):
        return self.step("scalar", body, deps)

    def G(self, body, deps=()):
        return self.step("gpsimd", body, deps)

    def dma(self, qeng, out, in_, deps=()):
        sem = self.dW if qeng == "gpsimd" else self.dX
        return self.step(qeng, lambda e: e.dma_start(out=out, in_=in_), deps, sem, 16)

    def all_done(self):
        return ([(s, s.n) for s in self.es.values()]
                + [(self.dW, self.dW.n), (self.dX, self.dX.n)])


def build_and_finalize(nc, C):
    with nc.Block() as block:
        @block.tensor
        def _(e):
            for fn in C.ops["tensor"]:
                fn(e)

        @block.vector
        def _(e):
            for fn in C.ops["vector"]:
                fn(e)

        @block.scalar
        def _(e):
            for fn in C.ops["scalar"]:
                fn(e)

        @block.gpsimd
        def _(e):
            for fn in C.ops["gpsimd"]:
                fn(e)

        @block.sync
        def _(e):
            for fn in C.ops["sync"]:
                fn(e)


class Pool:

    def __init__(self, C, name, dt, cols):
        self.t = C.sb(name, [P, cols], dt)
        self.cols = cols
        self.off = 0

    def take(self, cols):
        assert self.off + cols <= self.cols, f"pool overflow: {self.off}+{cols}>{self.cols}"
        ap = self.t[:, self.off:self.off + cols]
        self.off += cols
        return ap

    def reset(self, keep=0):
        self.off = keep


class Res:

    def __init__(self, C):
        self.idf = C.sb("s_idf", [P, P], F32)
        self.idb = C.sb("s_idb", [P, P], BF16)
        self.onecb = C.sb("s_onecb", [P, 1], BF16)
        self.onecf = C.sb("s_onecf", [P, 1], F32)
        self.m10b = C.sb("s_m10b", [P, 1], BF16)
        self.m11b = C.sb("s_m11b", [P, 1], BF16)
        self.m10f = C.sb("s_m10f", [P, 1], F32)
        self.eps56 = C.sb("s_eps56", [P, 2], F32)
        self.onerf = C.sb("s_onerf", [65, P], F32)
        self.pm = [C.ps(f"pm{i}", [P, 512], F32) for i in range(6)]
        self.ptr = C.ps("ptr", [P, 1024], BF16)
        self.pss = C.ps("pss", [P, 512], F32)
        self.sq = [C.sb(f"sc_sq{i}", [P, 512], BF16) for i in range(2)]
        self.ab = C.sb("sc_ab", [P, 512], F32)
        self.bb = C.sb("sc_bb", [P, 512], F32)
        self.big = C.sb("sc_big", [P, S], F32)
        self.selp = C.sb("s_selp", [2, P], BF16)
        self.mask01 = C.sb("s_mask01", [P, P], BF16)
        self.qsl = [C.sb(f"sc_qsl{i}", [P, 1024], I8) for i in range(2)]
        self.rows = C.sb("sc_rows", [1, 3 * S], F32)
        self.r0 = self.rows[0:1, 0:S]
        self.r1 = self.rows[0:1, S:2 * S]
        self.r2 = self.rows[0:1, 2 * S:3 * S]
        self.pbf = Pool(C, "pool_bf", BF16, BFPOOL_COLS)
        self.pf = Pool(C, "pool_f32", F32, F32POOL_COLS)

    def load(self, C, dr):
        d = [C.dma("sync", self.idf[:], dr["c_idf"]),
             C.dma("sync", self.idb[:], dr["c_idb"]),
             C.dma("sync", self.onecb[:], dr["c_onecb"]),
             C.dma("sync", self.onecf[:], dr["c_onecf"]),
             C.dma("sync", self.onerf[:], dr["c_onerf"]),
             C.dma("sync", self.m10b[:], dr["c_m10b"]),
             C.dma("sync", self.m11b[:], dr["c_m11b"]),
             C.dma("sync", self.m10f[:], dr["c_m10f"]),
             C.dma("sync", self.eps56[:], dr["c_eps56"])]
        self.deps = d
        return d

    @staticmethod
    def host_arrays():
        return {
            "c_idf": np.eye(P, dtype=np.float32),
            "c_idb": np.eye(P, dtype=BF),
            "c_onecb": np.ones((P, 1), dtype=BF),
            "c_onecf": np.ones((P, 1), dtype=np.float32),
            "c_onerf": np.ones((65, P), dtype=np.float32),
            "c_m10b": np.full((P, 1), 2.0 ** -10, dtype=BF),
            "c_m11b": np.full((P, 1), 2.0 ** -11, dtype=BF),
            "c_m10f": np.full((P, 1), 2.0 ** -10, dtype=np.float32),
            "c_eps56": np.tile(np.array([[1e-5, 1e-6]], np.float32), (P, 1)),
        }

    @staticmethod
    def decls():
        return [("c_idf", [P, P], F32), ("c_idb", [P, P], BF16),
                ("c_onecb", [P, 1], BF16), ("c_onecf", [P, 1], F32),
                ("c_onerf", [65, P], F32), ("c_m10b", [P, 1], BF16),
                ("c_m11b", [P, 1], BF16), ("c_m10f", [P, 1], F32),
                ("c_eps56", [P, 2], F32)]



def bcast_row(C, R, dst, row_ap, n, deps, pbank, base=0):
    d1 = C.T(lambda e: e.matmul(pbank[:, 0:n], R.onerf[base:base + 1, :], row_ap,
                                start=True, stop=True), deps)
    return C.V(lambda e: e.tensor_copy(dst[:, 0:n], pbank[:, 0:n]), [d1])


def ln_fm(C, R, SRC, DST, eps, deps, nf=None, sqw=None):
    nch = len(SRC)
    assert (nf or nch * P) == 1024
    NG = S // 512
    sum_deps = []
    prev_dsv = None
    prev_dqv = None
    sq_slots = None
    if sqw is not None:
        sq_slots = [sqw[:, i * 512:(i + 1) * 512] for i in range(8)]
    for ng in range(NG):
        sl = slice(ng * 512, (ng + 1) * 512)

        def mksum(sl=sl):
            def body(e):
                last = None
                for cc in range(nch):
                    last = e.matmul(R.pm[4][0:1, 0:512], R.m10f[:, 0:1], SRC[cc][:, sl],
                                    start=(cc == 0), stop=(cc == nch - 1))
                return last
            return body
        dsu = C.T(mksum(), list(deps) + ([prev_dsv] if prev_dsv else []))
        dsv = C.V(lambda e, sl=sl: e.tensor_copy(R.r0[:, sl], R.pm[4][0:1, 0:512]), [dsu])
        prev_dsv = dsv

        if sq_slots is not None:
            dqs = []
            for cc in range(nch):
                dq = C.A(lambda e, cc=cc, sl=sl: e.activation(sq_slots[cc][:], SRC[cc][:, sl],
                                                              AF.Square, bias=0.0, scale=1.0),
                         (list(deps) + ([prev_dqv] if prev_dqv else [])) if cc == 0 else [dqs[-1]])
                dqs.append(dq)

            def mkq2():
                def body(e):
                    last = None
                    for cc in range(nch):
                        last = e.matmul(R.pm[5][0:1, 0:512], R.m10b[:, 0:1], sq_slots[cc][:],
                                        start=(cc == 0), stop=(cc == nch - 1))
                    return last
                return body
            prev = C.T(mkq2(), [dqs[-1], dsu])
        else:
            prev = dsv
            prev_mm2 = None
            for cc in range(nch):
                sqdeps = [prev] if cc == 0 else ([prev_mm2] if prev_mm2 else [])
                if cc == 0:
                    sqdeps += list(deps) + ([prev_dqv] if prev_dqv else [])
                dq = C.A(lambda e, cc=cc, sl=sl, b=cc % 2:
                         e.activation(R.sq[b][:], SRC[cc][:, sl], AF.Square, bias=0.0, scale=1.0),
                         sqdeps)
                dmm = C.T(lambda e, cc=cc, b=cc % 2:
                          e.matmul(R.pm[5][0:1, 0:512], R.m10b[:, 0:1], R.sq[b][:],
                                   start=(cc == 0), stop=(cc == nch - 1)), [dq])
                prev_mm2 = prev
                prev = dmm
        dqv = C.V(lambda e, sl=sl: e.tensor_copy(R.r1[:, sl], R.pm[5][0:1, 0:512]), [prev])
        prev_dqv = dqv
        sum_deps += [dsv, dqv]

    def stats(e):
        e.tensor_tensor(R.r2[:, :], R.r0[:, :], R.r0[:, :], ALU.mult)
        return e.tensor_tensor(R.r1[:, :], R.r1[:, :], R.r2[:, :], ALU.subtract)
    dst_ = C.V(stats, sum_deps)
    dsq = C.A(lambda e: e.activation(R.r1[:, :], R.r1[:, :], AF.Sqrt, bias=R.eps56[0:1, 0:1], scale=1.0), [dst_])
    drc = C.V(lambda e: e.reciprocal(R.r1[:, :], R.r1[:, :]), [dsq])

    def mkb(e):
        e.tensor_tensor(R.r2[:, :], R.r0[:, :], R.r1[:, :], ALU.mult)
        return e.tensor_scalar(R.r2[:, :], R.r2[:, :], -1.0, None, ALU.mult)
    dbr = C.V(mkb, [drc])

    out_deps = []
    prev_ap = None
    X3s = R.X_r.rearrange("p (cc t) -> p cc t", cc=8)
    for ng in range(NG):
        sl = slice(ng * 512, (ng + 1) * 512)
        da = C.T(lambda e, sl=sl: e.matmul(R.pm[4][:, 0:512], R.onerf[0:1, :], R.r1[:, sl],
                                           start=True, stop=True), [dbr, prev_ap])
        db = C.T(lambda e, sl=sl: e.matmul(R.pm[5][:, 0:512], R.onerf[0:1, :], R.r2[:, sl],
                                           start=True, stop=True), [dbr, prev_ap])
        dc = C.V(lambda e: e.tensor_copy(R.ab[:], R.pm[4][:, 0:512]), [da])
        dd = C.A(lambda e: e.activation(R.bb[:], R.pm[5][:, 0:512], AF.Identity,
                                        bias=0.0, scale=1.0), [db])

        def mkap(sl=sl):
            def body(e):
                e.tensor_tensor(X3s[:, :, sl], X3s[:, :, sl],
                                R.ab[:, None, :].broadcast_to([P, 8, 512]).transpose([0, 1, 2])
                                if False else R.ab[:, None, 0:512].broadcast_to([P, 8, 512]),
                                ALU.mult)
                return e.tensor_tensor(X3s[:, :, sl], X3s[:, :, sl],
                                       R.bb[:, None, 0:512].broadcast_to([P, 8, 512]), ALU.add)
            return body
        dap = C.V(mkap(), [dc, dd])
        prev_ap = dap
        out_deps.append(dap)
    return out_deps


def final_store(C, R, SRC, out_dram, deps):
    bar = C.all_done()
    deps = list(deps) + bar
    R.pf.reset(keep=64)
    xtm = R.pf.take(D)
    dmas = []
    prev_copy = None
    for tt in range(S // P):
        t0 = tt * P

        def mktr(t0=t0):
            def body(e):
                last = None
                for cc in range(8):
                    pb = R.pm[cc // 4]
                    last = e.transpose(pb[:, (cc % 4) * P:(cc % 4 + 1) * P], SRC[cc][:, t0:t0 + P], R.idf[:])
                return last
            return body
        dtr = C.T(mktr(), list(deps) + ([prev_copy] if prev_copy else []))

        def mkcp():
            def body(e):
                e.tensor_copy(xtm[:, 0:512], R.pm[0][:, 0:512])
                return e.tensor_copy(xtm[:, 512:1024], R.pm[1][:, 0:512])
            return body
        dcp = C.V(mkcp(), [dtr])
        ddma = C.dma("sync", out_dram[t0:t0 + P, :], xtm, [dcp])
        prev_copy = ddma
        dmas.append(ddma)
    return dmas



def emit_l1ffn(C, R, X, dr):
    KC, MC, NG = 8, 8, 4
    bar = C.all_done()
    R.pbf.reset()
    R.pf.reset()
    wqr = R.pbf.take(8 * D)
    xqr = R.pbf.take(8 * S)
    wq = [wqr[:, k * D:(k + 1) * D] for k in range(KC)]
    xq = [xqr[:, c * S:(c + 1) * S] for c in range(KC)]
    fr = R.pf.take(64)
    cols = fr[:, 0:4]
    wsc = fr[:, 8:16]
    bia = fr[:, 16:24]
    scl = fr[:, 24:32]
    gmx = fr[0:1, 32:36]
    ccs = R.pf.take(P)[0:1, :]

    dw = [C.dma("gpsimd", wqr.rearrange("p (kc c) -> p kc c", kc=8),
                dr["l1f_w_t"].rearrange("(kc p) m -> p kc m", p=P), bar)]
    dsc = C.dma("sync", wsc, dr["l1f_wscale_t"], bar)
    dbi = C.dma("sync", bia, dr["l1f_bias_t"], bar)

    amx = R.pf.take(8)
    def absmax(e):
        last = None
        for cc in range(8):
            last = e.tensor_reduce(amx[:, cc:cc + 1], X[cc][:], AX.X, ALU.max, apply_absolute_value=True)
        return last
    d1a = C.V(absmax, bar)
    d1 = C.V(lambda e: e.tensor_reduce(cols[:, 0:1], amx[:], AX.X, ALU.max), [d1a])
    d2 = C.T(lambda e: e.transpose(R.pss[0:1, 0:P], cols[:, 0:1], R.idf[:]), [d1] + bar)
    d3 = C.V(lambda e: e.tensor_reduce(gmx[0:1, 0:1], R.pss[0:1, 0:P], AX.X, ALU.max), [d2])
    d4 = C.V(lambda e: e.tensor_copy(ccs, gmx[0:1, 0:1].broadcast_to([1, P])), [d3])

    dci = C.dma("gpsimd", dr["l1f_cc_in"], ccs, [d4])
    dco = C.step("gpsimd",
                 lambda e: e.collective_compute("AllReduce", ALU.max,
                                                replica_groups=REPLICA_GROUPS,
                                                ins=[dr["l1f_cc_in"]],
                                                outs=[dr["l1f_cc_out"]]),
                 [dci])
    dcb = C.dma("gpsimd", gmx[0:1, 0:2], dr["l1f_cc_out"][0:1, 0:2], [dco])

    d5 = C.V(lambda e: e.tensor_scalar(gmx[0:1, 2:3], gmx[0:1, 0:1], 1.0 / 127.0, 1e-5, ALU.mult, ALU.max), [dcb])
    d6 = C.V(lambda e: e.reciprocal(gmx[0:1, 3:4], gmx[0:1, 2:3]), [d5])
    d7 = C.T(lambda e: e.matmul(R.pss[:, 0:2], R.onerf[0:1, :], gmx[0:1, 2:4], start=True, stop=True),
             [d6])
    d8 = C.V(lambda e: e.tensor_copy(cols[:, 2:4], R.pss[:, 0:2]), [d7])
    d9 = C.V(lambda e: e.tensor_tensor(scl, wsc, cols[:, 2:3].broadcast_to([P, MC]), ALU.mult),
             [d8, dsc])

    qdeps = []
    xq3 = xqr.rearrange("p (cc t) -> p cc t", cc=8)
    X3 = R.X_r.rearrange("p (cc t) -> p cc t", cc=8)
    for cc in range(8):
        def mkq(cc=cc):
            def body(e):
                e.tensor_scalar(R.big[:], X[cc][:], cols[:, 3:4], None, ALU.mult)
                return e.tensor_scalar(xq[cc][:], R.big[:], MAGIC, MAGIC, ALU.add, ALU.subtract)
            return body
        qdeps.append(C.V(mkq(), [d8] if cc == 0 else [d8, qdeps[-1]]))

    prev_ev = {}
    ev_deps = []
    for mc in range(MC):
        for ng in range(NG):
            pbi = (mc * NG + ng) % 4
            pb = R.pm[pbi]

            def mkmm(mc=mc, ng=ng, pb=pb):
                def body(e):
                    last = None
                    for kc in range(KC):
                        last = e.matmul(pb[:, 0:512], wq[kc][:, mc * P:(mc + 1) * P],
                                        xq[kc][:, ng * 512:(ng + 1) * 512],
                                        start=(kc == 0), stop=(kc == KC - 1))
                    return last
                return body
            mmd = list(qdeps) + dw + [d9, dbi]
            if pbi in prev_ev:
                mmd.append(prev_ev[pbi])
            dmm = C.T(mkmm(), mmd)

            def mkev(mc=mc, ng=ng, pb=pb):
                sl = slice(ng * 512, (ng + 1) * 512)

                def body(e):
                    e.scalar_tensor_tensor(X[mc][:, sl], pb[:, 0:512], scl[:, mc:mc + 1],
                                           X[mc][:, sl], ALU.mult, ALU.add)
                    return e.tensor_scalar(X[mc][:, sl], X[mc][:, sl], bia[:, mc:mc + 1], None, ALU.add)
                return body
            dev = C.V(mkev(), [dmm])
            prev_ev[pbi] = dev
            ev_deps.append(dev)

    return ln_fm(C, R, X, X, 1e-5, ev_deps, sqw=xqr[:, 0:4096])


def l1ffn_host(inp):
    w = np.asarray(inp["L1_ffn_weight_int8"])
    ws = np.asarray(inp["L1_ffn_weight_scale"], np.float32)
    bi = np.asarray(inp["L1_ffn_bias"], np.float32)
    return {
        "l1f_w_t": np.ascontiguousarray(w.T),
        "l1f_wscale_t": np.ascontiguousarray(ws.reshape(8, 128).T),
        "l1f_bias_t": np.ascontiguousarray(bi.reshape(8, 128).T),
    }


def l1ffn_decls():
    return [("l1f_w_t", [D, D], I8), ("l1f_wscale_t", [P, 8], F32),
            ("l1f_bias_t", [P, 8], F32)]



def emit_l1attn(C, R, X, dr):
    KC, MC = 8, 8
    BLK, WIN = 64, 68
    H = 1024
    NGH = H // 512

    bar = C.all_done()
    R.pbf.reset()
    R.pf.reset()
    wr = R.pbf.take(8 * D)
    qr = R.pbf.take(8 * H)
    kr = R.pbf.take(8 * (H + 4))
    vr = R.pbf.take(8 * (H + 4))
    q = [qr[:, c * H:(c + 1) * H] for c in range(8)]
    kx = [kr[:, c * (H + 4):(c + 1) * (H + 4)] for c in range(8)]
    vx = [vr[:, c * (H + 4):(c + 1) * (H + 4)] for c in range(8)]
    ws = [wr[:, k * D:(k + 1) * D] for k in range(KC)]

    xbr = R.pbf.take(8 * 1024)
    xb2 = [xbr[:, c * 1024:(c + 1) * 1024] for c in range(8)]
    xexr = R.pbf.take(16)
    xex = [xexr[:, c * 2:(c + 1) * 2] for c in range(8)]
    fr = R.pf.take(512)
    bcols = fr[:, 0:24]
    mcol = fr[0:BLK, 24:28]
    sc = fr[0:BLK, 28:28 + WIN]
    ot = R.pf.take(D)[0:BLK, :]
    pbr = R.pbf.take(WIN + BLK)
    pb = pbr[0:BLK, 0:WIN]
    ptb = pbr[0:WIN, WIN:WIN + BLK]
    vwt = R.pbf.take(D)[0:WIN, :]
    masks = R.pf.take(3 * WIN)[0:BLK, :]

    dbc = C.dma("sync", bcols, dr["l1a_b_t"], bar)
    dmk = C.dma("sync", masks, dr["l1a_masks"], bar)

    xs_upd = []
    prev_half_kv = None
    prev_state = None
    lp_pss = [None]
    for hh in range(2):
        H0 = hh * H
        if hh == 1:
            def mvkv(e):
                last = None
                for cc in range(8):
                    e.tensor_copy(kx[cc][:, 0:2], kx[cc][:, H:H + 2])
                    last = e.tensor_copy(vx[cc][:, 0:2], vx[cc][:, H:H + 2])
                return last
            dkv = C.V(mvkv, [xs_upd[-1]])
            dpe = C.V(lambda e: [e.memset(kx[cc][:, H + 2:H + 4], 0.0) or
                                 e.memset(vx[cc][:, H + 2:H + 4], 0.0) for cc in range(8)][-1],
                      [dkv])
        else:
            def mkpad(e):
                last = None
                for cc in range(8):
                    e.memset(kx[cc][:, 0:2], 0.0)
                    last = e.memset(vx[cc][:, 0:2], 0.0)
                return last
            dpe = C.V(mkpad, bar)

        projs = [("l1a_wq", q, 0, 0), ("l1a_wk", kx, 2, 8), ("l1a_wv", vx, 2, 16)]
        ev_all = []
        prev_mm_last = prev_state["wmm"] if prev_state else None
        prev_ev = dict(prev_state["ev"]) if prev_state else {}
        xb_deps = []
        for ng in range(NGH):
            def mkxb(ng=ng, H0=H0):
                def body(e):
                    last = None
                    for kc in range(8):
                        last = e.tensor_copy(xb[kc][:, ng * 256:(ng + 1) * 256] if False else xb[kc],
                                             X[kc][:, H0 + ng * 512:H0 + (ng + 1) * 512])
                    return last
                return body
        xbr2 = None
        def mkxb_all(H0=H0):
            def body(e):
                last = None
                for kc in range(8):
                    for ng in range(NGH):
                        last = e.tensor_copy(xb2[kc][:, ng * 512:(ng + 1) * 512],
                                             X[kc][:, H0 + ng * 512:H0 + (ng + 1) * 512])
                return last
            return body
        dxb = C.V(mkxb_all(), bar if hh == 0 else [xs_upd[-1]])
        for j, (wname, dst, off, bofs) in enumerate(projs):
            wdeps = [C.dma("gpsimd", wr.rearrange("p (kc c) -> p kc c", kc=8),
                           dr[wname].rearrange("(kc p) m -> p kc m", p=P),
                           ([prev_mm_last] if prev_mm_last else bar))]
            for mc in range(MC):
                for ng in range(NGH):
                    pbi = (mc * NGH + ng) % 4
                    pbank = R.pm[pbi]

                    def mkmm(mc=mc, ng=ng, pbank=pbank):
                        def body(e):
                            last = None
                            for kc in range(KC):
                                last = e.matmul(pbank[0:P, 0:512],
                                                ws[kc][:, mc * P:(mc + 1) * P],
                                                xb2[kc][:, ng * 512:(ng + 1) * 512],
                                                start=(kc == 0), stop=(kc == KC - 1))
                            return last
                        return body
                    mmd = wdeps + [dxb] + ([prev_ev[pbi]] if pbi in prev_ev else [])
                    dmm = C.T(mkmm(), mmd)
                    if mc == MC - 1 and ng == NGH - 1:
                        prev_mm_last = dmm

                    def mkev(mc=mc, ng=ng, pbank=pbank, dst=dst, off=off, bofs=bofs):
                        def body(e):
                            return e.activation(dst[mc][:, off + ng * 512:off + (ng + 1) * 512],
                                                pbank[0:P, 0:512], AF.Identity,
                                                bias=bcols[:, bofs + mc:bofs + mc + 1], scale=1.0)
                        return body
                    dev = C.A(mkev(), [dmm, dbc, dpe])
                    prev_ev[pbi] = dev
                    ev_all.append(dev)
            if hh == 0 and j in (1, 2):
                dxe = C.V(lambda e: [e.tensor_copy(xex[kc], X[kc][:, H:H + 2]) for kc in range(8)][-1],
                          [dxb])
                def mkexmm(dst=dst):
                    def body(e):
                        last = None
                        for mc in range(MC):
                            for kc in range(KC):
                                last = e.matmul(R.pss[0:P, mc * 2:(mc + 1) * 2],
                                                ws[kc][:, mc * P:(mc + 1) * P], xex[kc],
                                                start=(kc == 0), stop=(kc == KC - 1))
                        return last
                    return body
                dxm = C.T(mkexmm(), [dxe, prev_mm_last] + ([lp_pss[0]] if lp_pss[0] else []))
                def mkexev(dst=dst, bofs=bofs):
                    def body(e):
                        last = None
                        for mc in range(MC):
                            last = e.activation(dst[mc][:, H + 2:H + 4], R.pss[0:P, mc * 2:(mc + 1) * 2],
                                                AF.Identity, bias=bcols[:, bofs + mc:bofs + mc + 1],
                                                scale=1.0)
                        return last
                    return body
                dxv = C.A(mkexev(), [dxm])
                prev_mm_last = dxm
                ev_all.append(dxv)
        prev_half_kv = ev_all[-1]

        if "dbgq" in dr and hh == 0:
            ddq = C.dma("sync", dr["dbgq"][0:P, :], q[0][:, :], [ev_all[-1]])
            ddk = C.dma("sync", dr["dbgk"][0:P, :], kx[0][:, :], [ev_all[-1]])
            ddv = C.dma("sync", dr["dbgv"][0:P, :], vx[0][:, :], [ev_all[-1]])

        prev = prev_state["blk"] if prev_state else \
            {"sc": None, "ptr": None, "out0": None, "out1": None, "tr": None, "ex": None,
             "pss": None}
        for b in range(16):
            t0 = b * BLK
            gb = hh * 16 + b
            mi = 1 if gb == 0 else (2 if gb == 31 else 0)
            msl = slice(mi * WIN, (mi + 1) * WIN)

            def mksc(t0=t0):
                def body(e):
                    last = None
                    for kc in range(KC):
                        last = e.matmul(R.pm[4][0:BLK, 0:WIN], q[kc][:, t0:t0 + BLK],
                                        kx[kc][:, t0:t0 + WIN],
                                        start=(kc == 0), stop=(kc == KC - 1))
                    return last
                return body
            dsc_ = C.T(mksc(), [ev_all[-1]] + ([prev["sc"]] if prev["sc"] else [dmk]))

            dms_a = C.V(lambda e, msl=msl: e.tensor_tensor(sc, R.pm[4][0:BLK, 0:WIN], masks[:, msl], ALU.add),
                        [dsc_] + ([prev["ex"]] if prev.get("ex") else []))
            dms_b = C.V(lambda e: e.tensor_reduce(mcol[:, 0:1], sc, AX.X, ALU.max), [dms_a])
            dms = C.V(lambda e: e.tensor_scalar(mcol[:, 1:2], mcol[:, 0:1], -1.0 / 32.0, None, ALU.mult),
                      [dms_b])
            prev["sc"] = dms

            dex = C.A(lambda e: e.activation(pb, sc, AF.Exp, bias=mcol[:, 1:2], scale=1.0 / 32.0),
                      [dms])
            prev["ex"] = dex

            dz_a = C.V(lambda e: e.tensor_reduce(mcol[:, 2:3], pb, AX.X, ALU.add), [dex])
            dz = C.V(lambda e: e.reciprocal(mcol[:, 3:4], mcol[:, 2:3]), [dz_a])

            dpt = C.T(lambda e: e.transpose(R.ptr[0:WIN, 0:BLK], pb, R.idb[0:BLK, 0:BLK]),
                      [dex] + ([prev["ptr"]] if prev["ptr"] else []))
            dptv = C.V(lambda e: e.tensor_copy(ptb, R.ptr[0:WIN, 0:BLK]), [dpt])

            dprev_e = dptv
            dprev_o = prev["pss"] if prev["pss"] else ev_all[-1]
            dvts = []
            for kc in range(KC):
                if kc % 2 == 0:
                    dvt = C.T(lambda e, kc=kc, t0=t0: e.transpose(
                        R.ptr[0:WIN, 0:P], vx[kc][:, t0:t0 + WIN], R.idb[:]), [dprev_e])
                    dvv = C.A(lambda e, kc=kc: e.activation(
                        vwt[:, kc * P:(kc + 1) * P], R.ptr[0:WIN, 0:P],
                        AF.Identity, bias=0.0, scale=1.0), [dvt])
                    dprev_e = dvv
                else:
                    dvt = C.T(lambda e, kc=kc, t0=t0: e.matmul(
                        R.pss[0:WIN, 0:P], vx[kc][:, t0:t0 + WIN], R.idb[:],
                        start=True, stop=True), [dprev_o])
                    dvv = C.V(lambda e, kc=kc: e.tensor_copy(
                        vwt[:, kc * P:(kc + 1) * P], R.pss[0:WIN, 0:P]), [dvt])
                    dprev_o = dvv
                dvts.append(dvv)
            prev["ptr"] = dprev_e
            prev["pss"] = dprev_o
            lp_pss[0] = dprev_o

            dm0 = C.T(lambda e: e.matmul(R.pm[0][0:BLK, 0:512], ptb, vwt[:, 0:512], start=True, stop=True),
                      [dptv, dvts[2], dvts[3]] + ([prev["out0"]] if prev["out0"] else []))
            dm1 = C.T(lambda e: e.matmul(R.pm[1][0:BLK, 0:512], ptb, vwt[:, 512:1024], start=True, stop=True),
                      [dvts[6], dvts[7]] + ([prev["out1"]] if prev["out1"] else []))
            de0 = C.A(lambda e: e.activation(ot[:, 0:512], R.pm[0][0:BLK, 0:512], AF.Copy,
                                             bias=0.0, scale=mcol[:, 3:4]), [dm0, dz])
            de1 = C.A(lambda e: e.activation(ot[:, 512:1024], R.pm[1][0:BLK, 0:512], AF.Copy,
                                             bias=0.0, scale=mcol[:, 3:4]), [dm1, dz])
            prev["out0"], prev["out1"] = de0, de1

            dtr = C.T(lambda e: [e.transpose(R.pm[2][:, kc * BLK:(kc + 1) * BLK],
                                             ot[0:BLK, kc * P:(kc + 1) * P], R.idf[0:BLK, 0:BLK])
                                 for kc in range(8)][-1],
                      [de0, de1] + ([prev["tr"]] if prev["tr"] else []))

            def mkadd(t0=t0, H0=H0):
                def body(e):
                    X3a = R.X_r.rearrange("p (cc t) -> p cc t", cc=8)
                    sl3 = slice(H0 + t0, H0 + t0 + BLK)
                    return e.tensor_tensor(X3a[:, :, sl3], X3a[:, :, sl3],
                                           R.pm[2][:, 0:512].rearrange("p (cc t) -> p cc t", cc=8),
                                           ALU.add)
                return body
            dad = C.V(mkadd(), [dtr])
            prev["tr"] = dad
            xs_upd.append(dad)
            if "dbg_sc" in dr and gb == 0:
                C.dma("sync", dr["dbg_sc"], sc, [dms])
                C.dma("sync", dr["dbg_pb"], pb, [dex])
                C.dma("sync", dr["dbg_pt"], ptb, [dptv])
                C.dma("sync", dr["dbg_vw"], vwt, [dvts[7]])
                C.dma("sync", dr["dbg_ot"], ot, [de0, de1])
                C.dma("sync", dr["dbg_mc"], mcol, [dz])
        prev_state = {"wmm": prev_mm_last,
                      "ev": {0: prev["out0"], 1: prev["out1"], 2: prev["tr"], 3: prev_ev.get(3)},
                      "blk": prev}

    if "dbgx" in dr:
        dx = [C.dma("sync", dr["dbgx"][cc * P:(cc + 1) * P, :], X[cc][:], [xs_upd[-1]]) for cc in range(8)]
        return ln_fm(C, R, X, X, 1e-5, [xs_upd[-1]] + dx)
    return ln_fm(C, R, X, X, 1e-5, [xs_upd[-1]], sqw=qr[:, 0:4096])


def l1attn_host(inp):
    b = np.concatenate([np.asarray(inp["L1_attn_bq"], np.float32).reshape(8, 128).T,
                        np.asarray(inp["L1_attn_bk"], np.float32).reshape(8, 128).T,
                        np.asarray(inp["L1_attn_bv"], np.float32).reshape(8, 128).T], axis=1)
    WIN, BLK = 68, 64
    masks = np.zeros((BLK, 3 * WIN), np.float32)
    for mi in range(3):
        m = np.full((BLK, WIN), -1e30, np.float32)
        for t in range(BLK):
            for i in range(WIN):
                if 0 <= i - t <= 4:
                    m[t, i] = 0.0
        if mi == 1:
            m[:, 0:2] = -1e30
        if mi == 2:
            m[:, WIN - 2:WIN] = -1e30
        masks[:, mi * WIN:(mi + 1) * WIN] = m
    return {
        "l1a_wq": np.ascontiguousarray(np.asarray(inp["L1_attn_Wq"], np.float32)),
        "l1a_wk": np.ascontiguousarray(np.asarray(inp["L1_attn_Wk"], np.float32)),
        "l1a_wv": np.ascontiguousarray(np.asarray(inp["L1_attn_Wv"], np.float32)),
        "l1a_b_t": np.ascontiguousarray(b),
        "l1a_masks": masks,
    }


def l1attn_decls():
    return [("l1a_wq", [D, D], F32), ("l1a_wk", [D, D], F32), ("l1a_wv", [D, D], F32),
            ("l1a_b_t", [P, 24], F32), ("l1a_masks", [64, 3 * 68], F32)]



def declare(nc, decls, kind="ExternalInput"):
    return {n: nc.dram_tensor(n, shp, dt, kind=kind).ap() for (n, shp, dt) in decls}


BLOCK_EMIT = {}
BLOCK_DECLS = {}


def _register_blocks():
    BLOCK_EMIT.update({"l1ffn": emit_l1ffn, "l1attn": emit_l1attn,
                       "l0attn": emit_l0attn})
    BLOCK_DECLS.update({"l1ffn": l1ffn_decls, "l1attn": l1attn_decls,
                        "l0ffn": l0ffn_decls, "l0attn": l0attn_decls})


def build_launch(blocks):
    nc = bass.Bass("TRN2", target_bir_lowering=False)
    st = ExitStack()
    _register_blocks()
    dr = {}
    decls = [("x_fm", [D, S], F32)] + Res.decls()
    for b in blocks:
        decls += BLOCK_DECLS[b]()
    dr.update(declare(nc, decls))
    dr.update(declare(nc, [("y", [S, D], F32)], kind="ExternalOutput"))
    import os
    if os.environ.get("KDBG") and "l0attn" in blocks:
        dr.update(declare(nc, [("dbg_q", [P, 2048], BF16)], kind="ExternalOutput"))
    if os.environ.get("KDBG") and "l0ffn" in blocks:
        dr.update(declare(nc, [("dbg_swr", [1, 4], F32), ("dbg_w1q", [P, 512], I8),
                               ("dbg_rows", [3, S], F32), ("dbg_xq", [P, S], BF16),
                               ("dbg_hs", [P, 512], BF16), ("dbg_x", [D, S], F32)], kind="ExternalOutput"))
    if os.environ.get("KDBG") and "l1attn" in blocks:
        dr.update(declare(nc, [("dbgq", [P, 1024], BF16), ("dbgk", [P, 1028], BF16),
                               ("dbgv", [P, 1028], BF16), ("dbgx", [D, S], F32),
                               ("dbg_sc", [64, 68], F32), ("dbg_pb", [64, 68], BF16),
                               ("dbg_pt", [68, 64], BF16), ("dbg_vw", [68, D], BF16),
                               ("dbg_ot", [64, D], F32), ("dbg_mc", [64, 4], F32)], kind="ExternalOutput"))
    if "l1ffn" in blocks:
        dr["l1f_cc_in"] = nc.dram_tensor("l1f_cc_in", [1, P], F32, kind="Internal").ap()
        dr["l1f_cc_out"] = nc.dram_tensor("l1f_cc_out", [1, P], F32, kind="Internal").ap()
    if "l0ffn" in blocks:
        dr["l0f_w1q"] = nc.dram_tensor("l0f_w1q", [D, 4096], BF16, kind="Internal").ap()
        dr["l0f_w2q"] = nc.dram_tensor("l0f_w2q", [4096, D], BF16, kind="Internal").ap()
        dr["l0f_scr"] = nc.dram_tensor("l0f_scr", [16, P], F32, kind="Internal").ap()
    if "l0attn" in blocks:
        dr["l0a_scr1"] = nc.dram_tensor("l0a_scr1", [NH, P], F32, kind="Internal").ap()
        dr["l0a_inproj_bf"] = nc.dram_tensor("l0a_inproj_bf", [D, 4384], BF16, kind="Internal").ap()
        dr["l0a_outproj_bf"] = nc.dram_tensor("l0a_outproj_bf", [2048, D], BF16, kind="Internal").ap()
        dr["l0a_scr2"] = nc.dram_tensor("l0a_scr2", [NH, P], BF16, kind="Internal").ap()
        dr["l0a_u2"] = nc.dram_tensor("l0a_u2", [D, S], BF16, kind="Internal").ap()

    C = Ctx(nc, st)
    R = Res(C)
    cdeps = R.load(C, dr)
    X_r = C.sb("x_all", [P, 8 * S], F32)
    X = [X_r[:, cc * S:(cc + 1) * S] for cc in range(8)]
    R.X_r = X_r
    xdeps = [C.dma("sync", X_r.rearrange("p (cc t) -> p cc t", cc=8),
                   dr["x_fm"].rearrange("(cc p) t -> p cc t", p=P))]
    dbar = C.V(lambda e: None, cdeps + xdeps)
    C.T(lambda e: None, [dbar])
    C.A(lambda e: None, [dbar])
    C.step("gpsimd", lambda e: None, [dbar])

    pre = None
    if "l0ffn" in blocks:
        pre = emit_l0ffn_prelude(C, R, dr, [dbar])
    deps = [dbar]
    for bname in blocks:
        if bname == "l0ffn":
            deps = emit_l0ffn(C, R, X, dr, pre[0], pre[1])
        else:
            deps = BLOCK_EMIT[bname](C, R, X, dr)

    final_store(C, R, X, dr["y"], deps)
    C.step("sync", lambda e: None, C.all_done())
    build_and_finalize(nc, C)
    st.close()
    return nc


def kernel(**inputs):
    x = inputs["L0_attn_hidden_states"]
    consts = Res.host_arrays()
    host = {}
    blocks = []
    if "L0_attn_in_proj_w" in inputs:
        host.update(l0attn_host(inputs))
        blocks.append("l0attn")
    if "L0_ffn_w1" in inputs:
        host.update(l0ffn_host(inputs))
        blocks.append("l0ffn")
    if "L1_attn_Wq" in inputs:
        host.update(l1attn_host(inputs))
        blocks.append("l1attn")
    if "L1_ffn_weight_int8" in inputs:
        host.update(l1ffn_host(inputs))
        blocks.append("l1ffn")
    nc = build_launch(blocks)
    in_maps = []
    for b in range(NCORES):
        m = {"x_fm": np.ascontiguousarray(np.asarray(x[b]).T)}
        m.update(consts)
        m.update(host)
        in_maps.append(m)
    res = run_bass_kernel_spmd(nc, in_maps, core_ids=list(range(NCORES)))
    return np.stack([res.results[b]["y"] for b in range(NCORES)], axis=0)



def emit_l0ffn_prelude(C, R, dr, deps):
    fr = R.pf.take(64)
    slots = fr[:, 0:48]
    swc = fr[:, 48:52]
    swr = fr[0:1, 56:60]

    slab_list = []
    for kc in range(8):
        for g in range(4):
            slab_list.append((dr["l0f_w1t"][kc * P:(kc + 1) * P, g * 1024:(g + 1) * 1024],
                              dr["l0f_w1q"][kc * P:(kc + 1) * P, g * 1024:(g + 1) * 1024],
                              kc))
    for kc in range(32):
        slab_list.append((dr["l0f_w2t"][kc * P:(kc + 1) * P, :],
                          dr["l0f_w2q"][kc * P:(kc + 1) * P, :],
                          16 + kc // 2))
    wslab_r = R.pbf.take(8192)
    bigs = [R.big[:, 0:1024], R.big[:, 1024:2048],
            wslab_r[:, 0:2048].bitcast(F32), wslab_r[:, 2048:4096].bitcast(F32)]
    qsl_r = R.pbf.take(2048)
    qbs = [qsl_r[:, 0:1024], qsl_r[:, 1024:2048]]

    red_deps = []
    prev_red = [None, None, None, None]
    acc_seen = set()
    for n, (ap_in, ap_out, slot) in enumerate(slab_list):
        buf = bigs[n % 4]
        dl = C.dma("sync" if n % 2 == 0 else "gpsimd", buf, ap_in,
                   list(deps) + ([prev_red[n % 4]] if prev_red[n % 4] else []))
        first = slot not in acc_seen
        acc_seen.add(slot)
        acol_ = fr[:, 60 + (n % 2):61 + (n % 2)]
        dabs = C.A(lambda e, buf=buf, acol_=acol_: e.activation(
            buf, buf, AF.Abs, bias=0.0, scale=1.0, accum_out=acol_),
            [dl] + (red_deps[-2:-1] if len(red_deps) >= 2 else []))
        if first:
            dr_ = C.V(lambda e, slot=slot, acol_=acol_: e.tensor_copy(
                slots[:, slot:slot + 1], acol_), [dabs])
        else:
            dr_ = C.V(lambda e, slot=slot, acol_=acol_: e.tensor_tensor(
                slots[:, slot:slot + 1], slots[:, slot:slot + 1], acol_, ALU.add),
                [dabs] + red_deps[-1:])
        prev_red[n % 4] = dr_
        red_deps.append(dr_)
    d1 = C.V(lambda e: e.tensor_reduce(swc[:, 0:1], slots[:, 0:8], AX.X, ALU.add), red_deps[-2:])
    d2 = C.V(lambda e: e.tensor_reduce(swc[:, 1:2], slots[:, 16:32], AX.X, ALU.add), [d1])
    d3 = C.T(lambda e: e.transpose(R.pss[0:2, 0:P], swc[:, 0:2], R.idf[:]), [d2])
    d4a = C.V(lambda e: e.tensor_copy(R.ab[0:2, 0:P], R.pss[0:2, 0:P]), [d3])
    d4b = C.V(lambda e: e.tensor_reduce(R.ab[0:2, P:P + 1], R.ab[0:2, 0:P], AX.X, ALU.add), [d4a])
    d4c = C.T(lambda e: e.transpose(R.pss[0:1, 256:258], R.ab[0:2, P:P + 1], R.idf[0:2, 0:2]), [d4b, d3])
    d5 = C.V(lambda e: e.tensor_copy(swr[0:1, 0:2], R.pss[0:1, 256:258]), [d4c])
    d6 = C.V(lambda e: e.tensor_scalar(swr[0:1, 0:2], swr[0:1, 0:2], 1.0 / (4096.0 * 1024.0), 1e-5,
                                       ALU.mult, ALU.max), [d5])
    d7 = C.V(lambda e: e.reciprocal(swr[0:1, 2:4], swr[0:1, 0:2]), [d6])
    d8 = C.T(lambda e: e.matmul(R.pss[:, 0:2], R.onerf[0:1, :], swr[0:1, 2:4], start=True, stop=True),
             [d7])
    d9 = C.V(lambda e: e.tensor_copy(swc[:, 2:4], R.pss[:, 0:2]), [d8])

    q_deps = []
    prev_qv = [None, None, None, None]
    for n, (ap_in, ap_out, slot) in enumerate(slab_list):
        buf = bigs[n % 4]
        qb = qbs[n % 2]
        swcol = swc[:, 2:3] if n < 32 else swc[:, 3:4]
        dl = C.dma("sync", buf, ap_in, [d9] + ([prev_qv[n % 4]] if prev_qv[n % 4] else []))

        def mkq(buf=buf, qb=qb, swcol=swcol):
            def body(e):
                e.tensor_scalar(buf, buf, swcol, None, ALU.mult)
                e.tensor_scalar(buf, buf, MAGIC, MAGIC, ALU.add, ALU.subtract)
                return e.tensor_scalar(qb[:], buf, -1.0, 1.0, ALU.max, ALU.min)
            return body
        dq = C.V(mkq(), [dl] + (q_deps[-2:-1] if len(q_deps) >= 2 else []))
        ds = C.dma("gpsimd", ap_out, qb[:], [dq])
        prev_qv[n % 4] = dq
        q_deps.append(ds)
    return q_deps[-2:], swr


def emit_l0ffn(C, R, X, dr, pre_deps, swr):
    NCH = 32
    bar = C.all_done() + list(pre_deps)
    R.pbf.reset()
    R.pf.reset(keep=64)
    xqr = R.pbf.take(8 * S)
    xq = [xqr[:, c * S:(c + 1) * S] for c in range(8)]
    wtr = R.pbf.take(2 * 8 * 512)
    wt = [wtr[:, i * 4096:(i + 1) * 4096] for i in range(2)]
    hsr = R.pbf.take(NCH * 512)
    hs = [hsr[:, c * 512:(c + 1) * 512] for c in range(NCH)]

    fq = R.pf.take(2 * 515)
    hraw = [fq[:, i * 515:(i + 1) * 515] for i in range(2)]
    fc = R.pf.take(256)
    cw = fc[:, 0:96]
    cb = fc[:, 96:128]
    hleft = fc[:, 128:160]
    hrsc = fc[:, 192:193]
    s2b = R.pf.take(512)
    o1b = R.pf.take(512)
    o2b = R.pf.take(512)

    dcw = C.dma("sync", cw, dr["l0f_convw"], bar)
    dcb = C.dma("sync", cb, dr["l0f_convb"], bar)

    amxc = R.pf.take(16)
    X3f = R.X_r.rearrange("p (cc t) -> p t cc", cc=8)
    tp_deps = []
    prev_tp = None
    dtree = C.V(lambda e: e.tensor_reduce(R.big[:, 0:S], X3f, AX.X, ALU.max,
                                          apply_absolute_value=True), bar)
    for tt in range(16):
        dtp = C.T(lambda e, tt=tt: e.transpose(R.pss[:, (tt % 4) * P:(tt % 4 + 1) * P],
                                               R.big[:, tt * P:(tt + 1) * P], R.idf[:]),
                  [dtree] + ([prev_tp] if prev_tp else []))
        dtv = C.V(lambda e, tt=tt: e.tensor_reduce(amxc[:, tt:tt + 1],
                                                   R.pss[:, (tt % 4) * P:(tt % 4 + 1) * P],
                                                   AX.X, ALU.max), [dtp])
        prev_tp = dtv
        tp_deps.append(dtv)
    dft = C.T(lambda e: e.transpose(R.pm[4][0:16, 0:P], amxc[:, 0:16], R.idf[:]), tp_deps)
    dfv = C.V(lambda e: e.tensor_copy(R.ab[0:16, 0:P], R.pm[4][0:16, 0:P]), [dft])
    prev_pm4_init = dfv
    dfi = C.dma("sync", R.r0[:, 0:S], R.ab[0:16, 0:P], [dfv])
    dsx = C.V(lambda e: e.tensor_single_scalar(R.r0[:, 0:S], R.r0[:, 0:S], 1e-5, ALU.max), [dfi])
    dr1 = C.V(lambda e: e.reciprocal(R.r1[:, 0:S], R.r0[:, 0:S]), [dsx])
    dr2 = C.V(lambda e: e.tensor_single_scalar(R.r1[:, 0:S], R.r1[:, 0:S], 127.0, ALU.mult), [dr1])
    do1r = C.V(lambda e: e.tensor_scalar(R.r2[:, 0:S], R.r0[:, 0:S], swr[0:1, 0:1], 1.0 / 127.0,
                                         ALU.mult, ALU.mult), [dsx] + list(pre_deps))

    q_deps = []
    prev_b = None
    qv_last = None
    X3q = R.X_r.rearrange("p (cc t) -> p cc t", cc=8)
    xq3 = xqr.rearrange("p (cc t) -> p cc t", cc=8)
    bigq = R.big.rearrange("p (ch q) -> p ch q", ch=4)
    for ng in range(4):
        sl = slice(ng * 512, (ng + 1) * 512)
        db = bcast_row(C, R, R.ab, R.r1[:, sl], 512, [dr2, do1r, prev_b, dfv], R.pm[4])
        for se in range(2):
            g_ = C.G(lambda e, se=se, sl=sl: e.tensor_tensor(
                bigq, X3q[:, se * 4:(se + 1) * 4, sl],
                R.ab[:, None, 0:512].broadcast_to([P, 4, 512]), ALU.mult),
                [db] + ([qv_last] if qv_last else []))
            qv_last = C.V(lambda e, se=se, sl=sl: e.tensor_scalar(
                xq3[:, se * 4:(se + 1) * 4, sl], bigq, MAGIC, MAGIC, ALU.add, ALU.subtract), [g_])
            q_deps.append(qv_last)
        prev_b = qv_last

    dhl = C.V(lambda e: e.memset(hleft, 0.0), bar)
    prev_pm4_init = prev_b if prev_b else prev_pm4_init

    if "dbg_swr" in dr:
        C.dma("sync", dr["dbg_swr"], swr, list(pre_deps))
        C.dma("sync", dr["dbg_w1q"], dr["l0f_w1q"][0:P, 0:512], list(pre_deps))
        C.dma("sync", dr["dbg_rows"][0:1, :], R.r1[:, 0:S], [dr2])
        C.dma("sync", dr["dbg_rows"][1:2, :], R.r2[:, 0:S], [do1r])
        dbg_xq = C.dma("sync", dr["dbg_xq"], xq[0][:, 0:S], q_deps)
    ev_deps = []
    prev_pm = {4: prev_pm4_init}
    prev_w2_last = None
    fuser = {}
    prev_conv = {}
    prev_wdma = None
    prev_hq = None
    prev_hrsc = None
    prev_cbuf = {}
    prev_dev2 = None
    for qq in range(4):
        sl = slice(qq * 512, (qq + 1) * 512)
        do1 = bcast_row(C, R, o1b, R.r2[:, sl], 512,
                        [do1r] + ([prev_pm.get(5)] if prev_pm.get(5) else []), R.pm[5])
        prev_pm[5] = do1
        if qq < 3:
            t1 = (qq + 1) * 512
            dhs = C.T(lambda e, t1=t1: e.matmul(R.pss[:, 508:509], R.onerf[0:1, :],
                                                R.r2[:, t1:t1 + 1], start=True, stop=True),
                      [do1r] + ([prev_pm.get(7)] if prev_pm.get(7) else []))
            dhsv = C.V(lambda e: e.tensor_copy(hrsc, R.pss[:, 508:509]), [dhs])
            prev_pm[7] = dhsv
            prev_hrsc = dhsv

        conv_done = []
        for mg in range(8):
            wb = wt[mg % 2]
            wdeps = [C.dma("sync", wb.rearrange("p (kc c) -> p kc c", kc=8),
                           dr["l0f_w1q"].rearrange("(kc p) m -> p kc m", p=P)
                           [:, :, mg * 512:(mg + 1) * 512],
                           [fuser.get(mg % 2)] if fuser.get(mg % 2) else bar)]
            for sub in range(4):
                mc = mg * 4 + sub
                pbi = (mg * 4 + sub) % 4
                pbank = R.pm[pbi]

                def mk1(wb=wb, sub=sub, sl=sl, pbank=pbank):
                    def body(e):
                        last = None
                        for kc in range(8):
                            last = e.matmul(pbank[:, 0:512],
                                            wb[:, kc * 512 + sub * P: kc * 512 + (sub + 1) * P],
                                            xq[kc][:, sl], start=(kc == 0), stop=(kc == 7))
                        return last
                    return body
                dmm = C.T(mk1(), wdeps + q_deps + ([prev_pm.get(pbi)] if prev_pm.get(pbi) else []))
                if sub == 3:
                    prev_wdma = dmm
                    fuser[mg % 2] = dmm
                dhm = None
                if qq < 3:
                    def mkh(wb=wb, sub=sub, t1=(qq + 1) * 512, mc=mc):
                        def body(e):
                            last = None
                            for kc in range(8):
                                last = e.matmul(R.pm[4][:, mc:mc + 1],
                                                wb[:, kc * 512 + sub * P: kc * 512 + (sub + 1) * P],
                                                xq[kc][:, t1:t1 + 1], start=(kc == 0), stop=(kc == 7))
                            return last
                        return body
                    dhm = C.T(mkh(), ([prev_pm.get(4)] if prev_pm.get(4) else []))

                hb = hraw[mc % 2]
                dev = C.V(lambda e, pbank=pbank, hb=hb: e.tensor_tensor(hb[:, 1:513], pbank[:, 0:512],
                                                                        o1b, ALU.mult),
                          [dmm, do1] + ([prev_conv.get(mc % 2)] if prev_conv.get(mc % 2) else []) + [dhl])
                prev_pm[pbi] = dev

                def mkhal(hb=hb, mc=mc, qq=qq):
                    def body(e):
                        e.tensor_copy(hb[:, 0:1], hleft[:, mc:mc + 1])
                        if qq < 3:
                            return e.tensor_scalar(hb[:, 513:514], R.pm[4][:, mc:mc + 1],
                                                   hrsc, None, ALU.mult)
                        return e.memset(hb[:, 513:514], 0.0)
                    return body
                hal_deps = [dev] + ([dhm] if dhm else []) + ([prev_hrsc] if qq < 3 else [])
                dhal = C.V(mkhal(), hal_deps)
                if dhm is not None:
                    prev_pm[4] = dhal
                dlh = C.V(lambda e, hb=hb, mc=mc: e.tensor_copy(hleft[:, mc:mc + 1], hb[:, 512:513]),
                          [dhal])

                cbuf = R.bb if mc % 2 == 0 else R.ab
                c1 = C.G(lambda e, hb=hb, mc=mc, cbuf=cbuf: e.tensor_tensor(
                    cbuf[:], hb[:, 0:512], cw[:, mc:mc + 1].broadcast_to([P, 512]), ALU.mult),
                    [dhal, dlh, dcw] + ([prev_cbuf.get(mc % 2)] if prev_cbuf.get(mc % 2) else [])
                    + ([prev_dev2] if mc <= 1 and prev_dev2 else []))

                def mkc23(hb=hb, mc=mc, cbuf=cbuf):
                    def body(e):
                        e.scalar_tensor_tensor(cbuf[:], hb[:, 1:513],
                                               cw[:, 32 + mc:33 + mc], cbuf[:],
                                               ALU.mult, ALU.add)
                        return e.scalar_tensor_tensor(cbuf[:], hb[:, 2:514],
                                                      cw[:, 64 + mc:65 + mc], cbuf[:],
                                                      ALU.mult, ALU.add)
                    return body
                c3 = C.V(mkc23(), [c1])
                c4 = C.A(lambda e, mc=mc, cbuf=cbuf: e.activation(hs[mc][:], cbuf[:], AF.Silu,
                                                                  bias=cb[:, mc:mc + 1], scale=1.0),
                         [c3, dcb] + ([prev_w2_last] if prev_w2_last else []))
                prev_cbuf[mc % 2] = c4
                prev_conv[mc % 2] = c4
                conv_done.append(c4)

        if "dbg_swr" in dr and qq == 0:
            C.dma("sync", dr["dbg_hs"], hs[0][:], conv_done)
        hs3 = hsr.rearrange("p (ch q) -> p q ch", ch=NCH)
        dt2 = C.V(lambda e: e.tensor_reduce(R.sq[1][:], hs3, AX.X, ALU.max,
                                            apply_absolute_value=True), conv_done)
        t2deps = []
        prev_t2 = prev_pm.get(7)
        for tt in range(4):
            dtp = C.T(lambda e, tt=tt: e.transpose(R.ptr[:, tt * P:(tt + 1) * P],
                                                   R.sq[1][:, tt * P:(tt + 1) * P], R.idb[:]),
                      [dt2] + ([prev_t2] if prev_t2 else []) + ([prev_pm.get(6)] if prev_pm.get(6) else []))
            dtv = C.V(lambda e, tt=tt: e.tensor_reduce(amxc[:, tt:tt + 1],
                                                       R.ptr[:, tt * P:(tt + 1) * P], AX.X, ALU.max),
                      [dtp])
            prev_t2 = dtv
            t2deps.append(dtv)
        prev_pm[7] = prev_t2
        prev_pm[6] = prev_t2
        dft2 = C.T(lambda e: e.transpose(R.pm[4][0:4, 0:P], amxc[:, 0:4], R.idf[:]),
                   t2deps + ([prev_pm.get(44)] if prev_pm.get(44) else []))
        dfv2 = C.V(lambda e: e.tensor_copy(R.ab[0:4, 0:P], R.pm[4][0:4, 0:P]), [dft2])
        prev_pm[44] = dfv2
        prev_pm[4] = dfv2
        dfi2 = C.dma("sync", R.r0[:, sl], R.ab[0:4, 0:P], [dfv2])
        dc2 = C.V(lambda e, sl=sl: e.tensor_single_scalar(R.r0[:, sl], R.r0[:, sl], 1e-5, ALU.max), [dfi2])
        ds2 = C.V(lambda e, sl=sl: e.reciprocal(R.r1[:, sl], R.r0[:, sl]), [dc2])
        ds2b = C.V(lambda e, sl=sl: e.tensor_single_scalar(R.r1[:, sl], R.r1[:, sl], 127.0, ALU.mult), [ds2])
        do2r = C.V(lambda e, sl=sl: e.tensor_scalar(R.r0[:, sl], R.r0[:, sl], swr[0:1, 1:2], 1.0 / 127.0,
                                                    ALU.mult, ALU.mult), [ds2b] + list(pre_deps))
        db2 = bcast_row(C, R, s2b, R.r1[:, sl], 512,
                        [ds2b] + ([prev_pm.get(5)] if prev_pm.get(5) else []), R.pm[5])
        do2 = bcast_row(C, R, o2b, R.r0[:, sl], 512, [do2r, db2], R.pm[5])
        prev_pm[5] = do2

        if "dbg_swr" in dr and qq == 0:
            C.dma("sync", dr["dbg_rows"][2:3, 0:512], R.r1[:, sl], [ds2b])
        hq_deps = []
        hs3b = hsr.rearrange("p (ch q) -> p ch q", ch=NCH)
        big3 = R.big.rearrange("p (ch q) -> p ch q", ch=4)
        dprev = [db2] + ([prev_hq] if prev_hq else [])
        prev_rnd = None
        for blk4 in range(8):
            dmul = C.G(lambda e, blk4=blk4: e.tensor_tensor(
                big3, hs3b[:, blk4 * 4:(blk4 + 1) * 4, :],
                s2b[:, None, 0:512].broadcast_to([P, 4, 512]), ALU.mult),
                dprev if blk4 == 0 else [prev_rnd])
            prev_rnd = C.V(lambda e, blk4=blk4: e.tensor_scalar(
                hs3b[:, blk4 * 4:(blk4 + 1) * 4, :], big3,
                MAGIC, MAGIC, ALU.add, ALU.subtract), [dmul])
            hq_deps.append(prev_rnd)
        prev_hq = prev_rnd

        for mcg in range(2):
            for kb in range(4):
                wb = wt[kb % 2]
                wdeps2 = [C.dma("sync", wb.rearrange("p (kc c) -> p kc c", kc=8),
                                dr["l0f_w2q"].rearrange("(kc p) m -> p kc m", p=P)
                                [:, kb * 8:(kb + 1) * 8, mcg * 512:(mcg + 1) * 512],
                                [fuser.get(kb % 2)] if fuser.get(kb % 2) else [prev_wdma])]
                for sub in range(4):
                    mc = mcg * 4 + sub
                    pbi = sub
                    pbank = R.pm[pbi]

                    def mk2(wb=wb, sub=sub, sl=sl, pbank=pbank, kb=kb):
                        def body(e):
                            last = None
                            for j in range(8):
                                kc = kb * 8 + j
                                last = e.matmul(pbank[:, 0:512],
                                                wb[:, j * 512 + sub * P: j * 512 + (sub + 1) * P],
                                                hs[kc][:, 0:512],
                                                start=(kb == 0 and j == 0), stop=(kb == 3 and j == 7))
                            return last
                        return body
                    mm2deps = wdeps2 + hq_deps[2 * kb:2 * kb + 2]
                    if kb == 0 and prev_pm.get(pbi):
                        mm2deps.append(prev_pm[pbi])
                    dmm2 = C.T(mk2(), mm2deps)
                    prev_w2_last = dmm2
                    if sub == 3:
                        prev_wdma = dmm2
                        fuser[kb % 2] = dmm2
                    if kb == 3:
                        def mkev2(pbank=pbank, mc=mc, sl=sl):
                            def body(e):
                                e.tensor_tensor(R.ab[:], pbank[:, 0:512], o2b, ALU.mult)
                                return e.tensor_tensor(X[mc][:, sl], X[mc][:, sl], R.ab[:], ALU.add)
                            return body
                        dev2 = C.V(mkev2(), [dmm2, do2])
                        prev_pm[pbi] = dev2
                        prev_dev2 = dev2
                        ev_deps.append(dev2)

    if "dbg_swr" in dr:
        dxx = [C.dma("sync", dr["dbg_x"][cc * P:(cc + 1) * P, :], X[cc][:], ev_deps) for cc in range(8)]
        ev_deps = ev_deps + dxx
    return ln_fm(C, R, X, X, 1e-5, ev_deps, sqw=xqr[:, 0:4096])


def l0ffn_host(inp):
    w1 = np.asarray(inp["L0_ffn_w1"], np.float32)
    w2 = np.asarray(inp["L0_ffn_w2"], np.float32)
    cwv = np.asarray(inp["L0_ffn_conv_w"], np.float32)
    cbv = np.asarray(inp["L0_ffn_conv_b"], np.float32)
    cw = np.zeros((P, 96), np.float32)
    for k in range(3):
        cw[:, k * 32:(k + 1) * 32] = cwv[:, 0, k].reshape(32, 128).T
    cb = cbv.reshape(32, 128).T
    return {
        "l0f_w1t": np.ascontiguousarray(w1.T),
        "l0f_w2t": np.ascontiguousarray(w2.T),
        "l0f_convw": np.ascontiguousarray(cw),
        "l0f_convb": np.ascontiguousarray(cb),
    }


def l0ffn_decls():
    return [("l0f_w1t", [D, 4096], F32), ("l0f_w2t", [4096, D], F32),
            ("l0f_convw", [P, 96], F32), ("l0f_convb", [P, 32], F32)]





NH = 32
NMC = 35
EMIN_CLAMP = 30.0


def emit_l0attn(C, R, X, dr):
    import os
    PH = int(os.environ.get("KPHASE", "99"))
    KCH = int(os.environ.get("KCH", "99"))
    bar = C.all_done()
    R.pbf.reset()
    R.pf.reset(keep=64)

    u_r = R.pbf.take(4096)
    u = [u_r[:, c * 512:(c + 1) * 512] for c in range(8)]
    tmpb = u
    exps_pair2 = u_r[0:2, 0:2048]
    Mb_r = u_r[:, 2048:3072]
    sz_r = R.pbf.take(16 * 512)
    sz = [sz_r[:, c * 512:(c + 1) * 512] for c in range(16)]
    xbcs_r = R.pbf.take(18 * 512)
    xbcs = [xbcs_r[:, c * 512:(c + 1) * 512] for c in range(18)]
    xstm_r = R.pbf.take(2048)
    exps_hm_bf = xstm_r[0:NH, 0:128]
    U_r = R.pbf.take(2048)
    M_r = R.pbf.take(1024)
    Lp_r = R.pbf.take(1024)
    G_r = R.pbf.take(128)
    Btm_r = R.pbf.take(128)
    hstb_r = R.pbf.take(2048)
    Y2_r = R.pbf.take(16 * 512)
    Y2 = [Y2_r[:, c * 512:(c + 1) * 512] for c in range(16)]
    wst_r = R.pbf.take(4096)
    wst = [wst_r[:, i * 2048:(i + 1) * 2048] for i in range(2)]
    EXPSP = wst_r[:, 0:2048]

    dtq = R.pf.take(512)[0:NH, :]
    hstate = R.pf.take(2048)
    raw_r = R.pf.take(1036)
    hraw = [raw_r[:, i * 518:i * 518 + 515] for i in range(2)]
    Ebuf = raw_r[:, 0:1024]
    fcm = R.pf.take(256)
    cwc = fcm[:, 0:72]
    cbc = fcm[:, 72:90]
    xleft = fcm[:, 90:144]
    nwc = fcm[:, 144:152]
    pwc = fcm[:, 152:160]
    dcol = fcm[:, 160:176]
    acol = fcm[0:NH, 176:177]
    dtbc = fcm[0:NH, 177:178]
    expq = fcm[:, 192:224]
    wsmc = fcm[:, 224:240]
    sm_r = R.pf.take(768)
    s_hm = sm_r[0:NH, 0:128]
    dtA = sm_r[0:NH, 128:256]
    lndt = sm_r[0:NH, 256:384]
    nslndt = sm_r[0:NH, 384:512]
    w_hm = sm_r[0:NH, 512:640]
    dtw_hm = sm_r[0:NH, 640:768]
    tm_r = R.pf.take(64)
    nslndt_tm = tm_r[:, 0:32]
    dtw_tm = tm_r[:, 32:64]
    onesrow = R.pf.take(128)[0:NH, :]
    s_flat = R.rows[0:1, 2048:2048 + 4096]

    dcw = C.dma("sync", cwc, dr["l0a_convw"], bar)
    dcb2 = C.dma("sync", cbc, dr["l0a_convb"], bar)
    dnw = C.dma("sync", nwc, dr["l0a_normw"], bar)
    dpw = C.dma("sync", pwc, dr["l0a_postw"], bar)
    dDc = C.dma("sync", dcol, dr["l0a_dcol"], bar)
    dAc = C.dma("sync", fcm[0:NH, 176:178], dr["l0a_adt"], bar)
    dws = C.dma("sync", wsmc, dr["l0a_wssm"], bar)
    dsel = C.dma("sync", R.selp[:], dr["c_selp"], bar)
    dmk01 = C.dma("sync", R.mask01[:], dr["c_mask01"], bar)
    dini = C.V(lambda e: (e.memset(xleft, 0.0), e.memset(onesrow, 1.0),
                          e.memset(hstate, 0.0), e.memset(hstb_r, 0.0))[-1], bar)
    dAe = C.A(lambda e: e.activation(acol, acol, AF.Exp, bias=0.0, scale=1.0), [dAc])
    dAn = C.V(lambda e: e.tensor_single_scalar(acol, acol, -1.0, ALU.mult), [dAe])

    u2_deps = []
    PIPE = {}
    wst_store = {}
    wst_sdep = {}
    wo_store = {}
    wo_sdep = {}
    pbar = bar + [dcw, dcb2, dnw, dpw, dDc, dAn, dws, dsel, dmk01, dini]
    for qq in range(4):
        sl = slice(qq * 512, (qq + 1) * 512)

        X3m = R.X_r.rearrange("p (cc t) -> p cc t", cc=8)
        sq3 = Y2_r[:, 0:4096].rearrange("p (cc t) -> p cc t", cc=8)
        st3 = Y2_r[:, 4096:8192].rearrange("p (cc t) -> p cc t", cc=8)
        dqw = C.A(lambda e, sl=sl: e.activation(sq3, X3m[:, :, sl],
                                                AF.Square, bias=0.0, scale=1.0), pbar)

        def mksqmm():
            def body(e):
                last = None
                for cc in range(8):
                    last = e.matmul(R.pss[0:1, 0:512], R.m10b[:, 0:1], Y2[cc][:],
                                    start=(cc == 0), stop=(cc == 7))
                return last
            return body
        prevmm = C.T(mksqmm(), [dqw] + pbar)
        d2 = C.A(lambda e: e.activation(R.r0[:, 0:512], R.pss[0:1, 0:512], AF.Sqrt,
                                        bias=R.eps56[0:1, 1:2], scale=1.0), [prevmm])
        d3 = C.V(lambda e: e.reciprocal(R.r0[:, 0:512], R.r0[:, 0:512]), [d2])
        db = C.T(lambda e: e.matmul(R.pm[4][:, 0:512], R.onerf[0:1, :], R.r0[0:1, 0:512],
                                    start=True, stop=True), [d3] + pbar)
        dcpa = C.V(lambda e: e.tensor_copy(R.ab[:], R.pm[4][:, 0:512]), [db])
        u3 = u_r.rearrange("p (cc t) -> p cc t", cc=8)
        du1 = C.G(lambda e, sl=sl: e.tensor_tensor(
            st3, X3m[:, :, sl],
            R.ab[:, None, 0:512].broadcast_to([P, 8, 512]), ALU.mult), [dcpa])
        du2w = C.V(lambda e: e.tensor_tensor(
            u3, st3,
            nwc[:, :, None].broadcast_to([P, 8, 512]), ALU.mult), [du1])
        u_deps = [du2w]
        ubar = C.all_done()

        conv_ct = 0
        prev_wd = None
        prev_pmb = {}
        prev_cv = {}
        wuser = {}
        for mcg in range(18):
            wb = wst[mcg % 2]
            mclist = [m for m in (2 * mcg, 2 * mcg + 1) if m < NMC]
            ncols = 256 if len(mclist) == 2 else 32
            ldeps = [wuser.get(mcg % 2)] if wuser.get(mcg % 2) else list(ubar)
            if qq == 0:
                wdeps = [C.dma("gpsimd",
                               wb.rearrange("p (kc c) -> p kc c", kc=8)[:, :, 0:ncols],
                               dr["l0a_inproj_t"].rearrange("(kc p) m -> p kc m", p=P)
                               [:, :, 2 * mcg * P: 2 * mcg * P + ncols],
                               ldeps + ([wst_sdep[mcg % 2]] if wst_sdep.get(mcg % 2) else []))]
                wsst = C.dma("sync",
                             dr["l0a_inproj_bf"].rearrange("(kc p) m -> p kc m", p=P)
                             [:, :, 2 * mcg * P: 2 * mcg * P + ncols],
                             wb.rearrange("p (kc c) -> p kc c", kc=8)[:, :, 0:ncols],
                             wdeps)
                wst_store[mcg] = wsst
                wst_sdep[mcg % 2] = wsst
            else:
                wdeps = [C.dma("sync",
                               wb.rearrange("p (kc c) -> p kc c", kc=8)[:, :, 0:ncols],
                               dr["l0a_inproj_bf"].rearrange("(kc p) m -> p kc m", p=P)
                               [:, :, 2 * mcg * P: 2 * mcg * P + ncols],
                               ldeps + [wst_store[mcg]]
                               + ([wst_sdep[mcg % 2]] if wst_sdep.get(mcg % 2) else []))]
            for mi, mc in enumerate(mclist):
                mwid = 32 if mc == 34 else 128
                pbi = (2 * mcg + mi) % 4
                pbank = R.pm[pbi]

                def mkmm(wb=wb, mi=mi, pbank=pbank, mwid=mwid):
                    def body(e):
                        last = None
                        for kc in range(8):
                            last = e.matmul(pbank[0:mwid, 0:512],
                                            wb[:, kc * 256 + mi * P: kc * 256 + mi * P + mwid],
                                            u[kc][:, :], start=(kc == 0), stop=(kc == 7))
                        return last
                    return body
                dmm = C.T(mkmm(), wdeps + ubar + ([prev_pmb.get(pbi)] if prev_pmb.get(pbi) else []))
                prev_wd = dmm
                wuser[mcg % 2] = dmm

                if mc < 16:
                    dev = C.A(lambda e, mc=mc, pbank=pbank:
                              e.activation(sz[mc][:], pbank[0:P, 0:512], AF.Silu, bias=0.0, scale=1.0),
                              [dmm])
                    prev_pmb[pbi] = dev
                elif mc < 34:
                    xc = mc - 16
                    hb = hraw[conv_ct % 2]
                    hkey = conv_ct % 2
                    conv_ct += 1
                    dev = C.V(lambda e, pbank=pbank, hb=hb:
                              e.tensor_copy(hb[:, 3:515], pbank[0:P, 0:512]),
                              [dmm] + ([prev_cv.get(hkey)] if prev_cv.get(hkey) else []))
                    prev_pmb[pbi] = dev
                    dhal = C.V(lambda e, hb=hb, xc=xc: e.tensor_copy(hb[:, 0:3], xleft[:, xc * 3:xc * 3 + 3]),
                               [dev])
                    dlh = C.V(lambda e, hb=hb, xc=xc: e.tensor_copy(xleft[:, xc * 3:xc * 3 + 3], hb[:, 512:515]),
                              [dhal])
                    a1 = C.A(lambda e, hb=hb, xc=xc: e.activation(R.ab[:], hb[:, 0:512], AF.Identity,
                                                                  bias=0.0, scale=cwc[:, xc:xc + 1]),
                             [dhal, dlh])
                    a2 = C.A(lambda e, hb=hb, xc=xc: e.activation(R.bb[:], hb[:, 1:513], AF.Identity,
                                                                  bias=0.0, scale=cwc[:, 18 + xc:19 + xc]),
                             [a1])
                    s1 = C.G(lambda e: e.tensor_tensor(R.ab[:], R.ab[:], R.bb[:], ALU.add), [a2])
                    v1 = C.V(lambda e, hb=hb, xc=xc: e.scalar_tensor_tensor(R.ab[:], hb[:, 2:514],
                                                                            cwc[:, 36 + xc:37 + xc],
                                                                            R.ab[:], ALU.mult, ALU.add),
                             [s1])
                    v2 = C.V(lambda e, hb=hb, xc=xc: e.scalar_tensor_tensor(R.ab[:], hb[:, 3:515],
                                                                            cwc[:, 54 + xc:55 + xc],
                                                                            R.ab[:], ALU.mult, ALU.add),
                             [v1])
                    c4 = C.A(lambda e, xc=xc: e.activation(xbcs[xc][:], R.ab[:], AF.Silu,
                                                           bias=cbc[:, xc:xc + 1], scale=1.0), [v2])
                    prev_cv[hkey] = c4
                else:
                    dev = C.V(lambda e, pbank=pbank: e.tensor_scalar(dtq[:, :], pbank[0:NH, 0:512],
                                                                     dtbc, None, ALU.add),
                              [dmm])
                    prev_pmb[pbi] = dev
                    sp1 = C.A(lambda e: e.activation(R.ab[0:NH, :], dtq, AF.Abs, bias=0.0, scale=1.0), [dev])
                    sp2 = C.A(lambda e: e.activation(R.ab[0:NH, :], R.ab[0:NH, :], AF.Exp, bias=0.0,
                                                     scale=-1.0), [sp1])
                    sp3 = C.A(lambda e: e.activation(R.ab[0:NH, :], R.ab[0:NH, :], AF.Ln, bias=1.0,
                                                     scale=1.0), [sp2])
                    sp4 = C.V(lambda e: e.tensor_single_scalar(R.bb[0:NH, :], dtq, 0.0, ALU.max), [sp3])
                    C.V(lambda e: e.tensor_tensor(dtq, R.bb[0:NH, :], R.ab[0:NH, :], ALU.add), [sp4])
        bbar = C.all_done()
        if "dbg_q" in dr and qq == 0:
            C.dma("sync", dr["dbg_q"][0:P, 0:512], sz[0], bbar)
            C.dma("sync", dr["dbg_q"][0:P, 512:1024], xbcs[0][:], bbar)
            C.dma("sync", dr["dbg_q"][0:P, 1024:1536], xbcs[17][:], bbar)
            C.dma("gpsimd", dr["dbg_q"][0:NH, 1536:2048], dtq, bbar)

        for c in (range(4) if PH >= 3 else []):
            csl = slice(c * P, (c + 1) * P)
            pp = PIPE
            m1 = C.A(lambda e, csl=csl: e.activation(dtA, dtq[:, csl], AF.Identity, bias=0.0, scale=acol),
                     bbar + [pp.get("m2")])
            m2 = C.V(lambda e: e.tensor_tensor_scan(s_hm, onesrow, dtA, 0.0, ALU.mult, ALU.add),
                     [m1, pp.get("m4"), pp.get("m5"), pp.get("m7"), pp.get("f1"), pp.get("q0")])
            m3 = C.A(lambda e, csl=csl: e.activation(lndt, dtq[:, csl], AF.Ln, bias=0.0, scale=1.0),
                     bbar + [pp.get("m4")])
            m4 = C.V(lambda e: e.tensor_tensor(nslndt, lndt, s_hm, ALU.subtract), [m2, m3, pp.get("t1")])
            m5 = C.A(lambda e: e.activation(w_hm, s_hm, AF.Exp, bias=s_hm[:, 127:128], scale=-1.0),
                     [m2, pp.get("m6")])
            m6 = C.V(lambda e, csl=csl: e.tensor_tensor(dtw_hm, dtq[:, csl], w_hm, ALU.mult),
                     [m5, pp.get("t2")])
            m7 = C.A(lambda e: e.activation(exps_hm_bf, s_hm, AF.Exp, bias=0.0, scale=1.0),
                     [m2, pp.get("f3"), pp.get("din")])
            t1 = C.T(lambda e: e.transpose(R.pss[:, 256:288], nslndt, R.idf[0:NH, 0:NH]),
                     [m4, pp.get("t1v"), pp.get("xvo")])
            t1v = C.V(lambda e: e.tensor_copy(nslndt_tm, R.pss[:, 256:288]), [t1, pp.get("ee")])
            t2 = C.T(lambda e: e.transpose(R.pss[:, 288:320], dtw_hm, R.idf[0:NH, 0:NH]), [m6, t1v])
            t2v = C.V(lambda e: e.tensor_copy(dtw_tm, R.pss[:, 288:320]), [t2, pp.get("uU")])
            f2 = C.dma("sync", s_flat, s_hm, [m2, pp.get("sr")])
            f1 = f2
            f3 = C.dma("sync", dr["l0a_scr2"], exps_hm_bf, [m7, pp.get("f4")])
            f4 = C.dma("sync", exps_pair2.rearrange("p (hh t) -> p hh t", hh=16),
                       dr["l0a_scr2"].rearrange("(hh two) t -> two hh t", two=2),
                       [f3, pp.get("e4")])
            q0 = C.T(lambda e: e.transpose(R.pss[0:1, 352:384], s_hm[:, 127:128],
                                           R.idf[0:NH, 0:NH]), [m2, t2v])
            q0v = C.V(lambda e: e.tensor_copy(R.r0[0:1, 512:544], R.pss[0:1, 352:384]), [q0])
            q1 = C.T(lambda e: e.matmul(R.pss[:, 320:352], R.onerf[0:1, :],
                                        R.r0[0:1, 512:544], start=True, stop=True),
                     [q0v, pp.get("hu1"), pp.get("q2")])
            q2 = C.A(lambda e: e.activation(expq, R.pss[:, 320:352], AF.Exp, bias=0.0, scale=1.0), [q1])
            if KCH < 2:
                continue
            g1 = C.T(lambda e, csl=csl: e.matmul(R.pss[0:P, 0:128], xbcs[16][:, csl], xbcs[17][:, csl],
                                                 start=True, stop=True), [q2] + bbar)
            g2 = C.A(lambda e: e.activation(G_r[:], R.pss[0:P, 0:128], AF.Identity, bias=0.0, scale=1.0),
                     [g1, pp.get("mm")])
            g3 = C.V(lambda e: e.tensor_tensor(G_r[:], G_r[:], R.mask01[:], ALU.mult), [g2])
            b1 = C.T(lambda e, csl=csl: e.transpose(R.ptr[0:P, 0:128], xbcs[16][:, csl], R.idb[:]),
                     [g1, pp.get("dz")])
            b2 = C.A(lambda e: e.activation(Btm_r[:], R.ptr[0:P, 0:128], AF.Identity,
                                            bias=0.0, scale=1.0), [b1])
            xst = []
            first_e = [f3, b2, pp.get("din"), pp.get("dz")]
            first_o = [b2, g2, pp.get("din")]
            prev_e = None
            prev_o = None
            xt0 = None
            for t16 in range(16):
                if t16 % 2 == 0:
                    xt = C.T(lambda e, t16=t16, csl=csl:
                             e.transpose(R.ptr[0:P, 128:256], xbcs[t16][:, csl], R.idb[:]),
                             ([prev_e] if prev_e else first_e))
                    xv = C.A(lambda e, t16=t16: e.activation(
                        xstm_r[:, t16 * 128:(t16 + 1) * 128], R.ptr[0:P, 128:256],
                        AF.Identity, bias=0.0, scale=1.0), [xt])
                    prev_e = xv
                else:
                    xt = C.T(lambda e, t16=t16, csl=csl:
                             e.matmul(R.pss[:, 0:128], xbcs[t16][:, csl], R.idb[:],
                                      start=True, stop=True),
                             ([prev_o] if prev_o else first_o))
                    xv = C.V(lambda e, t16=t16: e.tensor_copy(
                        xstm_r[:, t16 * 128:(t16 + 1) * 128], R.pss[:, 0:128]), [xt])
                    prev_o = xv
                if t16 == 0:
                    xt0 = xt
                xst.append(xv)
            uU = C.V(lambda e: e.tensor_tensor(
                U_r.rearrange("p (h x) -> p h x", h=NH),
                xstm_r.rearrange("p (h x) -> p h x", h=NH),
                dtw_tm[:, :, None].broadcast_to([P, NH, 64]), ALU.mult),
                xst + [t2v, pp.get("dz")])
            e1 = C.T(lambda e: e.matmul(R.pm[2][:, 0:512], R.selp[:], exps_pair2[:, 0:512],
                                        start=True, stop=True), [f4, g1, pp.get("hu2A")])
            e2 = C.T(lambda e: e.matmul(R.pm[3][:, 0:512], R.selp[:], exps_pair2[:, 512:1024],
                                        start=True, stop=True), [f4, pp.get("hu2B")])
            e1v = C.A(lambda e: e.activation(EXPSP[:, 0:512], R.pm[2][:, 0:512], AF.Identity,
                                             bias=0.0, scale=1.0), [e1, pp.get("o1")])
            e2v = C.A(lambda e: e.activation(EXPSP[:, 512:1024], R.pm[3][:, 0:512], AF.Identity,
                                             bias=0.0, scale=1.0), [e2])
            e3 = C.T(lambda e: e.matmul(R.pm[2][:, 0:512], R.selp[:], exps_pair2[:, 1024:1536],
                                        start=True, stop=True), [e1v])
            e4 = C.T(lambda e: e.matmul(R.pm[3][:, 0:512], R.selp[:], exps_pair2[:, 1536:2048],
                                        start=True, stop=True), [e2v])
            e3v = C.A(lambda e: e.activation(EXPSP[:, 1024:1536], R.pm[2][:, 0:512], AF.Identity,
                                             bias=0.0, scale=1.0), [e3])
            e4v = C.A(lambda e: e.activation(EXPSP[:, 1536:2048], R.pm[3][:, 0:512], AF.Identity,
                                             bias=0.0, scale=1.0), [e4])
            cbar = [e4v, e3v, uU, g3]
            PIPE_NEW = {"m2": m2, "m4": m4, "m5": m5, "m6": m6, "m7": m7,
                        "t1": t1, "t1v": t1v, "t2": t2, "f1": f1, "f2": f2,
                        "f3": f3, "f4": f4, "q0": q0, "uU": uU, "e4": e4,
                        "q2": q2, "xt0": xt0, "xvo": prev_o}
            if KCH < 4:
                PIPE = PIPE_NEW
                continue

            prevhup = {0: None, 1: None}
            prevsr = {0: None, 1: None}
            for g in range(4):
                st_ = g % 2
                BS, BI, BN = R.pm[0 + st_], R.pm[2 + st_], R.pm[4 + st_]
                ebase = raw_r[:, 0:1024] if st_ == 0 else R.big[:, 0:1024]
                eb = ebase[:, 0:512]
                eb2 = ebase[:, 512:1024]
                lpb = Lp_r if st_ == 0 else u_r[:, 3072:4096]
                m_t = M_r if st_ == 0 else Mb_r
                ftmp = R.big[:, 1024:1536] if st_ == 0 else R.big[:, 1536:2048]
                scrA = R.ab if st_ == 0 else R.bb
                gd = [prevhup[st_]] if prevhup[st_] else (cbar if g < 2 else [])
                sr1 = C.T(lambda e, g=g, BS=BS: e.matmul(BS[:, 0:512], R.onerf[0:1, :],
                                                         s_flat[0:1, g * 1024:g * 1024 + 512],
                                                         start=True, stop=True), gd + cbar)
                ee1 = C.V(lambda e, g=g, BS=BS, eb=eb: e.tensor_tensor(
                    eb.rearrange("p (h t) -> p h t", h=4),
                    BS[:, 0:512].rearrange("p (h t) -> p h t", h=4),
                    nslndt_tm[:, 8 * g:8 * g + 4, None].broadcast_to([P, 4, P]), ALU.add),
                    [sr1, t1v])
                sr2 = C.T(lambda e, g=g, BS=BS: e.matmul(BS[:, 0:512], R.onerf[0:1, :],
                                                         s_flat[0:1, g * 1024 + 512:(g + 1) * 1024],
                                                         start=True, stop=True), [ee1])
                ee2 = C.V(lambda e, g=g, BS=BS, eb2=eb2: e.tensor_tensor(
                    eb2.rearrange("p (h t) -> p h t", h=4),
                    BS[:, 0:512].rearrange("p (h t) -> p h t", h=4),
                    nslndt_tm[:, 8 * g + 4:8 * g + 8, None].broadcast_to([P, 4, P]), ALU.add),
                    [sr2])
                prevsr[st_] = sr2
                em = C.V(lambda e, ebase=ebase: e.tensor_single_scalar(
                    ebase[:, 0:1024], ebase[:, 0:1024], EMIN_CLAMP, ALU.min), [ee1, ee2])
                lp = C.A(lambda e, ebase=ebase, lpb=lpb: e.activation(
                    lpb[:, 0:1024], ebase[:, 0:1024], AF.Exp, bias=0.0, scale=1.0), [em])
                mm_ = C.G(lambda e, m_t=m_t, lpb=lpb: e.tensor_tensor(
                    m_t.rearrange("p (h t) -> p h t", h=8),
                    lpb.rearrange("p (h t) -> p h t", h=8),
                    G_r[:, None, :].broadcast_to([P, 8, P]), ALU.mult), [lp, g3])

                def mkintra(g=g, BI=BI, m_t=m_t):
                    def body(e):
                        last = None
                        for j in range(8):
                            h = 8 * g + j
                            last = e.matmul(BI[(h % 2) * 64:(h % 2) * 64 + 64,
                                               (j // 2) * 128:(j // 2) * 128 + 128],
                                            xstm_r[:, h * 64:(h + 1) * 64],
                                            m_t[:, j * 128:(j + 1) * 128],
                                            start=True, stop=True)
                        return last
                    return body
                din = C.T(mkintra(), [mm_, uU] + ([prevhup[st_]] if prevhup[st_] else []))

                def mkinter(g=g, BN=BN, csl=csl):
                    def body(e):
                        last = None
                        for jj in range(4):
                            pair = 4 * g + jj
                            last = e.matmul(BN[:, jj * 128:(jj + 1) * 128],
                                            hstb_r[:, pair * 128:(pair + 1) * 128],
                                            xbcs[17][:, csl], start=True, stop=True)
                        return last
                    return body
                dit = C.T(mkinter(), [din, pp.get("hsc")])

                xs3 = xbcs_r.rearrange("p (ch q) -> p ch q", ch=18)
                sz3 = sz_r.rearrange("p (ch q) -> p ch q", ch=16)
                y23 = Y2_r.rearrange("p (ch q) -> p ch q", ch=16)
                o1 = C.V(lambda e, g=g, BN=BN, scrA=scrA: e.tensor_tensor(
                    scrA[:], BN[:], EXPSP[:, g * 512:(g + 1) * 512], ALU.mult),
                    [dit, e4v])
                o2 = C.V(lambda e, BI=BI, scrA=scrA: e.tensor_tensor(scrA[:], scrA[:], BI[:], ALU.add),
                         [o1])
                o3 = C.G(lambda e, g=g, csl=csl, ftmp=ftmp: e.tensor_tensor(
                    ftmp.rearrange("p (ch q) -> p ch q", ch=4),
                    xs3[:, 4 * g:4 * g + 4, csl],
                    dcol[:, 4 * g:4 * g + 4, None].broadcast_to([P, 4, P]), ALU.mult),
                    [o2])
                o4 = C.V(lambda e, scrA=scrA, ftmp=ftmp: e.tensor_tensor(scrA[:], scrA[:],
                                                                         ftmp, ALU.add),
                         [o3])
                o5 = C.G(lambda e, g=g, c=c, csl=csl, scrA=scrA: e.tensor_tensor(
                    y23[:, 4 * g:4 * g + 4, c * P:(c + 1) * P],
                    scrA[:].rearrange("p (ch q) -> p ch q", ch=4),
                    sz3[:, 4 * g:4 * g + 4, csl], ALU.mult), [o4])

                def mkz(g=g, BI=BI):
                    def body(e):
                        last = None
                        for j in range(8):
                            h = 8 * g + j
                            last = e.matmul(BI[:, j * 64:(j + 1) * 64], Btm_r[:],
                                            U_r[:, h * 64:(h + 1) * 64], start=True, stop=True)
                        return last
                    return body
                dz_ = C.T(mkz(), [o2, o4, b2])
                hu1 = C.G(lambda e, g=g, ftmp=ftmp: e.tensor_tensor(
                    ftmp.rearrange("p (h x) -> p h x", h=8),
                    hstate[:, g * 512:(g + 1) * 512].rearrange("p (h x) -> p h x", h=8),
                    expq[:, 8 * g:8 * g + 8, None].broadcast_to([P, 8, 64]), ALU.mult),
                    [o5, o4, q2])
                hu2 = C.V(lambda e, g=g, BI=BI, ftmp=ftmp: e.tensor_tensor(
                    hstate[:, g * 512:(g + 1) * 512], ftmp, BI[:], ALU.add),
                    [hu1, dz_])
                prevhup[st_] = hu2
                PIPE_NEW["sr"] = sr2
                PIPE_NEW["ee"] = ee2
                PIPE_NEW["mm"] = mm_
                PIPE_NEW["din"] = din
                PIPE_NEW["dz"] = dz_
                PIPE_NEW["o1"] = o1
                PIPE_NEW["hu1"] = hu1
                PIPE_NEW["hu2A" if st_ == 0 else "hu2B"] = hu2
            if KCH >= 6:
                PIPE_NEW["hsc"] = C.V(lambda e: e.tensor_copy(hstb_r[:], hstate[:]),
                                      [prevhup[0], prevhup[1]])
            PIPE = PIPE_NEW

        if PH < 4:
            pbar = C.all_done()
            u2_deps.append(C.dma("sync", dr["l0a_u2"][0:P, sl], tmpb[0][:, :], pbar))
            continue
        gbar = C.all_done()
        gslots = [u_r[:, i * 512:(i + 1) * 512] for i in range(8)]
        dqs2 = []
        for ccc in range(8):
            dqs2.append(C.A(lambda e, ccc=ccc: e.activation(gslots[ccc][:], Y2[ccc][:],
                                                            AF.Square, bias=0.0, scale=1.0),
                            gbar if ccc == 0 else [dqs2[-1]]))

        def mkg2a():
            def body(e):
                last = None
                for j in range(8):
                    last = e.matmul(R.pss[0:1, 0:512], R.m11b[:, 0:1], gslots[j][:],
                                    start=(j == 0), stop=False)
                return last
            return body
        gmm1 = C.T(mkg2a(), [dqs2[-1]] + gbar)
        dqs3 = []
        for ccc in range(8, 16):
            dqs3.append(C.A(lambda e, ccc=ccc: e.activation(gslots[ccc - 8][:], Y2[ccc][:],
                                                            AF.Square, bias=0.0, scale=1.0),
                            [gmm1] if ccc == 8 else [dqs3[-1]]))

        def mkg2b():
            def body(e):
                last = None
                for j in range(8):
                    last = e.matmul(R.pss[0:1, 0:512], R.m11b[:, 0:1], gslots[j][:],
                                    start=False, stop=(j == 7))
                return last
            return body
        gmm2 = C.T(mkg2b(), [dqs3[-1]])
        gs3 = C.A(lambda e: e.activation(R.r0[:, 0:512], R.pss[0:1, 0:512], AF.Sqrt,
                                         bias=R.eps56[0:1, 1:2], scale=1.0), [gmm2])
        gs4 = C.V(lambda e: e.reciprocal(R.r0[:, 0:512], R.r0[:, 0:512]), [gs3])
        gb = C.T(lambda e: e.matmul(R.pm[4][:, 0:512], R.onerf[0:1, :], R.r0[0:1, 0:512],
                                    start=True, stop=True), [gs4] + gbar)
        y3d = []
        for ccc in range(16):
            y3d.append(C.V(lambda e, ccc=ccc: e.tensor_tensor(Y2[ccc][:], Y2[ccc][:],
                                                              R.pm[4][:, 0:512], ALU.mult),
                           [gb]))

        if PH < 5:
            pbar = C.all_done()
            continue
        prev_ev_o = None
        wouser = {}
        for mcg in range(2):
            for kc in range(16):
                wo = wst[kc % 2][:, 0:512]
                wdep0 = ([wouser.get(kc % 2)] if wouser.get(kc % 2) else gbar) + \
                    ([wo_sdep[kc % 2]] if wo_sdep.get(kc % 2) else [])
                if qq == 0:
                    dwo = C.dma("gpsimd", wo, dr["l0a_outproj_t"][kc * P:(kc + 1) * P,
                                                                 mcg * 512:(mcg + 1) * 512],
                                wdep0)
                    dsc_ = C.V(lambda e, wo=wo, kc=kc: e.tensor_scalar(wo, wo, wsmc[:, kc:kc + 1], None,
                                                                       ALU.mult), [dwo])
                    wost = C.dma("sync", dr["l0a_outproj_bf"][kc * P:(kc + 1) * P,
                                                              mcg * 512:(mcg + 1) * 512],
                                 wo, [dsc_])
                    wo_store[(mcg, kc)] = wost
                    wo_sdep[kc % 2] = wost
                else:
                    dsc_ = C.dma("sync", wo, dr["l0a_outproj_bf"][kc * P:(kc + 1) * P,
                                                                  mcg * 512:(mcg + 1) * 512],
                                 wdep0 + [wo_store[(mcg, kc)]])
                def mko(wo=wo, kc=kc):
                    def body(e):
                        last = None
                        for sub in range(4):
                            last = e.matmul(R.pm[sub][:, 0:512], wo[:, sub * 128:(sub + 1) * 128],
                                            Y2[kc][:], start=(kc == 0), stop=(kc == 15))
                        return last
                    return body
                dmo = C.T(mko(), [dsc_] + y3d + ([prev_ev_o] if kc == 0 and prev_ev_o else []))
                wouser[kc % 2] = dmo
            for sub in range(4):
                mc = mcg * 4 + sub
                ev1 = C.V(lambda e, sub=sub, mc=mc, sl=sl: e.tensor_tensor(
                    tmpb[mc][:, :], R.pm[sub][:, 0:512], X[mc][:, sl], ALU.add), [dmo])
                ev2 = C.V(lambda e, sub=sub, mc=mc, sl=sl: e.scalar_tensor_tensor(
                    X[mc][:, sl], X[mc][:, sl], 2.0, R.pm[sub][:, 0:512], ALU.mult, ALU.add), [ev1])
                prev_ev_o = ev2
        obar = C.all_done()
        usl = [Y2_r[:, i * 512:(i + 1) * 512] for i in range(8)]
        dqs4 = []
        for cc in range(8):
            dqs4.append(C.A(lambda e, cc=cc: e.activation(usl[cc][:], tmpb[cc][:, :],
                                                          AF.Square, bias=0.0, scale=1.0),
                            obar if cc == 0 else [dqs4[-1]]))

        def mkum():
            def body(e):
                last = None
                for cc in range(8):
                    last = e.matmul(R.pss[0:1, 0:512], R.m10b[:, 0:1], usl[cc][:],
                                    start=(cc == 0), stop=(cc == 7))
                return last
            return body
        usmm = C.T(mkum(), [dqs4[-1]] + obar)
        us3 = C.A(lambda e: e.activation(R.r0[:, 0:512], R.pss[0:1, 0:512], AF.Sqrt,
                                         bias=R.eps56[0:1, 1:2], scale=1.0), [usmm])
        us4 = C.V(lambda e: e.reciprocal(R.r0[:, 0:512], R.r0[:, 0:512]), [us3])
        ub = C.T(lambda e: e.matmul(R.pm[4][:, 0:512], R.onerf[0:1, :], R.r0[0:1, 0:512],
                                    start=True, stop=True), [us4] + obar)
        ucp = C.V(lambda e: e.tensor_copy(R.ab[:], R.pm[4][:, 0:512]), [ub])
        tmp3 = u_r.rearrange("p (cc t) -> p cc t", cc=8)
        dw1 = C.G(lambda e: e.tensor_tensor(
            Y2_r[:, 4096:8192].rearrange("p (cc t) -> p cc t", cc=8),
            tmp3, R.ab[:, None, 0:512].broadcast_to([P, 8, 512]), ALU.mult), [ucp])
        dw2 = C.V(lambda e: e.tensor_tensor(
            tmp3, Y2_r[:, 4096:8192].rearrange("p (cc t) -> p cc t", cc=8),
            pwc[:, :, None].broadcast_to([P, 8, 512]), ALU.mult), [dw1])
        u2_deps.append(C.dma("sync",
                             dr["l0a_u2"].rearrange("(cc p) t -> p cc t", p=P)[:, :, sl],
                             tmp3, [dw2]))
        pbar = C.all_done()

    if PH < 7:
        return ln_fm(C, R, X, X, 1e-5, C.all_done())
    mbar = C.all_done()
    R.pbf.reset()
    u2h_r = R.pbf.take(8 * 1024)
    u2h = [u2h_r[:, c * 1024:(c + 1) * 1024] for c in range(8)]
    wmt_r = R.pbf.take(2 * 4096)
    wmt = [wmt_r[:, i * 4096:(i + 1) * 4096] for i in range(2)]
    hm_r = R.pbf.take(22 * 1024)
    hm = [hm_r[:, c * 1024:(c + 1) * 1024] for c in range(22)]
    GATE_GROUPS = [(0, 512), (512, 512), (1024, 512), (1536, 512), (2048, 512), (2560, 192)]
    mlp_ev = []
    prev_wd2 = None
    muser = {}
    for hh in range(2):
        hsl = slice(hh * 1024, (hh + 1) * 1024)
        du2l = [C.dma("sync", u2h[cc], dr["l0a_u2"][cc * P:(cc + 1) * P, hsl],
                      mbar if hh == 0 else [mlp_ev[-1]]) for cc in range(8)]
        prev_ev_g = {}
        for (g0, gcols) in GATE_GROUPS:
            wdg = [C.dma("gpsimd", wmt[0].rearrange("p (kc c) -> p kc c", kc=8)[:, :, 0:gcols],
                         dr["l0a_gate_t"].rearrange("(kc p) m -> p kc m", p=P)[:, :, g0:g0 + gcols],
                         [muser.get(0)] if muser.get(0) else mbar)]
            wdu = [C.dma("gpsimd", wmt[1].rearrange("p (kc c) -> p kc c", kc=8)[:, :, 0:gcols],
                         dr["l0a_up_t"].rearrange("(kc p) m -> p kc m", p=P)[:, :, g0:g0 + gcols],
                         [muser.get(1)] if muser.get(1) else mbar)]
            nsub = (gcols + 127) // 128
            for sub in range(nsub):
                mw = min(128, gcols - sub * 128)
                mchunk = (g0 + sub * 128) // 128
                for ng in range(2):
                    def mkg(sub=sub, mw=mw, ng=ng, w=0):
                        def body(e):
                            last = None
                            for kc in range(8):
                                last = e.matmul(R.pm[0 + ng][0:mw, 0:512],
                                                wmt[0][:, kc * 512 + sub * 128: kc * 512 + sub * 128 + mw],
                                                u2h[kc][:, ng * 512:(ng + 1) * 512],
                                                start=(kc == 0), stop=(kc == 7))
                            return last
                        return body
                    dg = C.T(mkg(), wdg + du2l + ([prev_ev_g.get(ng)] if prev_ev_g.get(ng) else []))
                    def mkup(sub=sub, mw=mw, ng=ng):
                        def body(e):
                            last = None
                            for kc in range(8):
                                last = e.matmul(R.pm[2 + ng][0:mw, 0:512],
                                                wmt[1][:, kc * 512 + sub * 128: kc * 512 + sub * 128 + mw],
                                                u2h[kc][:, ng * 512:(ng + 1) * 512],
                                                start=(kc == 0), stop=(kc == 7))
                            return last
                        return body
                    du_ = C.T(mkup(), wdu + ([prev_ev_g.get(2 + ng)] if prev_ev_g.get(2 + ng) else []))
                    sgl = C.A(lambda e, mw=mw, ng=ng, b=ng: e.activation(
                        R.sq[b][0:mw, :], R.pm[0 + ng][0:mw, 0:512], AF.Silu, bias=0.0, scale=1.0), [dg])
                    hev = C.V(lambda e, mchunk=mchunk, mw=mw, ng=ng, b=ng: e.tensor_tensor(
                        hm[mchunk][0:mw, ng * 512:(ng + 1) * 512], R.sq[b][0:mw, :],
                        R.pm[2 + ng][0:mw, 0:512], ALU.mult), [sgl, du_])
                    prev_ev_g[ng] = hev
                    prev_ev_g[2 + ng] = hev
                    prev_wd2 = du_
                    muser[0] = dg
                    muser[1] = du_
        hbar = C.all_done()
        KCH = [(k * 128, 128) for k in range(21)] + [(2688, 64)]
        for mcg in range(2):
            for ng in range(2):
                prev_devd = None
                for kb in range(3):
                    kcs = KCH[kb * 8:(kb + 1) * 8]
                    wdd = [C.dma("gpsimd", wmt[kb % 2][:len_, j * 512:(j + 1) * 512],
                                 dr["l0a_down_t"][k0:k0 + len_, mcg * 512:(mcg + 1) * 512],
                                 [muser.get(kb % 2)] if muser.get(kb % 2) else hbar)
                           for j, (k0, len_) in enumerate(kcs)]
                    def mkd(kb=kb, kcs=kcs, mcg=mcg, ng=ng):
                        def body(e):
                            last = None
                            for j, (k0, len_) in enumerate(kcs):
                                kc = kb * 8 + j
                                for sub in range(4):
                                    last = e.matmul(
                                        R.pm[sub][:, 0:512],
                                        wmt[kb % 2][:len_, j * 512 + sub * 128: j * 512 + (sub + 1) * 128],
                                        hm[kc][0:len_, ng * 512:(ng + 1) * 512],
                                        start=(kb == 0 and j == 0), stop=(kb == 2 and j == len(kcs) - 1))
                            return last
                        return body
                    ddeps = wdd + ([prev_devd] if prev_devd else hbar)
                    if kb == 0:
                        ddeps = ddeps + (mlp_ev[-4:] if mlp_ev else [])
                    dmd = C.T(mkd(), ddeps)
                    prev_devd = dmd
                    prev_wd2 = dmd
                    muser[kb % 2] = dmd
                for sub in range(4):
                    mc = mcg * 4 + sub
                    tsl = slice(hh * 1024 + ng * 512, hh * 1024 + (ng + 1) * 512)
                    mev = C.V(lambda e, sub=sub, mc=mc, tsl=tsl: e.tensor_tensor(
                        X[mc][:, tsl], X[mc][:, tsl], R.pm[sub][:, 0:512], ALU.add), [dmd])
                    mlp_ev.append(mev)

    return ln_fm(C, R, X, X, 1e-5, mlp_ev[-4:], sqw=u2h_r[:, 0:4096])


def l0attn_host(inp):
    cwv = np.asarray(inp["L0_attn_conv_w"], np.float32)
    cbv = np.asarray(inp["L0_attn_conv_b"], np.float32)
    cw = np.zeros((P, 72), np.float32)
    for k in range(4):
        cw[:, k * 18:(k + 1) * 18] = cwv[:, k].reshape(18, 128).T
    Dv = np.asarray(inp["L0_attn_D"], np.float32)
    dcol = np.repeat(Dv, 64).reshape(16, 128).T
    adt = np.zeros((NH, 2), np.float32)
    adt[:, 0] = np.asarray(inp["L0_attn_A_log"], np.float32)
    adt[:, 1] = np.asarray(inp["L0_attn_dt_bias"], np.float32)
    selp = np.zeros((2, P), BF)
    selp[0, 0:64] = 1
    selp[1, 64:128] = 1
    mask01 = np.tril(np.ones((P, P), np.float32)).astype(BF)
    mask01 = np.triu(np.ones((P, P), np.float32)).astype(BF)
    return {
        "l0a_inproj_t": np.ascontiguousarray(np.asarray(inp["L0_attn_in_proj_w"], np.float32).T),
        "l0a_outproj_t": np.ascontiguousarray(np.asarray(inp["L0_attn_out_proj_w"], np.float32).T),
        "l0a_gate_t": np.ascontiguousarray(np.asarray(inp["L0_attn_gate_w"], np.float32).T),
        "l0a_up_t": np.ascontiguousarray(np.asarray(inp["L0_attn_up_w"], np.float32).T),
        "l0a_down_t": np.ascontiguousarray(np.asarray(inp["L0_attn_down_w"], np.float32).T),
        "l0a_convw": cw,
        "l0a_convb": np.ascontiguousarray(cbv.reshape(18, 128).T),
        "l0a_normw": np.ascontiguousarray(np.asarray(inp["L0_attn_norm_w"], np.float32).reshape(8, 128).T),
        "l0a_postw": np.ascontiguousarray(np.asarray(inp["L0_attn_post_norm_w"], np.float32).reshape(8, 128).T),
        "l0a_wssm": np.ascontiguousarray(np.asarray(inp["L0_attn_ssm_norm_w"], np.float32).reshape(16, 128).T),
        "l0a_dcol": dcol.astype(np.float32),
        "l0a_adt": adt,
        "c_selp": selp,
        "c_mask01": mask01,
    }


def l0attn_decls():
    return [("l0a_inproj_t", [D, 4384], F32), ("l0a_outproj_t", [2048, D], F32),
            ("l0a_gate_t", [D, 2752], F32), ("l0a_up_t", [D, 2752], F32),
            ("l0a_down_t", [2752, D], F32),
            ("l0a_convw", [P, 72], F32), ("l0a_convb", [P, 18], F32),
            ("l0a_normw", [P, 8], F32), ("l0a_postw", [P, 8], F32),
            ("l0a_wssm", [P, 16], F32), ("l0a_dcol", [P, 16], F32),
            ("l0a_adt", [NH, 2], F32), ("c_selp", [2, P], BF16),
            ("c_mask01", [P, P], BF16)]


# version 20 (final)
# speedup vs baseline: 1.0683x; 1.0022x over previous
from contextlib import ExitStack

import numpy as np
import ml_dtypes
import concourse.bass as bass
import concourse.mybir as mybir
from concourse.bass_utils import run_bass_kernel_spmd

F32 = mybir.dt.float32
BF16 = mybir.dt.bfloat16
I8 = mybir.dt.int8
AF = mybir.ActivationFunctionType
ALU = mybir.AluOpType
AX = mybir.AxisListType

P = 128
S = 2048
D = 1024
NCORES = 8
MAGIC = 12582912.0
ENGINES = ("tensor", "vector", "scalar", "gpsimd", "sync")
REPLICA_GROUPS = [list(range(8))]
BF = ml_dtypes.bfloat16

BFPOOL_COLS = 42240
F32POOL_COLS = 4928


class Sem:
    __slots__ = ("h", "n", "name")

    def __init__(self, h, name):
        self.h, self.n, self.name = h, 0, name


class Ctx:
    def __init__(self, nc, st):
        self.nc = nc
        self.st = st
        self.ops = {e: [] for e in ENGINES}
        self.es = {e: self.sem("es_" + e) for e in ENGINES}
        self.dW = self.sem("dW")
        self.dX = self.sem("dX")

    def sem(self, name):
        return Sem(self.st.enter_context(self.nc.semaphore(name)), name)

    def sb(self, name, shape, dt):
        return self.st.enter_context(self.nc.sbuf_tensor(name, shape, dt))

    def ps(self, name, shape, dt=F32):
        return self.st.enter_context(self.nc.psum_tensor(name, shape, dt))

    def step(self, eng, body, deps=(), sem=None, inc=1):
        if sem is None:
            sem = self.es[eng]
        best = {}
        for d in deps:
            if d is None:
                continue
            s_, v_ = d
            if best.get(id(s_), (None, -1))[1] < v_:
                best[id(s_)] = (s_, v_)
        deps = list(best.values())

        def fn(e, deps=tuple(deps), body=body, sem=sem, inc=inc):
            for s_, v_ in deps:
                e.wait_ge(s_.h, v_)
            last = body(e)
            if last is not None:
                last.then_inc(sem.h, inc)
            else:
                e.sem_inc(sem.h, inc)

        self.ops[eng].append(fn)
        sem.n += inc
        return (sem, sem.n)

    def T(self, body, deps=()):
        return self.step("tensor", body, deps)

    def V(self, body, deps=()):
        return self.step("vector", body, deps)

    def A(self, body, deps=()):
        return self.step("scalar", body, deps)

    def G(self, body, deps=()):
        return self.step("gpsimd", body, deps)

    def dma(self, qeng, out, in_, deps=()):
        sem = self.dW if qeng == "gpsimd" else self.dX
        return self.step(qeng, lambda e: e.dma_start(out=out, in_=in_), deps, sem, 16)

    def all_done(self):
        return ([(s, s.n) for s in self.es.values()]
                + [(self.dW, self.dW.n), (self.dX, self.dX.n)])


def build_and_finalize(nc, C):
    with nc.Block() as block:
        @block.tensor
        def _(e):
            for fn in C.ops["tensor"]:
                fn(e)

        @block.vector
        def _(e):
            for fn in C.ops["vector"]:
                fn(e)

        @block.scalar
        def _(e):
            for fn in C.ops["scalar"]:
                fn(e)

        @block.gpsimd
        def _(e):
            for fn in C.ops["gpsimd"]:
                fn(e)

        @block.sync
        def _(e):
            for fn in C.ops["sync"]:
                fn(e)


class Pool:

    def __init__(self, C, name, dt, cols):
        self.t = C.sb(name, [P, cols], dt)
        self.cols = cols
        self.off = 0

    def take(self, cols):
        assert self.off + cols <= self.cols, f"pool overflow: {self.off}+{cols}>{self.cols}"
        ap = self.t[:, self.off:self.off + cols]
        self.off += cols
        return ap

    def reset(self, keep=0):
        self.off = keep


class Res:

    def __init__(self, C):
        self.idf = C.sb("s_idf", [P, P], F32)
        self.idb = C.sb("s_idb", [P, P], BF16)
        self.onecb = C.sb("s_onecb", [P, 1], BF16)
        self.onecf = C.sb("s_onecf", [P, 1], F32)
        self.m10b = C.sb("s_m10b", [P, 1], BF16)
        self.m11b = C.sb("s_m11b", [P, 1], BF16)
        self.m10f = C.sb("s_m10f", [P, 1], F32)
        self.eps56 = C.sb("s_eps56", [P, 2], F32)
        self.onerf = C.sb("s_onerf", [65, P], F32)
        self.pm = [C.ps(f"pm{i}", [P, 512], F32) for i in range(6)]
        self.ptr = C.ps("ptr", [P, 1024], BF16)
        self.pss = C.ps("pss", [P, 512], F32)
        self.sq = [C.sb(f"sc_sq{i}", [P, 512], BF16) for i in range(2)]
        self.ab = C.sb("sc_ab", [P, 512], F32)
        self.bb = C.sb("sc_bb", [P, 512], F32)
        self.big = C.sb("sc_big", [P, S], F32)
        self.selp = C.sb("s_selp", [2, P], BF16)
        self.mask01 = C.sb("s_mask01", [P, P], BF16)
        self.qsl = [C.sb(f"sc_qsl{i}", [P, 1024], I8) for i in range(2)]
        self.rows = C.sb("sc_rows", [1, 3 * S], F32)
        self.r0 = self.rows[0:1, 0:S]
        self.r1 = self.rows[0:1, S:2 * S]
        self.r2 = self.rows[0:1, 2 * S:3 * S]
        self.pbf = Pool(C, "pool_bf", BF16, BFPOOL_COLS)
        self.pf = Pool(C, "pool_f32", F32, F32POOL_COLS)

    def load(self, C, dr):
        d = [C.dma("sync", self.idf[:], dr["c_idf"]),
             C.dma("sync", self.idb[:], dr["c_idb"]),
             C.dma("sync", self.onecb[:], dr["c_onecb"]),
             C.dma("sync", self.onecf[:], dr["c_onecf"]),
             C.dma("sync", self.onerf[:], dr["c_onerf"]),
             C.dma("sync", self.m10b[:], dr["c_m10b"]),
             C.dma("sync", self.m11b[:], dr["c_m11b"]),
             C.dma("sync", self.m10f[:], dr["c_m10f"]),
             C.dma("sync", self.eps56[:], dr["c_eps56"])]
        self.deps = d
        return d

    @staticmethod
    def host_arrays():
        return {
            "c_idf": np.eye(P, dtype=np.float32),
            "c_idb": np.eye(P, dtype=BF),
            "c_onecb": np.ones((P, 1), dtype=BF),
            "c_onecf": np.ones((P, 1), dtype=np.float32),
            "c_onerf": np.ones((65, P), dtype=np.float32),
            "c_m10b": np.full((P, 1), 2.0 ** -10, dtype=BF),
            "c_m11b": np.full((P, 1), 2.0 ** -11, dtype=BF),
            "c_m10f": np.full((P, 1), 2.0 ** -10, dtype=np.float32),
            "c_eps56": np.tile(np.array([[1e-5, 1e-6]], np.float32), (P, 1)),
        }

    @staticmethod
    def decls():
        return [("c_idf", [P, P], F32), ("c_idb", [P, P], BF16),
                ("c_onecb", [P, 1], BF16), ("c_onecf", [P, 1], F32),
                ("c_onerf", [65, P], F32), ("c_m10b", [P, 1], BF16),
                ("c_m11b", [P, 1], BF16), ("c_m10f", [P, 1], F32),
                ("c_eps56", [P, 2], F32)]



def bcast_row(C, R, dst, row_ap, n, deps, pbank, base=0):
    d1 = C.T(lambda e: e.matmul(pbank[:, 0:n], R.onerf[base:base + 1, :], row_ap,
                                start=True, stop=True), deps)
    return C.V(lambda e: e.tensor_copy(dst[:, 0:n], pbank[:, 0:n]), [d1])


def ln_fm(C, R, SRC, DST, eps, deps, nf=None, sqw=None):
    nch = len(SRC)
    assert (nf or nch * P) == 1024
    NG = S // 512
    sum_deps = []
    prev_dsv = None
    prev_dqv = None
    sq_slots = None
    if sqw is not None:
        sq_slots = [sqw[:, i * 512:(i + 1) * 512] for i in range(8)]
    for ng in range(NG):
        sl = slice(ng * 512, (ng + 1) * 512)

        def mksum(sl=sl):
            def body(e):
                last = None
                for cc in range(nch):
                    last = e.matmul(R.pm[4][0:1, 0:512], R.m10f[:, 0:1], SRC[cc][:, sl],
                                    start=(cc == 0), stop=(cc == nch - 1))
                return last
            return body
        dsu = C.T(mksum(), list(deps) + ([prev_dsv] if prev_dsv else []))
        dsv = C.V(lambda e, sl=sl: e.tensor_copy(R.r0[:, sl], R.pm[4][0:1, 0:512]), [dsu])
        prev_dsv = dsv

        if sq_slots is not None:
            dqs = []
            for cc in range(nch):
                dq = C.A(lambda e, cc=cc, sl=sl: e.activation(sq_slots[cc][:], SRC[cc][:, sl],
                                                              AF.Square, bias=0.0, scale=1.0),
                         (list(deps) + ([prev_dqv] if prev_dqv else [])) if cc == 0 else [dqs[-1]])
                dqs.append(dq)

            def mkq2():
                def body(e):
                    last = None
                    for cc in range(nch):
                        last = e.matmul(R.pm[5][0:1, 0:512], R.m10b[:, 0:1], sq_slots[cc][:],
                                        start=(cc == 0), stop=(cc == nch - 1))
                    return last
                return body
            prev = C.T(mkq2(), [dqs[-1], dsu])
        else:
            prev = dsv
            prev_mm2 = None
            for cc in range(nch):
                sqdeps = [prev] if cc == 0 else ([prev_mm2] if prev_mm2 else [])
                if cc == 0:
                    sqdeps += list(deps) + ([prev_dqv] if prev_dqv else [])
                dq = C.A(lambda e, cc=cc, sl=sl, b=cc % 2:
                         e.activation(R.sq[b][:], SRC[cc][:, sl], AF.Square, bias=0.0, scale=1.0),
                         sqdeps)
                dmm = C.T(lambda e, cc=cc, b=cc % 2:
                          e.matmul(R.pm[5][0:1, 0:512], R.m10b[:, 0:1], R.sq[b][:],
                                   start=(cc == 0), stop=(cc == nch - 1)), [dq])
                prev_mm2 = prev
                prev = dmm
        dqv = C.V(lambda e, sl=sl: e.tensor_copy(R.r1[:, sl], R.pm[5][0:1, 0:512]), [prev])
        prev_dqv = dqv
        sum_deps += [dsv, dqv]

    def stats(e):
        e.tensor_tensor(R.r2[:, :], R.r0[:, :], R.r0[:, :], ALU.mult)
        return e.tensor_tensor(R.r1[:, :], R.r1[:, :], R.r2[:, :], ALU.subtract)
    dst_ = C.V(stats, sum_deps)
    dsq = C.A(lambda e: e.activation(R.r1[:, :], R.r1[:, :], AF.Sqrt, bias=R.eps56[0:1, 0:1], scale=1.0), [dst_])
    drc = C.V(lambda e: e.reciprocal(R.r1[:, :], R.r1[:, :]), [dsq])

    def mkb(e):
        e.tensor_tensor(R.r2[:, :], R.r0[:, :], R.r1[:, :], ALU.mult)
        return e.tensor_scalar(R.r2[:, :], R.r2[:, :], -1.0, None, ALU.mult)
    dbr = C.V(mkb, [drc])

    out_deps = []
    prev_ap = None
    X3s = R.X_r.rearrange("p (cc t) -> p cc t", cc=8)
    for ng in range(NG):
        sl = slice(ng * 512, (ng + 1) * 512)
        da = C.T(lambda e, sl=sl: e.matmul(R.pm[4][:, 0:512], R.onerf[0:1, :], R.r1[:, sl],
                                           start=True, stop=True), [dbr, prev_ap])
        db = C.T(lambda e, sl=sl: e.matmul(R.pm[5][:, 0:512], R.onerf[0:1, :], R.r2[:, sl],
                                           start=True, stop=True), [dbr, prev_ap])
        dc = C.V(lambda e: e.tensor_copy(R.ab[:], R.pm[4][:, 0:512]), [da])
        dd = C.A(lambda e: e.activation(R.bb[:], R.pm[5][:, 0:512], AF.Identity,
                                        bias=0.0, scale=1.0), [db])

        def mkap(sl=sl):
            def body(e):
                e.tensor_tensor(X3s[:, :, sl], X3s[:, :, sl],
                                R.ab[:, None, :].broadcast_to([P, 8, 512]).transpose([0, 1, 2])
                                if False else R.ab[:, None, 0:512].broadcast_to([P, 8, 512]),
                                ALU.mult)
                return e.tensor_tensor(X3s[:, :, sl], X3s[:, :, sl],
                                       R.bb[:, None, 0:512].broadcast_to([P, 8, 512]), ALU.add)
            return body
        dap = C.V(mkap(), [dc, dd])
        prev_ap = dap
        out_deps.append(dap)
    return out_deps


def final_store(C, R, SRC, out_dram, deps):
    bar = C.all_done()
    deps = list(deps) + bar
    R.pf.reset(keep=64)
    xtm = R.pf.take(D)
    dmas = []
    prev_copy = None
    for tt in range(S // P):
        t0 = tt * P

        def mktr(t0=t0):
            def body(e):
                last = None
                for cc in range(8):
                    pb = R.pm[cc // 4]
                    last = e.transpose(pb[:, (cc % 4) * P:(cc % 4 + 1) * P], SRC[cc][:, t0:t0 + P], R.idf[:])
                return last
            return body
        dtr = C.T(mktr(), list(deps) + ([prev_copy] if prev_copy else []))

        def mkcp():
            def body(e):
                e.tensor_copy(xtm[:, 0:512], R.pm[0][:, 0:512])
                return e.tensor_copy(xtm[:, 512:1024], R.pm[1][:, 0:512])
            return body
        dcp = C.V(mkcp(), [dtr])
        ddma = C.dma("sync", out_dram[t0:t0 + P, :], xtm, [dcp])
        prev_copy = ddma
        dmas.append(ddma)
    return dmas



def emit_l1ffn(C, R, X, dr):
    KC, MC, NG = 8, 8, 4
    bar = C.all_done()
    R.pbf.reset()
    R.pf.reset()
    wqr = R.pbf.take(8 * D)
    xqr = R.pbf.take(8 * S)
    wq = [wqr[:, k * D:(k + 1) * D] for k in range(KC)]
    xq = [xqr[:, c * S:(c + 1) * S] for c in range(KC)]
    fr = R.pf.take(64)
    cols = fr[:, 0:4]
    wsc = fr[:, 8:16]
    bia = fr[:, 16:24]
    scl = fr[:, 24:32]
    gmx = fr[0:1, 32:36]
    ccs = R.pf.take(P)[0:1, :]

    dw = [C.dma("gpsimd", wqr.rearrange("p (kc c) -> p kc c", kc=8),
                dr["l1f_w_t"].rearrange("(kc p) m -> p kc m", p=P), bar)]
    dsc = C.dma("sync", wsc, dr["l1f_wscale_t"], bar)
    dbi = C.dma("sync", bia, dr["l1f_bias_t"], bar)

    amx = R.pf.take(8)
    def absmax(e):
        last = None
        for cc in range(8):
            last = e.tensor_reduce(amx[:, cc:cc + 1], X[cc][:], AX.X, ALU.max, apply_absolute_value=True)
        return last
    d1a = C.V(absmax, bar)
    d1 = C.V(lambda e: e.tensor_reduce(cols[:, 0:1], amx[:], AX.X, ALU.max), [d1a])
    d2 = C.T(lambda e: e.transpose(R.pss[0:1, 0:P], cols[:, 0:1], R.idf[:]), [d1] + bar)
    d3 = C.V(lambda e: e.tensor_reduce(gmx[0:1, 0:1], R.pss[0:1, 0:P], AX.X, ALU.max), [d2])
    d4 = C.V(lambda e: e.tensor_copy(ccs, gmx[0:1, 0:1].broadcast_to([1, P])), [d3])

    dci = C.dma("gpsimd", dr["l1f_cc_in"], ccs, [d4])
    dco = C.step("gpsimd",
                 lambda e: e.collective_compute("AllReduce", ALU.max,
                                                replica_groups=REPLICA_GROUPS,
                                                ins=[dr["l1f_cc_in"]],
                                                outs=[dr["l1f_cc_out"]]),
                 [dci])
    dcb = C.dma("gpsimd", gmx[0:1, 0:2], dr["l1f_cc_out"][0:1, 0:2], [dco])

    d5 = C.V(lambda e: e.tensor_scalar(gmx[0:1, 2:3], gmx[0:1, 0:1], 1.0 / 127.0, 1e-5, ALU.mult, ALU.max), [dcb])
    d6 = C.V(lambda e: e.reciprocal(gmx[0:1, 3:4], gmx[0:1, 2:3]), [d5])
    d7 = C.T(lambda e: e.matmul(R.pss[:, 0:2], R.onerf[0:1, :], gmx[0:1, 2:4], start=True, stop=True),
             [d6])
    d8 = C.V(lambda e: e.tensor_copy(cols[:, 2:4], R.pss[:, 0:2]), [d7])
    d9 = C.V(lambda e: e.tensor_tensor(scl, wsc, cols[:, 2:3].broadcast_to([P, MC]), ALU.mult),
             [d8, dsc])

    qdeps = []
    xq3 = xqr.rearrange("p (cc t) -> p cc t", cc=8)
    X3 = R.X_r.rearrange("p (cc t) -> p cc t", cc=8)
    for cc in range(8):
        def mkq(cc=cc):
            def body(e):
                e.tensor_scalar(R.big[:], X[cc][:], cols[:, 3:4], None, ALU.mult)
                return e.tensor_scalar(xq[cc][:], R.big[:], MAGIC, MAGIC, ALU.add, ALU.subtract)
            return body
        qdeps.append(C.V(mkq(), [d8] if cc == 0 else [d8, qdeps[-1]]))

    prev_ev = {}
    ev_deps = []
    for mc in range(MC):
        for ng in range(NG):
            pbi = (mc * NG + ng) % 4
            pb = R.pm[pbi]

            def mkmm(mc=mc, ng=ng, pb=pb):
                def body(e):
                    last = None
                    for kc in range(KC):
                        last = e.matmul(pb[:, 0:512], wq[kc][:, mc * P:(mc + 1) * P],
                                        xq[kc][:, ng * 512:(ng + 1) * 512],
                                        start=(kc == 0), stop=(kc == KC - 1))
                    return last
                return body
            mmd = list(qdeps) + dw + [d9, dbi]
            if pbi in prev_ev:
                mmd.append(prev_ev[pbi])
            dmm = C.T(mkmm(), mmd)

            def mkev(mc=mc, ng=ng, pb=pb):
                sl = slice(ng * 512, (ng + 1) * 512)

                def body(e):
                    e.scalar_tensor_tensor(X[mc][:, sl], pb[:, 0:512], scl[:, mc:mc + 1],
                                           X[mc][:, sl], ALU.mult, ALU.add)
                    return e.tensor_scalar(X[mc][:, sl], X[mc][:, sl], bia[:, mc:mc + 1], None, ALU.add)
                return body
            dev = C.V(mkev(), [dmm])
            prev_ev[pbi] = dev
            ev_deps.append(dev)

    return ln_fm(C, R, X, X, 1e-5, ev_deps, sqw=xqr[:, 0:4096])


def l1ffn_host(inp):
    w = np.asarray(inp["L1_ffn_weight_int8"])
    ws = np.asarray(inp["L1_ffn_weight_scale"], np.float32)
    bi = np.asarray(inp["L1_ffn_bias"], np.float32)
    return {
        "l1f_w_t": np.ascontiguousarray(w.T),
        "l1f_wscale_t": np.ascontiguousarray(ws.reshape(8, 128).T),
        "l1f_bias_t": np.ascontiguousarray(bi.reshape(8, 128).T),
    }


def l1ffn_decls():
    return [("l1f_w_t", [D, D], I8), ("l1f_wscale_t", [P, 8], F32),
            ("l1f_bias_t", [P, 8], F32)]



def emit_l1attn(C, R, X, dr):
    KC, MC = 8, 8
    BLK, WIN = 64, 68
    H = 1024
    NGH = H // 512

    bar = C.all_done()
    R.pbf.reset()
    R.pf.reset()
    wr = R.pbf.take(8 * D)
    qr = R.pbf.take(8 * H)
    kr = R.pbf.take(8 * (H + 4))
    vr = R.pbf.take(8 * (H + 4))
    q = [qr[:, c * H:(c + 1) * H] for c in range(8)]
    kx = [kr[:, c * (H + 4):(c + 1) * (H + 4)] for c in range(8)]
    vx = [vr[:, c * (H + 4):(c + 1) * (H + 4)] for c in range(8)]
    ws = [wr[:, k * D:(k + 1) * D] for k in range(KC)]

    xbr = R.pbf.take(8 * 1024)
    xb2 = [xbr[:, c * 1024:(c + 1) * 1024] for c in range(8)]
    xexr = R.pbf.take(16)
    xex = [xexr[:, c * 2:(c + 1) * 2] for c in range(8)]
    fr = R.pf.take(512)
    bcols = fr[:, 0:24]
    mcol = fr[0:BLK, 24:28]
    sc = fr[0:BLK, 28:28 + WIN]
    ot = R.pf.take(D)[0:BLK, :]
    pbr = R.pbf.take(WIN + BLK)
    pb = pbr[0:BLK, 0:WIN]
    ptb = pbr[0:WIN, WIN:WIN + BLK]
    vwt = R.pbf.take(D)[0:WIN, :]
    masks = R.pf.take(3 * WIN)[0:BLK, :]

    dbc = C.dma("sync", bcols, dr["l1a_b_t"], bar)
    dmk = C.dma("sync", masks, dr["l1a_masks"], bar)

    xs_upd = []
    prev_half_kv = None
    prev_state = None
    lp_pss = [None]
    qkv_store = {}
    for hh in range(2):
        H0 = hh * H
        if hh == 1:
            def mvkv(e):
                last = None
                for cc in range(8):
                    e.tensor_copy(kx[cc][:, 0:2], kx[cc][:, H:H + 2])
                    last = e.tensor_copy(vx[cc][:, 0:2], vx[cc][:, H:H + 2])
                return last
            dkv = C.V(mvkv, [xs_upd[-1]])
            dpe = C.V(lambda e: [e.memset(kx[cc][:, H + 2:H + 4], 0.0) or
                                 e.memset(vx[cc][:, H + 2:H + 4], 0.0) for cc in range(8)][-1],
                      [dkv])
        else:
            def mkpad(e):
                last = None
                for cc in range(8):
                    e.memset(kx[cc][:, 0:2], 0.0)
                    last = e.memset(vx[cc][:, 0:2], 0.0)
                return last
            dpe = C.V(mkpad, bar)

        projs = [("l1a_wq", q, 0, 0), ("l1a_wk", kx, 2, 8), ("l1a_wv", vx, 2, 16)]
        ev_all = []
        prev_mm_last = prev_state["wmm"] if prev_state else None
        prev_ev = dict(prev_state["ev"]) if prev_state else {}
        xb_deps = []
        for ng in range(NGH):
            def mkxb(ng=ng, H0=H0):
                def body(e):
                    last = None
                    for kc in range(8):
                        last = e.tensor_copy(xb[kc][:, ng * 256:(ng + 1) * 256] if False else xb[kc],
                                             X[kc][:, H0 + ng * 512:H0 + (ng + 1) * 512])
                    return last
                return body
        xbr2 = None
        def mkxb_all(H0=H0):
            def body(e):
                last = None
                for kc in range(8):
                    for ng in range(NGH):
                        last = e.tensor_copy(xb2[kc][:, ng * 512:(ng + 1) * 512],
                                             X[kc][:, H0 + ng * 512:H0 + (ng + 1) * 512])
                return last
            return body
        dxb = C.V(mkxb_all(), bar if hh == 0 else [xs_upd[-1]])
        for j, (wname, dst, off, bofs) in enumerate(projs):
            wbfap = dr["l1a_w_bf"][j * 1024:(j + 1) * 1024, :].rearrange("(kc p) m -> p kc m", p=P)
            wdep0 = ([prev_mm_last] if prev_mm_last else bar) + list(qkv_store.values())
            if hh == 0:
                wdeps = [C.dma("gpsimd", wr.rearrange("p (kc c) -> p kc c", kc=8),
                               dr[wname].rearrange("(kc p) m -> p kc m", p=P), wdep0)]
                qkv_store[wname] = C.dma("sync", wbfap,
                                         wr.rearrange("p (kc c) -> p kc c", kc=8), wdeps)
            else:
                wdeps = [C.dma("sync", wr.rearrange("p (kc c) -> p kc c", kc=8), wbfap, wdep0)]
            for mc in range(MC):
                for ng in range(NGH):
                    pbi = (mc * NGH + ng) % 4
                    pbank = R.pm[pbi]

                    def mkmm(mc=mc, ng=ng, pbank=pbank):
                        def body(e):
                            last = None
                            for kc in range(KC):
                                last = e.matmul(pbank[0:P, 0:512],
                                                ws[kc][:, mc * P:(mc + 1) * P],
                                                xb2[kc][:, ng * 512:(ng + 1) * 512],
                                                start=(kc == 0), stop=(kc == KC - 1))
                            return last
                        return body
                    mmd = wdeps + [dxb] + ([prev_ev[pbi]] if pbi in prev_ev else [])
                    dmm = C.T(mkmm(), mmd)
                    if mc == MC - 1 and ng == NGH - 1:
                        prev_mm_last = dmm

                    def mkev(mc=mc, ng=ng, pbank=pbank, dst=dst, off=off, bofs=bofs):
                        def body(e):
                            return e.activation(dst[mc][:, off + ng * 512:off + (ng + 1) * 512],
                                                pbank[0:P, 0:512], AF.Identity,
                                                bias=bcols[:, bofs + mc:bofs + mc + 1], scale=1.0)
                        return body
                    dev = C.A(mkev(), [dmm, dbc, dpe])
                    prev_ev[pbi] = dev
                    ev_all.append(dev)
            if hh == 0 and j in (1, 2):
                dxe = C.V(lambda e: [e.tensor_copy(xex[kc], X[kc][:, H:H + 2]) for kc in range(8)][-1],
                          [dxb])
                def mkexmm(dst=dst):
                    def body(e):
                        last = None
                        for mc in range(MC):
                            for kc in range(KC):
                                last = e.matmul(R.pss[0:P, mc * 2:(mc + 1) * 2],
                                                ws[kc][:, mc * P:(mc + 1) * P], xex[kc],
                                                start=(kc == 0), stop=(kc == KC - 1))
                        return last
                    return body
                dxm = C.T(mkexmm(), [dxe, prev_mm_last] + ([lp_pss[0]] if lp_pss[0] else []))
                def mkexev(dst=dst, bofs=bofs):
                    def body(e):
                        last = None
                        for mc in range(MC):
                            last = e.activation(dst[mc][:, H + 2:H + 4], R.pss[0:P, mc * 2:(mc + 1) * 2],
                                                AF.Identity, bias=bcols[:, bofs + mc:bofs + mc + 1],
                                                scale=1.0)
                        return last
                    return body
                dxv = C.A(mkexev(), [dxm])
                prev_mm_last = dxm
                ev_all.append(dxv)
        prev_half_kv = ev_all[-1]

        if "dbgq" in dr and hh == 0:
            ddq = C.dma("sync", dr["dbgq"][0:P, :], q[0][:, :], [ev_all[-1]])
            ddk = C.dma("sync", dr["dbgk"][0:P, :], kx[0][:, :], [ev_all[-1]])
            ddv = C.dma("sync", dr["dbgv"][0:P, :], vx[0][:, :], [ev_all[-1]])

        prev = prev_state["blk"] if prev_state else \
            {"sc": None, "ptr": None, "out0": None, "out1": None, "tr": None, "ex": None,
             "pss": None}
        for b in range(16):
            t0 = b * BLK
            gb = hh * 16 + b
            mi = 1 if gb == 0 else (2 if gb == 31 else 0)
            msl = slice(mi * WIN, (mi + 1) * WIN)

            def mksc(t0=t0):
                def body(e):
                    last = None
                    for kc in range(KC):
                        last = e.matmul(R.pm[4][0:BLK, 0:WIN], q[kc][:, t0:t0 + BLK],
                                        kx[kc][:, t0:t0 + WIN],
                                        start=(kc == 0), stop=(kc == KC - 1))
                    return last
                return body
            dsc_ = C.T(mksc(), [ev_all[-1]] + ([prev["sc"]] if prev["sc"] else [dmk]))

            dms_a = C.V(lambda e, msl=msl: e.tensor_tensor(sc, R.pm[4][0:BLK, 0:WIN], masks[:, msl], ALU.add),
                        [dsc_] + ([prev["ex"]] if prev.get("ex") else []))
            dms_b = C.V(lambda e: e.tensor_reduce(mcol[:, 0:1], sc, AX.X, ALU.max), [dms_a])
            dms = C.V(lambda e: e.tensor_scalar(mcol[:, 1:2], mcol[:, 0:1], -1.0 / 32.0, None, ALU.mult),
                      [dms_b])
            prev["sc"] = dms

            dex = C.A(lambda e: e.activation(pb, sc, AF.Exp, bias=mcol[:, 1:2], scale=1.0 / 32.0),
                      [dms])
            prev["ex"] = dex

            dz_a = C.V(lambda e: e.tensor_reduce(mcol[:, 2:3], pb, AX.X, ALU.add), [dex])
            dz = C.V(lambda e: e.reciprocal(mcol[:, 3:4], mcol[:, 2:3]), [dz_a])

            dpt = C.T(lambda e: e.transpose(R.ptr[0:WIN, 0:BLK], pb, R.idb[0:BLK, 0:BLK]),
                      [dex] + ([prev["ptr"]] if prev["ptr"] else []))
            dptv = C.V(lambda e: e.tensor_copy(ptb, R.ptr[0:WIN, 0:BLK]), [dpt])

            dprev_e = dptv
            dprev_o = prev["pss"] if prev["pss"] else ev_all[-1]
            dvts = []
            for kc in range(KC):
                if kc % 2 == 0:
                    dvt = C.T(lambda e, kc=kc, t0=t0: e.transpose(
                        R.ptr[0:WIN, 0:P], vx[kc][:, t0:t0 + WIN], R.idb[:]), [dprev_e])
                    dvv = C.A(lambda e, kc=kc: e.activation(
                        vwt[:, kc * P:(kc + 1) * P], R.ptr[0:WIN, 0:P],
                        AF.Identity, bias=0.0, scale=1.0), [dvt])
                    dprev_e = dvv
                else:
                    dvt = C.T(lambda e, kc=kc, t0=t0: e.matmul(
                        R.pss[0:WIN, 0:P], vx[kc][:, t0:t0 + WIN], R.idb[:],
                        start=True, stop=True), [dprev_o])
                    dvv = C.V(lambda e, kc=kc: e.tensor_copy(
                        vwt[:, kc * P:(kc + 1) * P], R.pss[0:WIN, 0:P]), [dvt])
                    dprev_o = dvv
                dvts.append(dvv)
            prev["ptr"] = dprev_e
            prev["pss"] = dprev_o
            lp_pss[0] = dprev_o

            dm0 = C.T(lambda e: e.matmul(R.pm[0][0:BLK, 0:512], ptb, vwt[:, 0:512], start=True, stop=True),
                      [dptv, dvts[2], dvts[3]] + ([prev["out0"]] if prev["out0"] else []))
            dm1 = C.T(lambda e: e.matmul(R.pm[1][0:BLK, 0:512], ptb, vwt[:, 512:1024], start=True, stop=True),
                      [dvts[6], dvts[7]] + ([prev["out1"]] if prev["out1"] else []))
            de0 = C.A(lambda e: e.activation(ot[:, 0:512], R.pm[0][0:BLK, 0:512], AF.Copy,
                                             bias=0.0, scale=mcol[:, 3:4]), [dm0, dz])
            de1 = C.A(lambda e: e.activation(ot[:, 512:1024], R.pm[1][0:BLK, 0:512], AF.Copy,
                                             bias=0.0, scale=mcol[:, 3:4]), [dm1, dz])
            prev["out0"], prev["out1"] = de0, de1

            dtr = C.T(lambda e: [e.transpose(R.pm[2][:, kc * BLK:(kc + 1) * BLK],
                                             ot[0:BLK, kc * P:(kc + 1) * P], R.idf[0:BLK, 0:BLK])
                                 for kc in range(8)][-1],
                      [de0, de1] + ([prev["tr"]] if prev["tr"] else []))

            def mkadd(t0=t0, H0=H0):
                def body(e):
                    X3a = R.X_r.rearrange("p (cc t) -> p cc t", cc=8)
                    sl3 = slice(H0 + t0, H0 + t0 + BLK)
                    return e.tensor_tensor(X3a[:, :, sl3], X3a[:, :, sl3],
                                           R.pm[2][:, 0:512].rearrange("p (cc t) -> p cc t", cc=8),
                                           ALU.add)
                return body
            dad = C.V(mkadd(), [dtr])
            prev["tr"] = dad
            xs_upd.append(dad)
            if "dbg_sc" in dr and gb == 0:
                C.dma("sync", dr["dbg_sc"], sc, [dms])
                C.dma("sync", dr["dbg_pb"], pb, [dex])
                C.dma("sync", dr["dbg_pt"], ptb, [dptv])
                C.dma("sync", dr["dbg_vw"], vwt, [dvts[7]])
                C.dma("sync", dr["dbg_ot"], ot, [de0, de1])
                C.dma("sync", dr["dbg_mc"], mcol, [dz])
        prev_state = {"wmm": prev_mm_last,
                      "ev": {0: prev["out0"], 1: prev["out1"], 2: prev["tr"], 3: prev_ev.get(3)},
                      "blk": prev}

    if "dbgx" in dr:
        dx = [C.dma("sync", dr["dbgx"][cc * P:(cc + 1) * P, :], X[cc][:], [xs_upd[-1]]) for cc in range(8)]
        return ln_fm(C, R, X, X, 1e-5, [xs_upd[-1]] + dx)
    return ln_fm(C, R, X, X, 1e-5, [xs_upd[-1]], sqw=qr[:, 0:4096])


def l1attn_host(inp):
    b = np.concatenate([np.asarray(inp["L1_attn_bq"], np.float32).reshape(8, 128).T,
                        np.asarray(inp["L1_attn_bk"], np.float32).reshape(8, 128).T,
                        np.asarray(inp["L1_attn_bv"], np.float32).reshape(8, 128).T], axis=1)
    WIN, BLK = 68, 64
    masks = np.zeros((BLK, 3 * WIN), np.float32)
    for mi in range(3):
        m = np.full((BLK, WIN), -1e30, np.float32)
        for t in range(BLK):
            for i in range(WIN):
                if 0 <= i - t <= 4:
                    m[t, i] = 0.0
        if mi == 1:
            m[:, 0:2] = -1e30
        if mi == 2:
            m[:, WIN - 2:WIN] = -1e30
        masks[:, mi * WIN:(mi + 1) * WIN] = m
    return {
        "l1a_wq": np.ascontiguousarray(np.asarray(inp["L1_attn_Wq"], np.float32)),
        "l1a_wk": np.ascontiguousarray(np.asarray(inp["L1_attn_Wk"], np.float32)),
        "l1a_wv": np.ascontiguousarray(np.asarray(inp["L1_attn_Wv"], np.float32)),
        "l1a_b_t": np.ascontiguousarray(b),
        "l1a_masks": masks,
    }


def l1attn_decls():
    return [("l1a_wq", [D, D], F32), ("l1a_wk", [D, D], F32), ("l1a_wv", [D, D], F32),
            ("l1a_b_t", [P, 24], F32), ("l1a_masks", [64, 3 * 68], F32)]



def declare(nc, decls, kind="ExternalInput"):
    return {n: nc.dram_tensor(n, shp, dt, kind=kind).ap() for (n, shp, dt) in decls}


BLOCK_EMIT = {}
BLOCK_DECLS = {}


def _register_blocks():
    BLOCK_EMIT.update({"l1ffn": emit_l1ffn, "l1attn": emit_l1attn,
                       "l0attn": emit_l0attn})
    BLOCK_DECLS.update({"l1ffn": l1ffn_decls, "l1attn": l1attn_decls,
                        "l0ffn": l0ffn_decls, "l0attn": l0attn_decls})


def build_launch(blocks):
    nc = bass.Bass("TRN2", target_bir_lowering=False)
    st = ExitStack()
    _register_blocks()
    dr = {}
    decls = [("x_fm", [D, S], F32)] + Res.decls()
    for b in blocks:
        decls += BLOCK_DECLS[b]()
    dr.update(declare(nc, decls))
    dr.update(declare(nc, [("y", [S, D], F32)], kind="ExternalOutput"))
    import os
    if os.environ.get("KDBG") and "l0attn" in blocks:
        dr.update(declare(nc, [("dbg_q", [P, 2048], BF16)], kind="ExternalOutput"))
    if os.environ.get("KDBG") and "l0ffn" in blocks:
        dr.update(declare(nc, [("dbg_swr", [1, 4], F32), ("dbg_w1q", [P, 512], I8),
                               ("dbg_rows", [3, S], F32), ("dbg_xq", [P, S], BF16),
                               ("dbg_hs", [P, 512], BF16), ("dbg_x", [D, S], F32)], kind="ExternalOutput"))
    if os.environ.get("KDBG") and "l1attn" in blocks:
        dr.update(declare(nc, [("dbgq", [P, 1024], BF16), ("dbgk", [P, 1028], BF16),
                               ("dbgv", [P, 1028], BF16), ("dbgx", [D, S], F32),
                               ("dbg_sc", [64, 68], F32), ("dbg_pb", [64, 68], BF16),
                               ("dbg_pt", [68, 64], BF16), ("dbg_vw", [68, D], BF16),
                               ("dbg_ot", [64, D], F32), ("dbg_mc", [64, 4], F32)], kind="ExternalOutput"))
    if "l1ffn" in blocks:
        dr["l1f_cc_in"] = nc.dram_tensor("l1f_cc_in", [1, P], F32, kind="Internal").ap()
        dr["l1f_cc_out"] = nc.dram_tensor("l1f_cc_out", [1, P], F32, kind="Internal").ap()
    if "l0ffn" in blocks:
        dr["l0f_w1q"] = nc.dram_tensor("l0f_w1q", [D, 4096], BF16, kind="Internal").ap()
        dr["l0f_w2q"] = nc.dram_tensor("l0f_w2q", [4096, D], BF16, kind="Internal").ap()
        dr["l0f_scr"] = nc.dram_tensor("l0f_scr", [16, P], F32, kind="Internal").ap()
    if "l0attn" in blocks:
        dr["l0a_scr1"] = nc.dram_tensor("l0a_scr1", [NH, P], F32, kind="Internal").ap()
        dr["l0a_inproj_bf"] = nc.dram_tensor("l0a_inproj_bf", [D, 4384], BF16, kind="Internal").ap()
        dr["l0a_outproj_bf"] = nc.dram_tensor("l0a_outproj_bf", [2048, D], BF16, kind="Internal").ap()
        dr["l1a_w_bf"] = nc.dram_tensor("l1a_w_bf", [3 * D, D], BF16, kind="Internal").ap()
        dr["l0a_scr2"] = nc.dram_tensor("l0a_scr2", [NH, P], BF16, kind="Internal").ap()
        dr["l0a_u2"] = nc.dram_tensor("l0a_u2", [D, S], BF16, kind="Internal").ap()

    C = Ctx(nc, st)
    R = Res(C)
    cdeps = R.load(C, dr)
    X_r = C.sb("x_all", [P, 8 * S], F32)
    X = [X_r[:, cc * S:(cc + 1) * S] for cc in range(8)]
    R.X_r = X_r
    xdeps = [C.dma("sync", X_r.rearrange("p (cc t) -> p cc t", cc=8),
                   dr["x_fm"].rearrange("(cc p) t -> p cc t", p=P))]
    dbar = C.V(lambda e: None, cdeps + xdeps)
    C.T(lambda e: None, [dbar])
    C.A(lambda e: None, [dbar])
    C.step("gpsimd", lambda e: None, [dbar])

    pre = None
    if "l0ffn" in blocks:
        pre = emit_l0ffn_prelude(C, R, dr, [dbar])
    deps = [dbar]
    for bname in blocks:
        if bname == "l0ffn":
            deps = emit_l0ffn(C, R, X, dr, pre[0], pre[1])
        else:
            deps = BLOCK_EMIT[bname](C, R, X, dr)

    final_store(C, R, X, dr["y"], deps)
    C.step("sync", lambda e: None, C.all_done())
    build_and_finalize(nc, C)
    st.close()
    return nc


def kernel(**inputs):
    x = inputs["L0_attn_hidden_states"]
    consts = Res.host_arrays()
    host = {}
    blocks = []
    if "L0_attn_in_proj_w" in inputs:
        host.update(l0attn_host(inputs))
        blocks.append("l0attn")
    if "L0_ffn_w1" in inputs:
        host.update(l0ffn_host(inputs))
        blocks.append("l0ffn")
    if "L1_attn_Wq" in inputs:
        host.update(l1attn_host(inputs))
        blocks.append("l1attn")
    if "L1_ffn_weight_int8" in inputs:
        host.update(l1ffn_host(inputs))
        blocks.append("l1ffn")
    nc = build_launch(blocks)
    in_maps = []
    for b in range(NCORES):
        m = {"x_fm": np.ascontiguousarray(np.asarray(x[b]).T)}
        m.update(consts)
        m.update(host)
        in_maps.append(m)
    res = run_bass_kernel_spmd(nc, in_maps, core_ids=list(range(NCORES)))
    return np.stack([res.results[b]["y"] for b in range(NCORES)], axis=0)



def emit_l0ffn_prelude(C, R, dr, deps):
    fr = R.pf.take(64)
    slots = fr[:, 0:48]
    swc = fr[:, 48:52]
    swr = fr[0:1, 56:60]

    slab_list = []
    for kc in range(8):
        for g in range(4):
            slab_list.append((dr["l0f_w1t"][kc * P:(kc + 1) * P, g * 1024:(g + 1) * 1024],
                              dr["l0f_w1q"][kc * P:(kc + 1) * P, g * 1024:(g + 1) * 1024],
                              kc))
    for kc in range(32):
        slab_list.append((dr["l0f_w2t"][kc * P:(kc + 1) * P, :],
                          dr["l0f_w2q"][kc * P:(kc + 1) * P, :],
                          16 + kc // 2))
    wslab_r = R.pbf.take(8192)
    bigs = [R.big[:, 0:1024], R.big[:, 1024:2048],
            wslab_r[:, 0:2048].bitcast(F32), wslab_r[:, 2048:4096].bitcast(F32)]
    qsl_r = R.pbf.take(2048)
    qbs = [qsl_r[:, 0:1024], qsl_r[:, 1024:2048]]

    red_deps = []
    prev_red = [None, None, None, None]
    acc_seen = set()
    for n, (ap_in, ap_out, slot) in enumerate(slab_list):
        buf = bigs[n % 4]
        dl = C.dma("sync" if n % 2 == 0 else "gpsimd", buf, ap_in,
                   list(deps) + ([prev_red[n % 4]] if prev_red[n % 4] else []))
        first = slot not in acc_seen
        acc_seen.add(slot)
        acol_ = fr[:, 60 + (n % 2):61 + (n % 2)]
        dabs = C.A(lambda e, buf=buf, acol_=acol_: e.activation(
            buf, buf, AF.Abs, bias=0.0, scale=1.0, accum_out=acol_),
            [dl] + (red_deps[-2:-1] if len(red_deps) >= 2 else []))
        if first:
            dr_ = C.V(lambda e, slot=slot, acol_=acol_: e.tensor_copy(
                slots[:, slot:slot + 1], acol_), [dabs])
        else:
            dr_ = C.V(lambda e, slot=slot, acol_=acol_: e.tensor_tensor(
                slots[:, slot:slot + 1], slots[:, slot:slot + 1], acol_, ALU.add),
                [dabs] + red_deps[-1:])
        prev_red[n % 4] = dr_
        red_deps.append(dr_)
    d1 = C.V(lambda e: e.tensor_reduce(swc[:, 0:1], slots[:, 0:8], AX.X, ALU.add), red_deps[-2:])
    d2 = C.V(lambda e: e.tensor_reduce(swc[:, 1:2], slots[:, 16:32], AX.X, ALU.add), [d1])
    d3 = C.T(lambda e: e.transpose(R.pss[0:2, 0:P], swc[:, 0:2], R.idf[:]), [d2])
    d4a = C.V(lambda e: e.tensor_copy(R.ab[0:2, 0:P], R.pss[0:2, 0:P]), [d3])
    d4b = C.V(lambda e: e.tensor_reduce(R.ab[0:2, P:P + 1], R.ab[0:2, 0:P], AX.X, ALU.add), [d4a])
    d4c = C.T(lambda e: e.transpose(R.pss[0:1, 256:258], R.ab[0:2, P:P + 1], R.idf[0:2, 0:2]), [d4b, d3])
    d5 = C.V(lambda e: e.tensor_copy(swr[0:1, 0:2], R.pss[0:1, 256:258]), [d4c])
    d6 = C.V(lambda e: e.tensor_scalar(swr[0:1, 0:2], swr[0:1, 0:2], 1.0 / (4096.0 * 1024.0), 1e-5,
                                       ALU.mult, ALU.max), [d5])
    d7 = C.V(lambda e: e.reciprocal(swr[0:1, 2:4], swr[0:1, 0:2]), [d6])
    d8 = C.T(lambda e: e.matmul(R.pss[:, 0:2], R.onerf[0:1, :], swr[0:1, 2:4], start=True, stop=True),
             [d7])
    d9 = C.V(lambda e: e.tensor_copy(swc[:, 2:4], R.pss[:, 0:2]), [d8])

    q_deps = []
    prev_qv = [None, None, None, None]
    for n, (ap_in, ap_out, slot) in enumerate(slab_list):
        buf = bigs[n % 4]
        qb = qbs[n % 2]
        swcol = swc[:, 2:3] if n < 32 else swc[:, 3:4]
        dl = C.dma("sync", buf, ap_in, [d9] + ([prev_qv[n % 4]] if prev_qv[n % 4] else []))

        def mkq(buf=buf, qb=qb, swcol=swcol):
            def body(e):
                e.tensor_scalar(buf, buf, swcol, None, ALU.mult)
                e.tensor_scalar(buf, buf, MAGIC, MAGIC, ALU.add, ALU.subtract)
                return e.tensor_scalar(qb[:], buf, -1.0, 1.0, ALU.max, ALU.min)
            return body
        dq = C.V(mkq(), [dl] + (q_deps[-2:-1] if len(q_deps) >= 2 else []))
        ds = C.dma("gpsimd", ap_out, qb[:], [dq])
        prev_qv[n % 4] = dq
        q_deps.append(ds)
    return q_deps[-2:], swr


def emit_l0ffn(C, R, X, dr, pre_deps, swr):
    NCH = 32
    bar = C.all_done() + list(pre_deps)
    R.pbf.reset()
    R.pf.reset(keep=64)
    xqr = R.pbf.take(8 * S)
    xq = [xqr[:, c * S:(c + 1) * S] for c in range(8)]
    wtr = R.pbf.take(2 * 8 * 512)
    wt = [wtr[:, i * 4096:(i + 1) * 4096] for i in range(2)]
    hsr = R.pbf.take(NCH * 512)
    hs = [hsr[:, c * 512:(c + 1) * 512] for c in range(NCH)]

    fq = R.pf.take(2 * 515)
    hraw = [fq[:, i * 515:(i + 1) * 515] for i in range(2)]
    fc = R.pf.take(256)
    cw = fc[:, 0:96]
    cb = fc[:, 96:128]
    hleft = fc[:, 128:160]
    hrsc = fc[:, 192:193]
    s2b = R.pf.take(512)
    o1b = R.pf.take(512)
    o2b = R.pf.take(512)

    dcw = C.dma("sync", cw, dr["l0f_convw"], bar)
    dcb = C.dma("sync", cb, dr["l0f_convb"], bar)

    amxc = R.pf.take(16)
    X3f = R.X_r.rearrange("p (cc t) -> p t cc", cc=8)
    tp_deps = []
    prev_tp = None
    dtree = C.V(lambda e: e.tensor_reduce(R.big[:, 0:S], X3f, AX.X, ALU.max,
                                          apply_absolute_value=True), bar)
    for tt in range(16):
        dtp = C.T(lambda e, tt=tt: e.transpose(R.pss[:, (tt % 4) * P:(tt % 4 + 1) * P],
                                               R.big[:, tt * P:(tt + 1) * P], R.idf[:]),
                  [dtree] + ([prev_tp] if prev_tp else []))
        dtv = C.V(lambda e, tt=tt: e.tensor_reduce(amxc[:, tt:tt + 1],
                                                   R.pss[:, (tt % 4) * P:(tt % 4 + 1) * P],
                                                   AX.X, ALU.max), [dtp])
        prev_tp = dtv
        tp_deps.append(dtv)
    dft = C.T(lambda e: e.transpose(R.pm[4][0:16, 0:P], amxc[:, 0:16], R.idf[:]), tp_deps)
    dfv = C.V(lambda e: e.tensor_copy(R.ab[0:16, 0:P], R.pm[4][0:16, 0:P]), [dft])
    prev_pm4_init = dfv
    dfi = C.dma("sync", R.r0[:, 0:S], R.ab[0:16, 0:P], [dfv])
    dsx = C.V(lambda e: e.tensor_single_scalar(R.r0[:, 0:S], R.r0[:, 0:S], 1e-5, ALU.max), [dfi])
    dr1 = C.V(lambda e: e.reciprocal(R.r1[:, 0:S], R.r0[:, 0:S]), [dsx])
    dr2 = C.V(lambda e: e.tensor_single_scalar(R.r1[:, 0:S], R.r1[:, 0:S], 127.0, ALU.mult), [dr1])
    do1r = C.V(lambda e: e.tensor_scalar(R.r2[:, 0:S], R.r0[:, 0:S], swr[0:1, 0:1], 1.0 / 127.0,
                                         ALU.mult, ALU.mult), [dsx] + list(pre_deps))

    q_deps = []
    prev_b = None
    qv_last = None
    X3q = R.X_r.rearrange("p (cc t) -> p cc t", cc=8)
    xq3 = xqr.rearrange("p (cc t) -> p cc t", cc=8)
    bigq = R.big.rearrange("p (ch q) -> p ch q", ch=4)
    for ng in range(4):
        sl = slice(ng * 512, (ng + 1) * 512)
        db = bcast_row(C, R, R.ab, R.r1[:, sl], 512, [dr2, do1r, prev_b, dfv], R.pm[4])
        for se in range(2):
            g_ = C.G(lambda e, se=se, sl=sl: e.tensor_tensor(
                bigq, X3q[:, se * 4:(se + 1) * 4, sl],
                R.ab[:, None, 0:512].broadcast_to([P, 4, 512]), ALU.mult),
                [db] + ([qv_last] if qv_last else []))
            qv_last = C.V(lambda e, se=se, sl=sl: e.tensor_scalar(
                xq3[:, se * 4:(se + 1) * 4, sl], bigq, MAGIC, MAGIC, ALU.add, ALU.subtract), [g_])
            q_deps.append(qv_last)
        prev_b = qv_last

    dhl = C.V(lambda e: e.memset(hleft, 0.0), bar)
    prev_pm4_init = prev_b if prev_b else prev_pm4_init

    if "dbg_swr" in dr:
        C.dma("sync", dr["dbg_swr"], swr, list(pre_deps))
        C.dma("sync", dr["dbg_w1q"], dr["l0f_w1q"][0:P, 0:512], list(pre_deps))
        C.dma("sync", dr["dbg_rows"][0:1, :], R.r1[:, 0:S], [dr2])
        C.dma("sync", dr["dbg_rows"][1:2, :], R.r2[:, 0:S], [do1r])
        dbg_xq = C.dma("sync", dr["dbg_xq"], xq[0][:, 0:S], q_deps)
    ev_deps = []
    prev_pm = {4: prev_pm4_init}
    prev_w2_last = None
    fuser = {}
    prev_conv = {}
    prev_wdma = None
    prev_hq = None
    prev_hrsc = None
    prev_cbuf = {}
    prev_dev2 = None
    for qq in range(4):
        sl = slice(qq * 512, (qq + 1) * 512)
        do1 = bcast_row(C, R, o1b, R.r2[:, sl], 512,
                        [do1r] + ([prev_pm.get(5)] if prev_pm.get(5) else []), R.pm[5])
        prev_pm[5] = do1
        if qq < 3:
            t1 = (qq + 1) * 512
            dhs = C.T(lambda e, t1=t1: e.matmul(R.pss[:, 508:509], R.onerf[0:1, :],
                                                R.r2[:, t1:t1 + 1], start=True, stop=True),
                      [do1r] + ([prev_pm.get(7)] if prev_pm.get(7) else []))
            dhsv = C.V(lambda e: e.tensor_copy(hrsc, R.pss[:, 508:509]), [dhs])
            prev_pm[7] = dhsv
            prev_hrsc = dhsv

        conv_done = []
        for mg in range(8):
            wb = wt[mg % 2]
            wdeps = [C.dma("sync", wb.rearrange("p (kc c) -> p kc c", kc=8),
                           dr["l0f_w1q"].rearrange("(kc p) m -> p kc m", p=P)
                           [:, :, mg * 512:(mg + 1) * 512],
                           [fuser.get(mg % 2)] if fuser.get(mg % 2) else bar)]
            for sub in range(4):
                mc = mg * 4 + sub
                pbi = (mg * 4 + sub) % 4
                pbank = R.pm[pbi]

                def mk1(wb=wb, sub=sub, sl=sl, pbank=pbank):
                    def body(e):
                        last = None
                        for kc in range(8):
                            last = e.matmul(pbank[:, 0:512],
                                            wb[:, kc * 512 + sub * P: kc * 512 + (sub + 1) * P],
                                            xq[kc][:, sl], start=(kc == 0), stop=(kc == 7))
                        return last
                    return body
                dmm = C.T(mk1(), wdeps + q_deps + ([prev_pm.get(pbi)] if prev_pm.get(pbi) else []))
                if sub == 3:
                    prev_wdma = dmm
                    fuser[mg % 2] = dmm
                dhm = None
                if qq < 3:
                    def mkh(wb=wb, sub=sub, t1=(qq + 1) * 512, mc=mc):
                        def body(e):
                            last = None
                            for kc in range(8):
                                last = e.matmul(R.pm[4][:, mc:mc + 1],
                                                wb[:, kc * 512 + sub * P: kc * 512 + (sub + 1) * P],
                                                xq[kc][:, t1:t1 + 1], start=(kc == 0), stop=(kc == 7))
                            return last
                        return body
                    dhm = C.T(mkh(), ([prev_pm.get(4)] if prev_pm.get(4) else []))

                hb = hraw[mc % 2]
                dev = C.V(lambda e, pbank=pbank, hb=hb: e.tensor_tensor(hb[:, 1:513], pbank[:, 0:512],
                                                                        o1b, ALU.mult),
                          [dmm, do1] + ([prev_conv.get(mc % 2)] if prev_conv.get(mc % 2) else []) + [dhl])
                prev_pm[pbi] = dev

                def mkhal(hb=hb, mc=mc, qq=qq):
                    def body(e):
                        e.tensor_copy(hb[:, 0:1], hleft[:, mc:mc + 1])
                        if qq < 3:
                            return e.tensor_scalar(hb[:, 513:514], R.pm[4][:, mc:mc + 1],
                                                   hrsc, None, ALU.mult)
                        return e.memset(hb[:, 513:514], 0.0)
                    return body
                hal_deps = [dev] + ([dhm] if dhm else []) + ([prev_hrsc] if qq < 3 else [])
                dhal = C.V(mkhal(), hal_deps)
                if dhm is not None:
                    prev_pm[4] = dhal
                dlh = C.V(lambda e, hb=hb, mc=mc: e.tensor_copy(hleft[:, mc:mc + 1], hb[:, 512:513]),
                          [dhal])

                cbuf = R.bb if mc % 2 == 0 else R.ab
                c1 = C.G(lambda e, hb=hb, mc=mc, cbuf=cbuf: e.tensor_tensor(
                    cbuf[:], hb[:, 0:512], cw[:, mc:mc + 1].broadcast_to([P, 512]), ALU.mult),
                    [dhal, dlh, dcw] + ([prev_cbuf.get(mc % 2)] if prev_cbuf.get(mc % 2) else [])
                    + ([prev_dev2] if mc <= 1 and prev_dev2 else []))

                def mkc23(hb=hb, mc=mc, cbuf=cbuf):
                    def body(e):
                        e.scalar_tensor_tensor(cbuf[:], hb[:, 1:513],
                                               cw[:, 32 + mc:33 + mc], cbuf[:],
                                               ALU.mult, ALU.add)
                        return e.scalar_tensor_tensor(cbuf[:], hb[:, 2:514],
                                                      cw[:, 64 + mc:65 + mc], cbuf[:],
                                                      ALU.mult, ALU.add)
                    return body
                c3 = C.V(mkc23(), [c1])
                c4 = C.A(lambda e, mc=mc, cbuf=cbuf: e.activation(hs[mc][:], cbuf[:], AF.Silu,
                                                                  bias=cb[:, mc:mc + 1], scale=1.0),
                         [c3, dcb] + ([prev_w2_last] if prev_w2_last else []))
                prev_cbuf[mc % 2] = c4
                prev_conv[mc % 2] = c4
                conv_done.append(c4)

        if "dbg_swr" in dr and qq == 0:
            C.dma("sync", dr["dbg_hs"], hs[0][:], conv_done)
        hs3 = hsr.rearrange("p (ch q) -> p q ch", ch=NCH)
        dt2 = C.V(lambda e: e.tensor_reduce(R.sq[1][:], hs3, AX.X, ALU.max,
                                            apply_absolute_value=True), conv_done)
        t2deps = []
        prev_t2 = prev_pm.get(7)
        for tt in range(4):
            dtp = C.T(lambda e, tt=tt: e.transpose(R.ptr[:, tt * P:(tt + 1) * P],
                                                   R.sq[1][:, tt * P:(tt + 1) * P], R.idb[:]),
                      [dt2] + ([prev_t2] if prev_t2 else []) + ([prev_pm.get(6)] if prev_pm.get(6) else []))
            dtv = C.V(lambda e, tt=tt: e.tensor_reduce(amxc[:, tt:tt + 1],
                                                       R.ptr[:, tt * P:(tt + 1) * P], AX.X, ALU.max),
                      [dtp])
            prev_t2 = dtv
            t2deps.append(dtv)
        prev_pm[7] = prev_t2
        prev_pm[6] = prev_t2
        dft2 = C.T(lambda e: e.transpose(R.pm[4][0:4, 0:P], amxc[:, 0:4], R.idf[:]),
                   t2deps + ([prev_pm.get(44)] if prev_pm.get(44) else []))
        dfv2 = C.V(lambda e: e.tensor_copy(R.ab[0:4, 0:P], R.pm[4][0:4, 0:P]), [dft2])
        prev_pm[44] = dfv2
        prev_pm[4] = dfv2
        dfi2 = C.dma("sync", R.r0[:, sl], R.ab[0:4, 0:P], [dfv2])
        dc2 = C.V(lambda e, sl=sl: e.tensor_single_scalar(R.r0[:, sl], R.r0[:, sl], 1e-5, ALU.max), [dfi2])
        ds2 = C.V(lambda e, sl=sl: e.reciprocal(R.r1[:, sl], R.r0[:, sl]), [dc2])
        ds2b = C.V(lambda e, sl=sl: e.tensor_single_scalar(R.r1[:, sl], R.r1[:, sl], 127.0, ALU.mult), [ds2])
        do2r = C.V(lambda e, sl=sl: e.tensor_scalar(R.r0[:, sl], R.r0[:, sl], swr[0:1, 1:2], 1.0 / 127.0,
                                                    ALU.mult, ALU.mult), [ds2b] + list(pre_deps))
        db2 = bcast_row(C, R, s2b, R.r1[:, sl], 512,
                        [ds2b] + ([prev_pm.get(5)] if prev_pm.get(5) else []), R.pm[5])
        do2 = bcast_row(C, R, o2b, R.r0[:, sl], 512, [do2r, db2], R.pm[5])
        prev_pm[5] = do2

        if "dbg_swr" in dr and qq == 0:
            C.dma("sync", dr["dbg_rows"][2:3, 0:512], R.r1[:, sl], [ds2b])
        hq_deps = []
        hs3b = hsr.rearrange("p (ch q) -> p ch q", ch=NCH)
        big3 = R.big.rearrange("p (ch q) -> p ch q", ch=4)
        dprev = [db2] + ([prev_hq] if prev_hq else [])
        prev_rnd = None
        for blk4 in range(8):
            dmul = C.G(lambda e, blk4=blk4: e.tensor_tensor(
                big3, hs3b[:, blk4 * 4:(blk4 + 1) * 4, :],
                s2b[:, None, 0:512].broadcast_to([P, 4, 512]), ALU.mult),
                dprev if blk4 == 0 else [prev_rnd])
            prev_rnd = C.V(lambda e, blk4=blk4: e.tensor_scalar(
                hs3b[:, blk4 * 4:(blk4 + 1) * 4, :], big3,
                MAGIC, MAGIC, ALU.add, ALU.subtract), [dmul])
            hq_deps.append(prev_rnd)
        prev_hq = prev_rnd

        for mcg in range(2):
            for kb in range(4):
                wb = wt[kb % 2]
                wdeps2 = [C.dma("sync", wb.rearrange("p (kc c) -> p kc c", kc=8),
                                dr["l0f_w2q"].rearrange("(kc p) m -> p kc m", p=P)
                                [:, kb * 8:(kb + 1) * 8, mcg * 512:(mcg + 1) * 512],
                                [fuser.get(kb % 2)] if fuser.get(kb % 2) else [prev_wdma])]
                for sub in range(4):
                    mc = mcg * 4 + sub
                    pbi = sub
                    pbank = R.pm[pbi]

                    def mk2(wb=wb, sub=sub, sl=sl, pbank=pbank, kb=kb):
                        def body(e):
                            last = None
                            for j in range(8):
                                kc = kb * 8 + j
                                last = e.matmul(pbank[:, 0:512],
                                                wb[:, j * 512 + sub * P: j * 512 + (sub + 1) * P],
                                                hs[kc][:, 0:512],
                                                start=(kb == 0 and j == 0), stop=(kb == 3 and j == 7))
                            return last
                        return body
                    mm2deps = wdeps2 + hq_deps[2 * kb:2 * kb + 2]
                    if kb == 0 and prev_pm.get(pbi):
                        mm2deps.append(prev_pm[pbi])
                    dmm2 = C.T(mk2(), mm2deps)
                    prev_w2_last = dmm2
                    if sub == 3:
                        prev_wdma = dmm2
                        fuser[kb % 2] = dmm2
                    if kb == 3:
                        def mkev2(pbank=pbank, mc=mc, sl=sl):
                            def body(e):
                                e.tensor_tensor(R.ab[:], pbank[:, 0:512], o2b, ALU.mult)
                                return e.tensor_tensor(X[mc][:, sl], X[mc][:, sl], R.ab[:], ALU.add)
                            return body
                        dev2 = C.V(mkev2(), [dmm2, do2])
                        prev_pm[pbi] = dev2
                        prev_dev2 = dev2
                        ev_deps.append(dev2)

    if "dbg_swr" in dr:
        dxx = [C.dma("sync", dr["dbg_x"][cc * P:(cc + 1) * P, :], X[cc][:], ev_deps) for cc in range(8)]
        ev_deps = ev_deps + dxx
    return ln_fm(C, R, X, X, 1e-5, ev_deps, sqw=xqr[:, 0:4096])


def l0ffn_host(inp):
    w1 = np.asarray(inp["L0_ffn_w1"], np.float32)
    w2 = np.asarray(inp["L0_ffn_w2"], np.float32)
    cwv = np.asarray(inp["L0_ffn_conv_w"], np.float32)
    cbv = np.asarray(inp["L0_ffn_conv_b"], np.float32)
    cw = np.zeros((P, 96), np.float32)
    for k in range(3):
        cw[:, k * 32:(k + 1) * 32] = cwv[:, 0, k].reshape(32, 128).T
    cb = cbv.reshape(32, 128).T
    return {
        "l0f_w1t": np.ascontiguousarray(w1.T),
        "l0f_w2t": np.ascontiguousarray(w2.T),
        "l0f_convw": np.ascontiguousarray(cw),
        "l0f_convb": np.ascontiguousarray(cb),
    }


def l0ffn_decls():
    return [("l0f_w1t", [D, 4096], F32), ("l0f_w2t", [4096, D], F32),
            ("l0f_convw", [P, 96], F32), ("l0f_convb", [P, 32], F32)]





NH = 32
NMC = 35
EMIN_CLAMP = 30.0


def emit_l0attn(C, R, X, dr):
    import os
    PH = int(os.environ.get("KPHASE", "99"))
    KCH = int(os.environ.get("KCH", "99"))
    bar = C.all_done()
    R.pbf.reset()
    R.pf.reset(keep=64)

    u_r = R.pbf.take(4096)
    u = [u_r[:, c * 512:(c + 1) * 512] for c in range(8)]
    tmpb = u
    exps_pair2 = u_r[0:2, 0:2048]
    Mb_r = u_r[:, 2048:3072]
    sz_r = R.pbf.take(16 * 512)
    sz = [sz_r[:, c * 512:(c + 1) * 512] for c in range(16)]
    xbcs_r = R.pbf.take(18 * 512)
    xbcs = [xbcs_r[:, c * 512:(c + 1) * 512] for c in range(18)]
    xstm_r = R.pbf.take(2048)
    exps_hm_bf = xstm_r[0:NH, 0:128]
    U_r = R.pbf.take(2048)
    M_r = R.pbf.take(1024)
    Lp_r = R.pbf.take(1024)
    G_r = R.pbf.take(128)
    Btm_r = R.pbf.take(128)
    hstb_r = R.pbf.take(2048)
    Y2_r = R.pbf.take(16 * 512)
    Y2 = [Y2_r[:, c * 512:(c + 1) * 512] for c in range(16)]
    wst_r = R.pbf.take(4096)
    wst = [wst_r[:, i * 2048:(i + 1) * 2048] for i in range(2)]
    EXPSP = wst_r[:, 0:2048]

    dtq = R.pf.take(512)[0:NH, :]
    hstate = R.pf.take(2048)
    raw_r = R.pf.take(1036)
    hraw = [raw_r[:, i * 518:i * 518 + 515] for i in range(2)]
    Ebuf = raw_r[:, 0:1024]
    fcm = R.pf.take(256)
    cwc = fcm[:, 0:72]
    cbc = fcm[:, 72:90]
    xleft = fcm[:, 90:144]
    nwc = fcm[:, 144:152]
    pwc = fcm[:, 152:160]
    dcol = fcm[:, 160:176]
    acol = fcm[0:NH, 176:177]
    dtbc = fcm[0:NH, 177:178]
    expq = fcm[:, 192:224]
    wsmc = fcm[:, 224:240]
    sm_r = R.pf.take(768)
    s_hm = sm_r[0:NH, 0:128]
    dtA = sm_r[0:NH, 128:256]
    lndt = sm_r[0:NH, 256:384]
    nslndt = sm_r[0:NH, 384:512]
    w_hm = sm_r[0:NH, 512:640]
    dtw_hm = sm_r[0:NH, 640:768]
    tm_r = R.pf.take(64)
    nslndt_tm = tm_r[:, 0:32]
    dtw_tm = tm_r[:, 32:64]
    onesrow = R.pf.take(128)[0:NH, :]
    s_flat = R.rows[0:1, 2048:2048 + 4096]

    dcw = C.dma("sync", cwc, dr["l0a_convw"], bar)
    dcb2 = C.dma("sync", cbc, dr["l0a_convb"], bar)
    dnw = C.dma("sync", nwc, dr["l0a_normw"], bar)
    dpw = C.dma("sync", pwc, dr["l0a_postw"], bar)
    dDc = C.dma("sync", dcol, dr["l0a_dcol"], bar)
    dAc = C.dma("sync", fcm[0:NH, 176:178], dr["l0a_adt"], bar)
    dws = C.dma("sync", wsmc, dr["l0a_wssm"], bar)
    dsel = C.dma("sync", R.selp[:], dr["c_selp"], bar)
    dmk01 = C.dma("sync", R.mask01[:], dr["c_mask01"], bar)
    dini = C.V(lambda e: (e.memset(xleft, 0.0), e.memset(onesrow, 1.0),
                          e.memset(hstate, 0.0), e.memset(hstb_r, 0.0))[-1], bar)
    dAe = C.A(lambda e: e.activation(acol, acol, AF.Exp, bias=0.0, scale=1.0), [dAc])
    dAn = C.V(lambda e: e.tensor_single_scalar(acol, acol, -1.0, ALU.mult), [dAe])

    u2_deps = []
    PIPE = {}
    wst_store = {}
    wst_sdep = {}
    wo_store = {}
    wo_sdep = {}
    pbar = bar + [dcw, dcb2, dnw, dpw, dDc, dAn, dws, dsel, dmk01, dini]
    for qq in range(4):
        sl = slice(qq * 512, (qq + 1) * 512)

        X3m = R.X_r.rearrange("p (cc t) -> p cc t", cc=8)
        sq3 = Y2_r[:, 0:4096].rearrange("p (cc t) -> p cc t", cc=8)
        st3 = Y2_r[:, 4096:8192].rearrange("p (cc t) -> p cc t", cc=8)
        dqw = C.A(lambda e, sl=sl: e.activation(sq3, X3m[:, :, sl],
                                                AF.Square, bias=0.0, scale=1.0), pbar)

        def mksqmm():
            def body(e):
                last = None
                for cc in range(8):
                    last = e.matmul(R.pss[0:1, 0:512], R.m10b[:, 0:1], Y2[cc][:],
                                    start=(cc == 0), stop=(cc == 7))
                return last
            return body
        prevmm = C.T(mksqmm(), [dqw] + pbar)
        d2 = C.A(lambda e: e.activation(R.r0[:, 0:512], R.pss[0:1, 0:512], AF.Sqrt,
                                        bias=R.eps56[0:1, 1:2], scale=1.0), [prevmm])
        d3 = C.V(lambda e: e.reciprocal(R.r0[:, 0:512], R.r0[:, 0:512]), [d2])
        db = C.T(lambda e: e.matmul(R.pm[4][:, 0:512], R.onerf[0:1, :], R.r0[0:1, 0:512],
                                    start=True, stop=True), [d3] + pbar)
        dcpa = C.V(lambda e: e.tensor_copy(R.ab[:], R.pm[4][:, 0:512]), [db])
        u3 = u_r.rearrange("p (cc t) -> p cc t", cc=8)
        du1 = C.G(lambda e, sl=sl: e.tensor_tensor(
            st3, X3m[:, :, sl],
            R.ab[:, None, 0:512].broadcast_to([P, 8, 512]), ALU.mult), [dcpa])
        du2w = C.V(lambda e: e.tensor_tensor(
            u3, st3,
            nwc[:, :, None].broadcast_to([P, 8, 512]), ALU.mult), [du1])
        u_deps = [du2w]
        ubar = C.all_done()

        conv_ct = 0
        prev_wd = None
        prev_pmb = {}
        prev_cv = {}
        wuser = {}
        for mcg in range(18):
            wb = wst[mcg % 2]
            mclist = [m for m in (2 * mcg, 2 * mcg + 1) if m < NMC]
            ncols = 256 if len(mclist) == 2 else 32
            ldeps = [wuser.get(mcg % 2)] if wuser.get(mcg % 2) else list(ubar)
            if qq == 0:
                wdeps = [C.dma("gpsimd",
                               wb.rearrange("p (kc c) -> p kc c", kc=8)[:, :, 0:ncols],
                               dr["l0a_inproj_t"].rearrange("(kc p) m -> p kc m", p=P)
                               [:, :, 2 * mcg * P: 2 * mcg * P + ncols],
                               ldeps + ([wst_sdep[mcg % 2]] if wst_sdep.get(mcg % 2) else []))]
                wsst = C.dma("sync",
                             dr["l0a_inproj_bf"].rearrange("(kc p) m -> p kc m", p=P)
                             [:, :, 2 * mcg * P: 2 * mcg * P + ncols],
                             wb.rearrange("p (kc c) -> p kc c", kc=8)[:, :, 0:ncols],
                             wdeps)
                wst_store[mcg] = wsst
                wst_sdep[mcg % 2] = wsst
            else:
                wdeps = [C.dma("sync",
                               wb.rearrange("p (kc c) -> p kc c", kc=8)[:, :, 0:ncols],
                               dr["l0a_inproj_bf"].rearrange("(kc p) m -> p kc m", p=P)
                               [:, :, 2 * mcg * P: 2 * mcg * P + ncols],
                               ldeps + [wst_store[mcg]]
                               + ([wst_sdep[mcg % 2]] if wst_sdep.get(mcg % 2) else []))]
            for mi, mc in enumerate(mclist):
                mwid = 32 if mc == 34 else 128
                pbi = (2 * mcg + mi) % 4
                pbank = R.pm[pbi]

                def mkmm(wb=wb, mi=mi, pbank=pbank, mwid=mwid):
                    def body(e):
                        last = None
                        for kc in range(8):
                            last = e.matmul(pbank[0:mwid, 0:512],
                                            wb[:, kc * 256 + mi * P: kc * 256 + mi * P + mwid],
                                            u[kc][:, :], start=(kc == 0), stop=(kc == 7))
                        return last
                    return body
                dmm = C.T(mkmm(), wdeps + ubar + ([prev_pmb.get(pbi)] if prev_pmb.get(pbi) else []))
                prev_wd = dmm
                wuser[mcg % 2] = dmm

                if mc < 16:
                    dev = C.A(lambda e, mc=mc, pbank=pbank:
                              e.activation(sz[mc][:], pbank[0:P, 0:512], AF.Silu, bias=0.0, scale=1.0),
                              [dmm])
                    prev_pmb[pbi] = dev
                elif mc < 34:
                    xc = mc - 16
                    hb = hraw[conv_ct % 2]
                    hkey = conv_ct % 2
                    conv_ct += 1
                    dev = C.V(lambda e, pbank=pbank, hb=hb:
                              e.tensor_copy(hb[:, 3:515], pbank[0:P, 0:512]),
                              [dmm] + ([prev_cv.get(hkey)] if prev_cv.get(hkey) else []))
                    prev_pmb[pbi] = dev
                    dhal = C.V(lambda e, hb=hb, xc=xc: e.tensor_copy(hb[:, 0:3], xleft[:, xc * 3:xc * 3 + 3]),
                               [dev])
                    dlh = C.V(lambda e, hb=hb, xc=xc: e.tensor_copy(xleft[:, xc * 3:xc * 3 + 3], hb[:, 512:515]),
                              [dhal])
                    a1 = C.A(lambda e, hb=hb, xc=xc: e.activation(R.ab[:], hb[:, 0:512], AF.Identity,
                                                                  bias=0.0, scale=cwc[:, xc:xc + 1]),
                             [dhal, dlh])
                    a2 = C.A(lambda e, hb=hb, xc=xc: e.activation(R.bb[:], hb[:, 1:513], AF.Identity,
                                                                  bias=0.0, scale=cwc[:, 18 + xc:19 + xc]),
                             [a1])
                    s1 = C.G(lambda e: e.tensor_tensor(R.ab[:], R.ab[:], R.bb[:], ALU.add), [a2])
                    v1 = C.V(lambda e, hb=hb, xc=xc: e.scalar_tensor_tensor(R.ab[:], hb[:, 2:514],
                                                                            cwc[:, 36 + xc:37 + xc],
                                                                            R.ab[:], ALU.mult, ALU.add),
                             [s1])
                    v2 = C.V(lambda e, hb=hb, xc=xc: e.scalar_tensor_tensor(R.ab[:], hb[:, 3:515],
                                                                            cwc[:, 54 + xc:55 + xc],
                                                                            R.ab[:], ALU.mult, ALU.add),
                             [v1])
                    c4 = C.A(lambda e, xc=xc: e.activation(xbcs[xc][:], R.ab[:], AF.Silu,
                                                           bias=cbc[:, xc:xc + 1], scale=1.0), [v2])
                    prev_cv[hkey] = c4
                else:
                    dev = C.V(lambda e, pbank=pbank: e.tensor_scalar(dtq[:, :], pbank[0:NH, 0:512],
                                                                     dtbc, None, ALU.add),
                              [dmm])
                    prev_pmb[pbi] = dev
                    sp1 = C.A(lambda e: e.activation(R.ab[0:NH, :], dtq, AF.Abs, bias=0.0, scale=1.0), [dev])
                    sp2 = C.A(lambda e: e.activation(R.ab[0:NH, :], R.ab[0:NH, :], AF.Exp, bias=0.0,
                                                     scale=-1.0), [sp1])
                    sp3 = C.A(lambda e: e.activation(R.ab[0:NH, :], R.ab[0:NH, :], AF.Ln, bias=1.0,
                                                     scale=1.0), [sp2])
                    sp4 = C.V(lambda e: e.tensor_single_scalar(R.bb[0:NH, :], dtq, 0.0, ALU.max), [sp3])
                    C.V(lambda e: e.tensor_tensor(dtq, R.bb[0:NH, :], R.ab[0:NH, :], ALU.add), [sp4])
        bbar = C.all_done()
        if "dbg_q" in dr and qq == 0:
            C.dma("sync", dr["dbg_q"][0:P, 0:512], sz[0], bbar)
            C.dma("sync", dr["dbg_q"][0:P, 512:1024], xbcs[0][:], bbar)
            C.dma("sync", dr["dbg_q"][0:P, 1024:1536], xbcs[17][:], bbar)
            C.dma("gpsimd", dr["dbg_q"][0:NH, 1536:2048], dtq, bbar)

        for c in (range(4) if PH >= 3 else []):
            csl = slice(c * P, (c + 1) * P)
            pp = PIPE
            m1 = C.A(lambda e, csl=csl: e.activation(dtA, dtq[:, csl], AF.Identity, bias=0.0, scale=acol),
                     bbar + [pp.get("m2")])
            m2 = C.V(lambda e: e.tensor_tensor_scan(s_hm, onesrow, dtA, 0.0, ALU.mult, ALU.add),
                     [m1, pp.get("m4"), pp.get("m5"), pp.get("m7"), pp.get("f1"), pp.get("q0")])
            m3 = C.A(lambda e, csl=csl: e.activation(lndt, dtq[:, csl], AF.Ln, bias=0.0, scale=1.0),
                     bbar + [pp.get("m4")])
            m4 = C.V(lambda e: e.tensor_tensor(nslndt, lndt, s_hm, ALU.subtract), [m2, m3, pp.get("t1")])
            m5 = C.A(lambda e: e.activation(w_hm, s_hm, AF.Exp, bias=s_hm[:, 127:128], scale=-1.0),
                     [m2, pp.get("m6")])
            m6 = C.V(lambda e, csl=csl: e.tensor_tensor(dtw_hm, dtq[:, csl], w_hm, ALU.mult),
                     [m5, pp.get("t2")])
            m7 = C.A(lambda e: e.activation(exps_hm_bf, s_hm, AF.Exp, bias=0.0, scale=1.0),
                     [m2, pp.get("f3"), pp.get("din")])
            t1 = C.T(lambda e: e.transpose(R.pss[:, 256:288], nslndt, R.idf[0:NH, 0:NH]),
                     [m4, pp.get("t1v"), pp.get("xvo")])
            t1v = C.V(lambda e: e.tensor_copy(nslndt_tm, R.pss[:, 256:288]), [t1, pp.get("ee")])
            t2 = C.T(lambda e: e.transpose(R.pss[:, 288:320], dtw_hm, R.idf[0:NH, 0:NH]), [m6, t1v])
            t2v = C.V(lambda e: e.tensor_copy(dtw_tm, R.pss[:, 288:320]), [t2, pp.get("uU")])
            f2 = C.dma("sync", s_flat, s_hm, [m2, pp.get("sr")])
            f1 = f2
            f3 = C.dma("sync", dr["l0a_scr2"], exps_hm_bf, [m7, pp.get("f4")])
            f4 = C.dma("sync", exps_pair2.rearrange("p (hh t) -> p hh t", hh=16),
                       dr["l0a_scr2"].rearrange("(hh two) t -> two hh t", two=2),
                       [f3, pp.get("e4")])
            q0 = C.T(lambda e: e.transpose(R.pss[0:1, 352:384], s_hm[:, 127:128],
                                           R.idf[0:NH, 0:NH]), [m2, t2v])
            q0v = C.V(lambda e: e.tensor_copy(R.r0[0:1, 512:544], R.pss[0:1, 352:384]), [q0])
            q1 = C.T(lambda e: e.matmul(R.pss[:, 320:352], R.onerf[0:1, :],
                                        R.r0[0:1, 512:544], start=True, stop=True),
                     [q0v, pp.get("hu1"), pp.get("q2")])
            q2 = C.A(lambda e: e.activation(expq, R.pss[:, 320:352], AF.Exp, bias=0.0, scale=1.0), [q1])
            if KCH < 2:
                continue
            g1 = C.T(lambda e, csl=csl: e.matmul(R.pss[0:P, 0:128], xbcs[16][:, csl], xbcs[17][:, csl],
                                                 start=True, stop=True), [q2] + bbar)
            g2 = C.A(lambda e: e.activation(G_r[:], R.pss[0:P, 0:128], AF.Identity, bias=0.0, scale=1.0),
                     [g1, pp.get("mm")])
            g3 = C.V(lambda e: e.tensor_tensor(G_r[:], G_r[:], R.mask01[:], ALU.mult), [g2])
            b1 = C.T(lambda e, csl=csl: e.transpose(R.ptr[0:P, 0:128], xbcs[16][:, csl], R.idb[:]),
                     [g1, pp.get("dz")])
            b2 = C.A(lambda e: e.activation(Btm_r[:], R.ptr[0:P, 0:128], AF.Identity,
                                            bias=0.0, scale=1.0), [b1])
            xst = []
            first_e = [f3, b2, pp.get("din"), pp.get("dz")]
            first_o = [b2, g2, pp.get("din")]
            prev_e = None
            prev_o = None
            xt0 = None
            for t16 in range(16):
                if t16 % 2 == 0:
                    xt = C.T(lambda e, t16=t16, csl=csl:
                             e.transpose(R.ptr[0:P, 128:256], xbcs[t16][:, csl], R.idb[:]),
                             ([prev_e] if prev_e else first_e))
                    xv = C.A(lambda e, t16=t16: e.activation(
                        xstm_r[:, t16 * 128:(t16 + 1) * 128], R.ptr[0:P, 128:256],
                        AF.Identity, bias=0.0, scale=1.0), [xt])
                    prev_e = xv
                else:
                    xt = C.T(lambda e, t16=t16, csl=csl:
                             e.matmul(R.pss[:, 0:128], xbcs[t16][:, csl], R.idb[:],
                                      start=True, stop=True),
                             ([prev_o] if prev_o else first_o))
                    xv = C.V(lambda e, t16=t16: e.tensor_copy(
                        xstm_r[:, t16 * 128:(t16 + 1) * 128], R.pss[:, 0:128]), [xt])
                    prev_o = xv
                if t16 == 0:
                    xt0 = xt
                xst.append(xv)
            uU = C.V(lambda e: e.tensor_tensor(
                U_r.rearrange("p (h x) -> p h x", h=NH),
                xstm_r.rearrange("p (h x) -> p h x", h=NH),
                dtw_tm[:, :, None].broadcast_to([P, NH, 64]), ALU.mult),
                xst + [t2v, pp.get("dz")])
            e1 = C.T(lambda e: e.matmul(R.pm[2][:, 0:512], R.selp[:], exps_pair2[:, 0:512],
                                        start=True, stop=True), [f4, g1, pp.get("hu2A")])
            e2 = C.T(lambda e: e.matmul(R.pm[3][:, 0:512], R.selp[:], exps_pair2[:, 512:1024],
                                        start=True, stop=True), [f4, pp.get("hu2B")])
            e1v = C.A(lambda e: e.activation(EXPSP[:, 0:512], R.pm[2][:, 0:512], AF.Identity,
                                             bias=0.0, scale=1.0), [e1, pp.get("o1")])
            e2v = C.A(lambda e: e.activation(EXPSP[:, 512:1024], R.pm[3][:, 0:512], AF.Identity,
                                             bias=0.0, scale=1.0), [e2])
            e3 = C.T(lambda e: e.matmul(R.pm[2][:, 0:512], R.selp[:], exps_pair2[:, 1024:1536],
                                        start=True, stop=True), [e1v])
            e4 = C.T(lambda e: e.matmul(R.pm[3][:, 0:512], R.selp[:], exps_pair2[:, 1536:2048],
                                        start=True, stop=True), [e2v])
            e3v = C.A(lambda e: e.activation(EXPSP[:, 1024:1536], R.pm[2][:, 0:512], AF.Identity,
                                             bias=0.0, scale=1.0), [e3])
            e4v = C.A(lambda e: e.activation(EXPSP[:, 1536:2048], R.pm[3][:, 0:512], AF.Identity,
                                             bias=0.0, scale=1.0), [e4])
            cbar = [e4v, e3v, uU, g3]
            PIPE_NEW = {"m2": m2, "m4": m4, "m5": m5, "m6": m6, "m7": m7,
                        "t1": t1, "t1v": t1v, "t2": t2, "f1": f1, "f2": f2,
                        "f3": f3, "f4": f4, "q0": q0, "uU": uU, "e4": e4,
                        "q2": q2, "xt0": xt0, "xvo": prev_o}
            if KCH < 4:
                PIPE = PIPE_NEW
                continue

            prevhup = {0: None, 1: None}
            prevsr = {0: None, 1: None}
            for g in range(4):
                st_ = g % 2
                BS, BI, BN = R.pm[0 + st_], R.pm[2 + st_], R.pm[4 + st_]
                ebase = raw_r[:, 0:1024] if st_ == 0 else R.big[:, 0:1024]
                eb = ebase[:, 0:512]
                eb2 = ebase[:, 512:1024]
                lpb = Lp_r if st_ == 0 else u_r[:, 3072:4096]
                m_t = M_r if st_ == 0 else Mb_r
                ftmp = R.big[:, 1024:1536] if st_ == 0 else R.big[:, 1536:2048]
                scrA = R.ab if st_ == 0 else R.bb
                gd = [prevhup[st_]] if prevhup[st_] else (cbar if g < 2 else [])
                sr1 = C.T(lambda e, g=g, BS=BS: e.matmul(BS[:, 0:512], R.onerf[0:1, :],
                                                         s_flat[0:1, g * 1024:g * 1024 + 512],
                                                         start=True, stop=True), gd + cbar)
                ee1 = C.V(lambda e, g=g, BS=BS, eb=eb: e.tensor_tensor(
                    eb.rearrange("p (h t) -> p h t", h=4),
                    BS[:, 0:512].rearrange("p (h t) -> p h t", h=4),
                    nslndt_tm[:, 8 * g:8 * g + 4, None].broadcast_to([P, 4, P]), ALU.add),
                    [sr1, t1v])
                sr2 = C.T(lambda e, g=g, BS=BS: e.matmul(BS[:, 0:512], R.onerf[0:1, :],
                                                         s_flat[0:1, g * 1024 + 512:(g + 1) * 1024],
                                                         start=True, stop=True), [ee1])
                ee2 = C.V(lambda e, g=g, BS=BS, eb2=eb2: e.tensor_tensor(
                    eb2.rearrange("p (h t) -> p h t", h=4),
                    BS[:, 0:512].rearrange("p (h t) -> p h t", h=4),
                    nslndt_tm[:, 8 * g + 4:8 * g + 8, None].broadcast_to([P, 4, P]), ALU.add),
                    [sr2])
                prevsr[st_] = sr2
                em = C.V(lambda e, ebase=ebase: e.tensor_single_scalar(
                    ebase[:, 0:1024], ebase[:, 0:1024], EMIN_CLAMP, ALU.min), [ee1, ee2])
                lp = C.A(lambda e, ebase=ebase, lpb=lpb: e.activation(
                    lpb[:, 0:1024], ebase[:, 0:1024], AF.Exp, bias=0.0, scale=1.0), [em])
                mm_ = C.G(lambda e, m_t=m_t, lpb=lpb: e.tensor_tensor(
                    m_t.rearrange("p (h t) -> p h t", h=8),
                    lpb.rearrange("p (h t) -> p h t", h=8),
                    G_r[:, None, :].broadcast_to([P, 8, P]), ALU.mult), [lp, g3])

                def mkintra(g=g, BI=BI, m_t=m_t):
                    def body(e):
                        last = None
                        for j in range(8):
                            h = 8 * g + j
                            last = e.matmul(BI[(h % 2) * 64:(h % 2) * 64 + 64,
                                               (j // 2) * 128:(j // 2) * 128 + 128],
                                            xstm_r[:, h * 64:(h + 1) * 64],
                                            m_t[:, j * 128:(j + 1) * 128],
                                            start=True, stop=True)
                        return last
                    return body
                din = C.T(mkintra(), [mm_, uU] + ([prevhup[st_]] if prevhup[st_] else []))

                def mkinter(g=g, BN=BN, csl=csl):
                    def body(e):
                        last = None
                        for jj in range(4):
                            pair = 4 * g + jj
                            last = e.matmul(BN[:, jj * 128:(jj + 1) * 128],
                                            hstb_r[:, pair * 128:(pair + 1) * 128],
                                            xbcs[17][:, csl], start=True, stop=True)
                        return last
                    return body
                dit = C.T(mkinter(), [din, pp.get("hsc")])

                xs3 = xbcs_r.rearrange("p (ch q) -> p ch q", ch=18)
                sz3 = sz_r.rearrange("p (ch q) -> p ch q", ch=16)
                y23 = Y2_r.rearrange("p (ch q) -> p ch q", ch=16)
                o1 = C.V(lambda e, g=g, BN=BN, scrA=scrA: e.tensor_tensor(
                    scrA[:], BN[:], EXPSP[:, g * 512:(g + 1) * 512], ALU.mult),
                    [dit, e4v])
                o2 = C.V(lambda e, BI=BI, scrA=scrA: e.tensor_tensor(scrA[:], scrA[:], BI[:], ALU.add),
                         [o1])
                o3 = C.G(lambda e, g=g, csl=csl, ftmp=ftmp: e.tensor_tensor(
                    ftmp.rearrange("p (ch q) -> p ch q", ch=4),
                    xs3[:, 4 * g:4 * g + 4, csl],
                    dcol[:, 4 * g:4 * g + 4, None].broadcast_to([P, 4, P]), ALU.mult),
                    [o2])
                o4 = C.V(lambda e, scrA=scrA, ftmp=ftmp: e.tensor_tensor(scrA[:], scrA[:],
                                                                         ftmp, ALU.add),
                         [o3])
                o5 = C.G(lambda e, g=g, c=c, csl=csl, scrA=scrA: e.tensor_tensor(
                    y23[:, 4 * g:4 * g + 4, c * P:(c + 1) * P],
                    scrA[:].rearrange("p (ch q) -> p ch q", ch=4),
                    sz3[:, 4 * g:4 * g + 4, csl], ALU.mult), [o4])

                def mkz(g=g, BI=BI):
                    def body(e):
                        last = None
                        for j in range(8):
                            h = 8 * g + j
                            last = e.matmul(BI[:, j * 64:(j + 1) * 64], Btm_r[:],
                                            U_r[:, h * 64:(h + 1) * 64], start=True, stop=True)
                        return last
                    return body
                dz_ = C.T(mkz(), [o2, o4, b2])
                hu1 = C.G(lambda e, g=g, ftmp=ftmp: e.tensor_tensor(
                    ftmp.rearrange("p (h x) -> p h x", h=8),
                    hstate[:, g * 512:(g + 1) * 512].rearrange("p (h x) -> p h x", h=8),
                    expq[:, 8 * g:8 * g + 8, None].broadcast_to([P, 8, 64]), ALU.mult),
                    [o5, o4, q2])
                hu2 = C.V(lambda e, g=g, BI=BI, ftmp=ftmp: e.tensor_tensor(
                    hstate[:, g * 512:(g + 1) * 512], ftmp, BI[:], ALU.add),
                    [hu1, dz_])
                prevhup[st_] = hu2
                PIPE_NEW["sr"] = sr2
                PIPE_NEW["ee"] = ee2
                PIPE_NEW["mm"] = mm_
                PIPE_NEW["din"] = din
                PIPE_NEW["dz"] = dz_
                PIPE_NEW["o1"] = o1
                PIPE_NEW["hu1"] = hu1
                PIPE_NEW["hu2A" if st_ == 0 else "hu2B"] = hu2
            if KCH >= 6:
                PIPE_NEW["hsc"] = C.V(lambda e: e.tensor_copy(hstb_r[:], hstate[:]),
                                      [prevhup[0], prevhup[1]])
            PIPE = PIPE_NEW

        if PH < 4:
            pbar = C.all_done()
            u2_deps.append(C.dma("sync", dr["l0a_u2"][0:P, sl], tmpb[0][:, :], pbar))
            continue
        gbar = C.all_done()
        gslots = [u_r[:, i * 512:(i + 1) * 512] for i in range(8)]
        dqs2 = []
        for ccc in range(8):
            dqs2.append(C.A(lambda e, ccc=ccc: e.activation(gslots[ccc][:], Y2[ccc][:],
                                                            AF.Square, bias=0.0, scale=1.0),
                            gbar if ccc == 0 else [dqs2[-1]]))

        def mkg2a():
            def body(e):
                last = None
                for j in range(8):
                    last = e.matmul(R.pss[0:1, 0:512], R.m11b[:, 0:1], gslots[j][:],
                                    start=(j == 0), stop=False)
                return last
            return body
        gmm1 = C.T(mkg2a(), [dqs2[-1]] + gbar)
        dqs3 = []
        for ccc in range(8, 16):
            dqs3.append(C.A(lambda e, ccc=ccc: e.activation(gslots[ccc - 8][:], Y2[ccc][:],
                                                            AF.Square, bias=0.0, scale=1.0),
                            [gmm1] if ccc == 8 else [dqs3[-1]]))

        def mkg2b():
            def body(e):
                last = None
                for j in range(8):
                    last = e.matmul(R.pss[0:1, 0:512], R.m11b[:, 0:1], gslots[j][:],
                                    start=False, stop=(j == 7))
                return last
            return body
        gmm2 = C.T(mkg2b(), [dqs3[-1]])
        gs3 = C.A(lambda e: e.activation(R.r0[:, 0:512], R.pss[0:1, 0:512], AF.Sqrt,
                                         bias=R.eps56[0:1, 1:2], scale=1.0), [gmm2])
        gs4 = C.V(lambda e: e.reciprocal(R.r0[:, 0:512], R.r0[:, 0:512]), [gs3])
        gb = C.T(lambda e: e.matmul(R.pm[4][:, 0:512], R.onerf[0:1, :], R.r0[0:1, 0:512],
                                    start=True, stop=True), [gs4] + gbar)
        y3d = []
        for ccc in range(16):
            y3d.append(C.V(lambda e, ccc=ccc: e.tensor_tensor(Y2[ccc][:], Y2[ccc][:],
                                                              R.pm[4][:, 0:512], ALU.mult),
                           [gb]))

        if PH < 5:
            pbar = C.all_done()
            continue
        prev_ev_o = None
        wouser = {}
        for mcg in range(2):
            for kc in range(16):
                wo = wst[kc % 2][:, 0:512]
                wdep0 = ([wouser.get(kc % 2)] if wouser.get(kc % 2) else gbar) + \
                    ([wo_sdep[kc % 2]] if wo_sdep.get(kc % 2) else [])
                if qq == 0:
                    dwo = C.dma("gpsimd", wo, dr["l0a_outproj_t"][kc * P:(kc + 1) * P,
                                                                 mcg * 512:(mcg + 1) * 512],
                                wdep0)
                    dsc_ = C.V(lambda e, wo=wo, kc=kc: e.tensor_scalar(wo, wo, wsmc[:, kc:kc + 1], None,
                                                                       ALU.mult), [dwo])
                    wost = C.dma("sync", dr["l0a_outproj_bf"][kc * P:(kc + 1) * P,
                                                              mcg * 512:(mcg + 1) * 512],
                                 wo, [dsc_])
                    wo_store[(mcg, kc)] = wost
                    wo_sdep[kc % 2] = wost
                else:
                    dsc_ = C.dma("sync", wo, dr["l0a_outproj_bf"][kc * P:(kc + 1) * P,
                                                                  mcg * 512:(mcg + 1) * 512],
                                 wdep0 + [wo_store[(mcg, kc)]])
                def mko(wo=wo, kc=kc):
                    def body(e):
                        last = None
                        for sub in range(4):
                            last = e.matmul(R.pm[sub][:, 0:512], wo[:, sub * 128:(sub + 1) * 128],
                                            Y2[kc][:], start=(kc == 0), stop=(kc == 15))
                        return last
                    return body
                dmo = C.T(mko(), [dsc_] + y3d + ([prev_ev_o] if kc == 0 and prev_ev_o else []))
                wouser[kc % 2] = dmo
            for sub in range(4):
                mc = mcg * 4 + sub
                ev1 = C.V(lambda e, sub=sub, mc=mc, sl=sl: e.tensor_tensor(
                    tmpb[mc][:, :], R.pm[sub][:, 0:512], X[mc][:, sl], ALU.add), [dmo])
                ev2 = C.V(lambda e, sub=sub, mc=mc, sl=sl: e.scalar_tensor_tensor(
                    X[mc][:, sl], X[mc][:, sl], 2.0, R.pm[sub][:, 0:512], ALU.mult, ALU.add), [ev1])
                prev_ev_o = ev2
        obar = C.all_done()
        usl = [Y2_r[:, i * 512:(i + 1) * 512] for i in range(8)]
        dqs4 = []
        for cc in range(8):
            dqs4.append(C.A(lambda e, cc=cc: e.activation(usl[cc][:], tmpb[cc][:, :],
                                                          AF.Square, bias=0.0, scale=1.0),
                            obar if cc == 0 else [dqs4[-1]]))

        def mkum():
            def body(e):
                last = None
                for cc in range(8):
                    last = e.matmul(R.pss[0:1, 0:512], R.m10b[:, 0:1], usl[cc][:],
                                    start=(cc == 0), stop=(cc == 7))
                return last
            return body
        usmm = C.T(mkum(), [dqs4[-1]] + obar)
        us3 = C.A(lambda e: e.activation(R.r0[:, 0:512], R.pss[0:1, 0:512], AF.Sqrt,
                                         bias=R.eps56[0:1, 1:2], scale=1.0), [usmm])
        us4 = C.V(lambda e: e.reciprocal(R.r0[:, 0:512], R.r0[:, 0:512]), [us3])
        ub = C.T(lambda e: e.matmul(R.pm[4][:, 0:512], R.onerf[0:1, :], R.r0[0:1, 0:512],
                                    start=True, stop=True), [us4] + obar)
        ucp = C.V(lambda e: e.tensor_copy(R.ab[:], R.pm[4][:, 0:512]), [ub])
        tmp3 = u_r.rearrange("p (cc t) -> p cc t", cc=8)
        dw1 = C.G(lambda e: e.tensor_tensor(
            Y2_r[:, 4096:8192].rearrange("p (cc t) -> p cc t", cc=8),
            tmp3, R.ab[:, None, 0:512].broadcast_to([P, 8, 512]), ALU.mult), [ucp])
        dw2 = C.V(lambda e: e.tensor_tensor(
            tmp3, Y2_r[:, 4096:8192].rearrange("p (cc t) -> p cc t", cc=8),
            pwc[:, :, None].broadcast_to([P, 8, 512]), ALU.mult), [dw1])
        u2_deps.append(C.dma("sync",
                             dr["l0a_u2"].rearrange("(cc p) t -> p cc t", p=P)[:, :, sl],
                             tmp3, [dw2]))
        pbar = C.all_done()

    if PH < 7:
        return ln_fm(C, R, X, X, 1e-5, C.all_done())
    mbar = C.all_done()
    R.pbf.reset()
    u2h_r = R.pbf.take(8 * 1024)
    u2h = [u2h_r[:, c * 1024:(c + 1) * 1024] for c in range(8)]
    wmt_r = R.pbf.take(2 * 4096)
    wmt = [wmt_r[:, i * 4096:(i + 1) * 4096] for i in range(2)]
    hm_r = R.pbf.take(22 * 1024)
    hm = [hm_r[:, c * 1024:(c + 1) * 1024] for c in range(22)]
    GATE_GROUPS = [(0, 512), (512, 512), (1024, 512), (1536, 512), (2048, 512), (2560, 192)]
    mlp_ev = []
    prev_wd2 = None
    muser = {}
    for hh in range(2):
        hsl = slice(hh * 1024, (hh + 1) * 1024)
        du2l = [C.dma("sync", u2h[cc], dr["l0a_u2"][cc * P:(cc + 1) * P, hsl],
                      mbar if hh == 0 else [mlp_ev[-1]]) for cc in range(8)]
        prev_ev_g = {}
        for (g0, gcols) in GATE_GROUPS:
            wdg = [C.dma("gpsimd", wmt[0].rearrange("p (kc c) -> p kc c", kc=8)[:, :, 0:gcols],
                         dr["l0a_gate_t"].rearrange("(kc p) m -> p kc m", p=P)[:, :, g0:g0 + gcols],
                         [muser.get(0)] if muser.get(0) else mbar)]
            wdu = [C.dma("gpsimd", wmt[1].rearrange("p (kc c) -> p kc c", kc=8)[:, :, 0:gcols],
                         dr["l0a_up_t"].rearrange("(kc p) m -> p kc m", p=P)[:, :, g0:g0 + gcols],
                         [muser.get(1)] if muser.get(1) else mbar)]
            nsub = (gcols + 127) // 128
            for sub in range(nsub):
                mw = min(128, gcols - sub * 128)
                mchunk = (g0 + sub * 128) // 128
                for ng in range(2):
                    def mkg(sub=sub, mw=mw, ng=ng, w=0):
                        def body(e):
                            last = None
                            for kc in range(8):
                                last = e.matmul(R.pm[0 + ng][0:mw, 0:512],
                                                wmt[0][:, kc * 512 + sub * 128: kc * 512 + sub * 128 + mw],
                                                u2h[kc][:, ng * 512:(ng + 1) * 512],
                                                start=(kc == 0), stop=(kc == 7))
                            return last
                        return body
                    dg = C.T(mkg(), wdg + du2l + ([prev_ev_g.get(ng)] if prev_ev_g.get(ng) else []))
                    def mkup(sub=sub, mw=mw, ng=ng):
                        def body(e):
                            last = None
                            for kc in range(8):
                                last = e.matmul(R.pm[2 + ng][0:mw, 0:512],
                                                wmt[1][:, kc * 512 + sub * 128: kc * 512 + sub * 128 + mw],
                                                u2h[kc][:, ng * 512:(ng + 1) * 512],
                                                start=(kc == 0), stop=(kc == 7))
                            return last
                        return body
                    du_ = C.T(mkup(), wdu + ([prev_ev_g.get(2 + ng)] if prev_ev_g.get(2 + ng) else []))
                    sgl = C.A(lambda e, mw=mw, ng=ng, b=ng: e.activation(
                        R.sq[b][0:mw, :], R.pm[0 + ng][0:mw, 0:512], AF.Silu, bias=0.0, scale=1.0), [dg])
                    hev = C.V(lambda e, mchunk=mchunk, mw=mw, ng=ng, b=ng: e.tensor_tensor(
                        hm[mchunk][0:mw, ng * 512:(ng + 1) * 512], R.sq[b][0:mw, :],
                        R.pm[2 + ng][0:mw, 0:512], ALU.mult), [sgl, du_])
                    prev_ev_g[ng] = hev
                    prev_ev_g[2 + ng] = hev
                    prev_wd2 = du_
                    muser[0] = dg
                    muser[1] = du_
        hbar = C.all_done()
        KCH = [(k * 128, 128) for k in range(21)] + [(2688, 64)]
        for mcg in range(2):
            for ng in range(2):
                prev_devd = None
                for kb in range(3):
                    kcs = KCH[kb * 8:(kb + 1) * 8]
                    wdd = [C.dma("gpsimd", wmt[kb % 2][:len_, j * 512:(j + 1) * 512],
                                 dr["l0a_down_t"][k0:k0 + len_, mcg * 512:(mcg + 1) * 512],
                                 [muser.get(kb % 2)] if muser.get(kb % 2) else hbar)
                           for j, (k0, len_) in enumerate(kcs)]
                    def mkd(kb=kb, kcs=kcs, mcg=mcg, ng=ng):
                        def body(e):
                            last = None
                            for j, (k0, len_) in enumerate(kcs):
                                kc = kb * 8 + j
                                for sub in range(4):
                                    last = e.matmul(
                                        R.pm[sub][:, 0:512],
                                        wmt[kb % 2][:len_, j * 512 + sub * 128: j * 512 + (sub + 1) * 128],
                                        hm[kc][0:len_, ng * 512:(ng + 1) * 512],
                                        start=(kb == 0 and j == 0), stop=(kb == 2 and j == len(kcs) - 1))
                            return last
                        return body
                    ddeps = wdd + ([prev_devd] if prev_devd else hbar)
                    if kb == 0:
                        ddeps = ddeps + (mlp_ev[-4:] if mlp_ev else [])
                    dmd = C.T(mkd(), ddeps)
                    prev_devd = dmd
                    prev_wd2 = dmd
                    muser[kb % 2] = dmd
                for sub in range(4):
                    mc = mcg * 4 + sub
                    tsl = slice(hh * 1024 + ng * 512, hh * 1024 + (ng + 1) * 512)
                    mev = C.V(lambda e, sub=sub, mc=mc, tsl=tsl: e.tensor_tensor(
                        X[mc][:, tsl], X[mc][:, tsl], R.pm[sub][:, 0:512], ALU.add), [dmd])
                    mlp_ev.append(mev)

    return ln_fm(C, R, X, X, 1e-5, mlp_ev[-4:], sqw=u2h_r[:, 0:4096])


def l0attn_host(inp):
    cwv = np.asarray(inp["L0_attn_conv_w"], np.float32)
    cbv = np.asarray(inp["L0_attn_conv_b"], np.float32)
    cw = np.zeros((P, 72), np.float32)
    for k in range(4):
        cw[:, k * 18:(k + 1) * 18] = cwv[:, k].reshape(18, 128).T
    Dv = np.asarray(inp["L0_attn_D"], np.float32)
    dcol = np.repeat(Dv, 64).reshape(16, 128).T
    adt = np.zeros((NH, 2), np.float32)
    adt[:, 0] = np.asarray(inp["L0_attn_A_log"], np.float32)
    adt[:, 1] = np.asarray(inp["L0_attn_dt_bias"], np.float32)
    selp = np.zeros((2, P), BF)
    selp[0, 0:64] = 1
    selp[1, 64:128] = 1
    mask01 = np.tril(np.ones((P, P), np.float32)).astype(BF)
    mask01 = np.triu(np.ones((P, P), np.float32)).astype(BF)
    return {
        "l0a_inproj_t": np.ascontiguousarray(np.asarray(inp["L0_attn_in_proj_w"], np.float32).T),
        "l0a_outproj_t": np.ascontiguousarray(np.asarray(inp["L0_attn_out_proj_w"], np.float32).T),
        "l0a_gate_t": np.ascontiguousarray(np.asarray(inp["L0_attn_gate_w"], np.float32).T),
        "l0a_up_t": np.ascontiguousarray(np.asarray(inp["L0_attn_up_w"], np.float32).T),
        "l0a_down_t": np.ascontiguousarray(np.asarray(inp["L0_attn_down_w"], np.float32).T),
        "l0a_convw": cw,
        "l0a_convb": np.ascontiguousarray(cbv.reshape(18, 128).T),
        "l0a_normw": np.ascontiguousarray(np.asarray(inp["L0_attn_norm_w"], np.float32).reshape(8, 128).T),
        "l0a_postw": np.ascontiguousarray(np.asarray(inp["L0_attn_post_norm_w"], np.float32).reshape(8, 128).T),
        "l0a_wssm": np.ascontiguousarray(np.asarray(inp["L0_attn_ssm_norm_w"], np.float32).reshape(16, 128).T),
        "l0a_dcol": dcol.astype(np.float32),
        "l0a_adt": adt,
        "c_selp": selp,
        "c_mask01": mask01,
    }


def l0attn_decls():
    return [("l0a_inproj_t", [D, 4384], F32), ("l0a_outproj_t", [2048, D], F32),
            ("l0a_gate_t", [D, 2752], F32), ("l0a_up_t", [D, 2752], F32),
            ("l0a_down_t", [2752, D], F32),
            ("l0a_convw", [P, 72], F32), ("l0a_convb", [P, 18], F32),
            ("l0a_normw", [P, 8], F32), ("l0a_postw", [P, 8], F32),
            ("l0a_wssm", [P, 16], F32), ("l0a_dcol", [P, 16], F32),
            ("l0a_adt", [NH, 2], F32), ("c_selp", [2, P], BF16),
            ("c_mask01", [P, P], BF16)]
